# version 08
# baseline (speedup 1.0000x reference)
import math
import contextlib
import numpy as np
import ml_dtypes
import concourse.bass as bass
import concourse.mybir as mybir
from concourse.bass import IndirectOffsetOnAxis
from concourse.bass_utils import run_bass_kernel_spmd

F32 = mybir.dt.float32
BF16 = mybir.dt.bfloat16
U32 = mybir.dt.uint32
AF = mybir.ActivationFunctionType
ALU = mybir.AluOpType
AX = mybir.AxisListType
bf16 = ml_dtypes.bfloat16

B, S, D = 2, 4096, 2048
H, HD = 16, 128
FF, E, R = 5632, 8, 16
TOK, IBS, NIB = 1024, 512, 2
NFT = FF // 128
NCH = D // 128
NPT = S // 128
SCALE = 1.0 / math.sqrt(HD)
LORA = 2.0
SLOPES = [2.0 ** (-k / 2.0) for k in range(H)]
NEGBIG = -1.0e30
SIM_SAFE = [False]


class Sem:
    def __init__(self, es, nc, name):
        self.h = es.enter_context(nc.semaphore(name))
        self.v = 0

    def inc(self, instr, n=1):
        instr.then_inc(self.h, n)
        self.v += n
        return instr


class Builder:
    def __init__(self, dbg_names=()):
        self.nc = bass.Bass("TRN2", target_bir_lowering=False, dynamic_dma_scratch_size=16384)
        self.dbg_names = set(dbg_names)
        self.dbg = {}

    def ap(self, *a, **k):
        return self.nc.dram_tensor(*a, **k).ap()

    def dbgout(self, name, shape, dt=F32):
        if name in self.dbg_names:
            self.dbg[name] = self.ap("dbg_" + name, shape, dt, kind="ExternalOutput")
        return self.dbg.get(name)

    def declare_dram(self):
        ap = self.ap
        self.xb = ap("xb", [S, D], F32, kind="ExternalInput")
        self.xsl = ap("xsl", [TOK, D], F32, kind="ExternalInput")
        self.wqkv = ap("wqkv", [D, 1536], F32, kind="ExternalInput")
        self.bqkv_col = ap("bqkv_col", [128, 12], F32, kind="ExternalInput")
        self.wo = ap("wo", [D, D], F32, kind="ExternalInput")
        self.bo_row = ap("bo_row", [1, D], F32, kind="ExternalInput")
        self.gwT = ap("gwT", [D, E], F32, kind="ExternalInput")
        self.w1T = ap("w1T", [D, FF], F32, kind="ExternalInput")
        self.w3T = ap("w3T", [D, FF], F32, kind="ExternalInput")
        self.w2T = ap("w2T", [FF, D], F32, kind="ExternalInput")
        self.au1 = ap("au1", [D, 256], F32, kind="ExternalInput")
        self.au3 = ap("au3", [D, 256], F32, kind="ExternalInput")
        self.b1pk = ap("b1pk", [NFT, 16, 8 * 128], F32, kind="ExternalInput")
        self.b3pk = ap("b3pk", [NFT, 16, 8 * 128], F32, kind="ExternalInput")
        self.a2cat = ap("a2cat", [FF, 128], F32, kind="ExternalInput")
        self.b2cat = ap("b2cat", [16, E * D], F32, kind="ExternalInput")
        self.identf = ap("identf", [128, 128], F32, kind="ExternalInput")
        self.onehot = ap("onehot", [128, 8 * 128], F32, kind="ExternalInput")
        self.e1row_d = ap("e1row_d", [128, 128], F32, kind="ExternalInput")
        self.onecol_d = ap("onecol_d", [128, 128], F32, kind="ExternalInput")
        self.slneg_d = ap("slneg_d", [1, 4 * S], F32, kind="ExternalInput")
        self.sjh_d = ap("sjh_d", [128, 4 * 32], F32, kind="ExternalInput")
        self.mask_d = ap("mask_d", [128, 4 * 512], F32, kind="ExternalInput")
        self.svec_d = ap("svec_d", [32, 1], F32, kind="ExternalInput")
        self.gidx_d = ap("gidx_d", [128, NCH], U32, kind="ExternalInput")
        self.out = ap("out", [TOK, D], F32, kind="ExternalOutput")
        self.x1dram = ap("x1dram", [TOK, D], F32, kind="Internal")
        self.maccd = [ap(f"maccd{ib}", [NFT, 128, 512], BF16, kind="Internal")
                      for ib in range(2)]
        self.ccin = self.nc.dram_tensor("ccin", [512, S], BF16, kind="Internal")
        self.ccout = self.nc.dram_tensor("ccout", [D, S], BF16, kind="Internal")

    def phase0_alloc(self):
        sb, sem = self.sb, self.sem
        self.idb = sb("idb", [128, 128], BF16)
        self.idf = sb("idf", [128, 128], F32)
        self.e1b = sb("e1b", [128, 128], BF16)
        self.oc1b = sb("oc1b", [128, 128], BF16)
        self.svec = sb("svec", [32, 1], F32)
        self.bqcol = sb("bqcol", [128, 12], F32)
        self.bqcs = sb("bqcs", [128, 4], F32)
        self.epsv = sb("epsv", [128, 1], F32)
        self.onehots = sb("onehots", [128, 8 * 128], BF16)
        self.stats = sb("stats", [128, 24], F32)
        self.gidx_s = sb("gidx_s", [128, NCH], U32)
        self.mv = [sb(f"mv{i}", [128, 2], F32) for i in range(2)]
        self.rstd = [sb(f"rstd{i}", [128, 1], F32) for i in range(2)]
        self.nbias = [sb(f"nbias{i}", [128, 1], F32) for i in range(2)]

        self.sem_x = sem("sem_x")
        self.sem_w = sem("sem_w")
        self.s_c0 = sem("s_c0")
        self.s_c1 = sem("s_c1")

    def phase0(self):
        block = self.block
        me = self

        @block.gpsimd
        def _(g):
            me.sem_w.inc(g.dma_start(out=me.idb[:, :], in_=me.identf[:, :]), 16)
            me.sem_w.inc(g.dma_start(out=me.e1b[:, :], in_=me.e1row_d[:, :]), 16)
            me.sem_w.inc(g.dma_start(out=me.oc1b[:, :], in_=me.onecol_d[:, :]), 16)
            me.sem_w.inc(g.dma_start(out=me.onehots[:, :], in_=me.onehot[:, :]), 16)

        @block.sync
        def _(sy):
            me.s_c0.inc(sy.dma_start(out=me.idf[:, :], in_=me.identf[:, :]), 16)
            me.s_c0.inc(sy.dma_start(out=me.svec[:, :], in_=me.svec_d[:, :]), 16)
            me.s_c0.inc(sy.dma_start(out=me.bqcol[:, :], in_=me.bqkv_col[:, :]), 16)
            me.s_c0.inc(sy.dma_start(out=me.gidx_s[:, :], in_=me.gidx_d[:, :]), 16)

        @block.vector
        def _(v):
            v.memset(me.epsv[:, :], 1e-5)
            v.wait_ge(me.s_c0.h, 64)
            me.s_c1.inc(v.tensor_scalar(me.bqcs[:, :], me.bqcol[:, 0:4], SCALE,
                                        None, op0=ALU.mult))

    def phase1(self):
        nc, block, me = self.nc, self.block, self
        sbA = self.sbA

        self.qT = [sbA(f"qT{h}", [128, S], BF16) for h in range(4)]
        self.kT = [sbA(f"kT{h}", [128, S], BF16) for h in range(4)]
        self.Vt = [sbA(f"Vt{h}", [128, 32 * 129], BF16) for h in range(4)]
        self.cmx8 = sbA("cmx8", [1, 8], F32)
        self.bnd = sbA("bnd", [1, 24], F32)
        es1 = self.esA.enter_context(contextlib.ExitStack())
        sb1 = lambda n, s, d: es1.enter_context(nc.sbuf_tensor(n, s, d))
        ps1 = lambda n, s, d: es1.enter_context(nc.psum_tensor(n, s, d))
        self.es1 = es1
        xbuf = [sb1(f"xbuf{i}", [128, D], F32) for i in range(2)]
        lnx = [sb1(f"lnx{i}", [128, D], BF16) for i in range(2)]
        lnxT = [sb1(f"lnxT{i}", [128, NCH * 512], BF16) for i in range(2)]
        wqkv_sb = sb1("wqkv_sb", [128, NCH * 1536], BF16)
        vtmp = [sb1(f"vtmp{i}", [128, 512], BF16) for i in range(2)]
        self.sqbuf = lnx[0]
        pf = [self.psA(f"pf{i}", [128, 512], F32) for i in range(2)]
        pbb = [self.psA(f"pbb{i}", [128, 1024], BF16) for i in range(2)]
        self.p1_pf, self.p1_pbb = pf, pbb

        sem = self.sem
        s_stat = sem("s_stat"); s_rstd = sem("s_rstd"); s_nb = sem("s_nb")
        self.s_stat, self.s_rstd, self.s_nb = s_stat, s_rstd, s_nb
        s_lnx = sem("s_lnx")
        s_tp = sem("s_tp"); s_tpev = sem("s_tpev")
        s_qk = sem("s_qk"); s_ev = sem("s_ev")
        self.s_qk, self.s_ev, self.s_tp, self.s_tpev = s_qk, s_ev, s_tp, s_tpev

        def ln_ep(pt, half):
            tc, j = pt // 4, pt % 4
            return 12 * tc + 2 * j + half

        def v_ep(tc, h):
            return 12 * tc + 8 + h

        def qkv_grp(tc, h, P):
            return 12 * tc + 3 * h + P

        @block.gpsimd
        def _(g):
            src = me.wqkv[:, :].rearrange("(c p) q -> p c q", p=128)
            dst = wqkv_sb[:, :].rearrange("p (c q) -> p c q", c=NCH)
            me.sem_w.inc(g.dma_start(out=dst, in_=src), 16)
        w_qkv = self.sem_w.v

        @block.vector
        def _(v):
            for h in range(4):
                vv = me.Vt[h][:, :].rearrange("p (t c) -> p t c", c=129)
                v.memset(vv[:, :, 128:129], 1.0)

        x_base = self.sem_x.v

        @block.sync
        def _(sy):
            for pt in range(NPT):
                if pt >= 2:
                    sy.wait_ge(s_lnx.h, pt - 1)
                me.sem_x.inc(sy.dma_start(
                    out=xbuf[pt % 2][:, :],
                    in_=me.xb[128 * pt:128 * (pt + 1), :]), 16)

        lnxT3 = [lnxT[i][:, :].rearrange("p (c t) -> p c t", c=NCH) for i in range(2)]
        wq3 = wqkv_sb[:, :].rearrange("p (c q) -> p c q", c=NCH)

        @block.vector
        def _(v):
            def ln_evacs(pt):
                for half in range(2):
                    nep = ln_ep(pt, half)
                    v.wait_ge(s_tp.h, nep + 1)
                    tc, j = pt // 4, pt % 4
                    if tc >= 2 and j == 0 and half == 0:
                        v.wait_ge(s_qk.h, 12 * (tc - 1))
                    bank = pbb[nep % 2]
                    dst = lnxT3[tc % 2][:, 8 * half:8 * half + 8, 128 * j:128 * (j + 1)]
                    srcb = bank[:, 0:1024].rearrange("p (c t) -> p c t", c=8)
                    s_tpev.inc(v.tensor_copy(dst, srcb))

            def v_evacs(tc):
                for h in range(4):
                    nep = v_ep(tc, h)
                    v.wait_ge(s_tp.h, nep + 1)
                    bank = pbb[nep % 2]
                    vt3 = me.Vt[h][:, :].rearrange("p (t c) -> p t c", c=129)
                    dst = vt3[:, 4 * tc:4 * tc + 4, 0:128]
                    srcb = bank[:, 0:512].rearrange("p (t c) -> p t c", c=128)
                    s_tpev.inc(v.tensor_copy(dst, srcb))

            def recip_nb(pt):
                v.wait_ge(s_rstd.h, pt + 1)
                v.reciprocal(me.rstd[pt % 2][:, :], me.rstd[pt % 2][:, :])
                v.drain()
                s_nb.inc(v.tensor_scalar(
                    me.nbias[pt % 2][:, :], me.rstd[pt % 2][:, :],
                    me.mv[pt % 2][:, 0:1], -1.0, op0=ALU.mult, op1=ALU.mult))

            for pt in range(NPT):
                v.wait_ge(me.sem_x.h, x_base + 16 * min(pt + 2, NPT))
                xt = xbuf[pt % 2]
                for q in range(4):
                    v.bn_stats(me.stats[:, 6 * q:6 * q + 6],
                               xt[:, 512 * q:512 * (q + 1)])
                v.drain()
                s_stat.inc(v.bn_aggr(me.mv[pt % 2][:, :], me.stats[:, :]))
                if pt >= 1:
                    ln_evacs(pt - 1)
                if pt % 4 == 0 and pt >= 4:
                    v_evacs(pt // 4 - 1)
                recip_nb(pt)
            ln_evacs(NPT - 1)
            v_evacs(7)


        @block.tensor
        def _(t):
            t.wait_ge(me.sem_w.h, w_qkv)
            for tc in range(8):
                for j in range(4):
                    pt = 4 * tc + j
                    t.wait_ge(s_lnx.h, pt + 1)
                    for half in range(2):
                        nep = ln_ep(pt, half)
                        if nep >= 2:
                            t.wait_ge(s_tpev.h, nep - 1)
                        bank = pbb[nep % 2]
                        for c8 in range(8):
                            ch = 8 * half + c8
                            ins = t.transpose(bank[:, 128 * c8:128 * (c8 + 1)],
                                              lnx[pt % 2][:, 128 * ch:128 * (ch + 1)],
                                              me.idb[:, :])
                            if c8 == 7:
                                s_tp.inc(ins)
                t.wait_ge(s_tpev.h, 12 * tc + 8)
                for h in range(4):
                    for P in range(3):
                        grp = qkv_grp(tc, h, P)
                        if grp >= 2:
                            t.wait_ge(s_ev.h, grp - 1)
                        bank = pf[grp % 2]
                        col0 = 512 * P + 128 * h
                        for ch in range(NCH):
                            ins = t.matmul(bank[:, :], wq3[:, ch, col0:col0 + 128],
                                           lnxT3[tc % 2][:, ch, :],
                                           start=(ch == 0), stop=(ch == NCH - 1))
                            if ch == NCH - 1:
                                s_qk.inc(ins)
                    t.wait_ge(s_ev.h, qkv_grp(tc, h, 2) + 1)
                    nep = v_ep(tc, h)
                    t.wait_ge(s_tpev.h, nep - 1)
                    bank = pbb[nep % 2]
                    for ts4 in range(4):
                        ins = t.transpose(bank[:, 128 * ts4:128 * (ts4 + 1)],
                                          vtmp[h % 2][:, 128 * ts4:128 * (ts4 + 1)],
                                          me.idb[:, :])
                        if ts4 == 3:
                            s_tp.inc(ins)

        @block.scalar
        def _(s):
            s.wait_ge(me.s_c1.h, 1)
            for tc in range(8):
                for j in range(4):
                    pt = 4 * tc + j
                    s.wait_ge(s_stat.h, pt + 1)
                    s_rstd.inc(s.activation(
                        me.rstd[pt % 2][:, :], me.mv[pt % 2][:, 1:2],
                        AF.Sqrt, bias=me.epsv[:, 0:1], scale=1.0))
                    s.wait_ge(s_nb.h, pt + 1)
                    if pt >= 2:
                        s.wait_ge(s_tp.h, ln_ep(pt - 2, 1) + 1)
                    s_lnx.inc(s.activation(
                        lnx[pt % 2][:, :], xbuf[pt % 2][:, :], AF.Identity,
                        bias=me.nbias[pt % 2][:, 0:1], scale=me.rstd[pt % 2][:, 0:1]))
                for h in range(4):
                    for P in range(3):
                        grp = qkv_grp(tc, h, P)
                        s.wait_ge(s_qk.h, grp + 1)
                        bank = pf[grp % 2]
                        if P == 0:
                            s_ev.inc(s.activation(
                                me.qT[h][:, 512 * tc:512 * (tc + 1)], bank[:, :],
                                AF.Identity, bias=me.bqcs[:, h:h + 1], scale=SCALE))
                        elif P == 1:
                            s_ev.inc(s.activation(
                                me.kT[h][:, 512 * tc:512 * (tc + 1)], bank[:, :],
                                AF.Identity, bias=me.bqcol[:, 4 + h:5 + h], scale=1.0))
                        else:
                            if 12 * tc + 3 * h >= 12:
                                tcp, hp = (tc, h - 2) if h >= 2 else (tc - 1, h + 2)
                                s.wait_ge(s_tp.h, v_ep(tcp, hp) + 1)
                            s_ev.inc(s.activation(
                                vtmp[h % 2][:, :], bank[:, :],
                                AF.Identity, bias=me.bqcol[:, 8 + h:9 + h], scale=1.0))

        self.p1_qk_total = s_qk.v
        self.p1_ev_total = s_ev.v
        self.p1_tp_total = s_tp.v
        self.p1_tpev_total = s_tpev.v
        self.s_lnx = s_lnx

    def phase2(self):
        block, me = self.block, self
        sem = self.sem
        s_sq = sem("s_sq")
        s_bsum = sem("s_bsum")
        s_red = sem("s_red")
        s_bnd = sem("s_bnd")
        s_sqr = sem("s_sqr")
        s_negb = sem("s_negb")
        self.s_negb = s_negb
        pf = self.p1_pf

        @block.vector
        def _(v):
            v.wait_ge(me.s_ev.h, me.p1_ev_total)
            for strip in range(8):
                h, which = strip // 2, strip % 2
                srcq = me.qT[h] if which == 0 else me.kT[h]
                for hh in range(2):
                    if strip + hh > 0:
                        v.wait_ge(s_bsum.h, 8 * strip + 4 * hh)
                    s_sq.inc(v.tensor_tensor(
                        me.sqbuf[:, :], srcq[:, 2048 * hh:2048 * (hh + 1)],
                        srcq[:, 2048 * hh:2048 * (hh + 1)], op=ALU.mult))
                    for c in range(4 * hh, 4 * hh + 4):
                        v.wait_ge(s_bsum.h, 8 * strip + c + 1)
                        s_red.inc(v.reduce_max(me.cmx8[:, c:c + 1],
                                               pf[c % 2][0:1, :], axis=AX.X))
                v.drain()
                s_bnd.inc(v.reduce_max(me.bnd[:, strip:strip + 1],
                                       me.cmx8[:, :], axis=AX.X))
                v.drain()

        @block.tensor
        def _(t):
            for strip in range(8):
                for c in range(8):
                    hh = c // 4
                    t.wait_ge(s_sq.h, 2 * strip + hh + 1)
                    n = 8 * strip + c
                    if n >= 2:
                        t.wait_ge(s_red.h, n - 1)
                    s_bsum.inc(t.matmul(pf[c % 2][:, :], me.oc1b[:, :],
                                        me.sqbuf[:, 512 * (c % 4):512 * (c % 4 + 1)],
                                        start=True, stop=True))

        @block.scalar
        def _(s):
            for h in range(4):
                s.wait_ge(s_bnd.h, 2 * h + 2)
                s_sqr.inc(s.activation(me.bnd[:, 8 + 2 * h:9 + 2 * h],
                                       me.bnd[:, 2 * h:2 * h + 1],
                                       AF.Sqrt, bias=0.0, scale=1.0))
                s_sqr.inc(s.activation(me.bnd[:, 9 + 2 * h:10 + 2 * h],
                                       me.bnd[:, 2 * h + 1:2 * h + 2],
                                       AF.Sqrt, bias=0.0, scale=1.0))

        @block.vector
        def _(v):
            for h in range(4):
                v.wait_ge(s_sqr.h, 2 * h + 2)
                s_negb.inc(v.tensor_scalar(
                    me.bnd[:, 16 + h:17 + h], me.bnd[:, 8 + 2 * h:9 + 2 * h],
                    me.bnd[:, 9 + 2 * h:10 + 2 * h], -1.0,
                    op0=ALU.mult, op1=ALU.mult))

    def phase3(self):
        nc, block, me = self.nc, self.block, self
        sbA = self.sbA
        es3 = self.esA.enter_context(contextlib.ExitStack())
        ps3 = lambda n, s, d: es3.enter_context(nc.psum_tensor(n, s, d))
        self.slbuf = [sbA(f"slbuf{i}", [1, 512], F32) for i in range(2)]
        self.sjs = sbA("sjs", [128, 4 * 32], F32)
        self.masks = sbA("masks", [128, 4 * 512], BF16)
        self.cibuf = [sbA(f"cibuf{i}", [128, 512], BF16) for i in range(2)]
        self.strip = sbA("strip", [128, 32 * 512], BF16)
        self.aoT = [sbA(f"aoT{h}", [128, S], BF16) for h in range(4)]
        ao_sb = sbA("ao_sb", [128, 512], BF16)
        rcp = sbA("rcp", [128, 4], F32)
        pav = [ps3(f"pav{i}", [128, 512], F32) for i in range(4)]
        pf = self.p1_pf
        pbb = self.p1_pbb

        sem = self.sem
        s_sc = sem("s_sc"); s_exp = sem("s_exp"); s_cib = sem("s_cib")
        s_av = sem("s_av"); s_norm = sem("s_norm")
        s_aotp = sem("s_aotp"); s_aot = sem("s_aot")
        self.s_aot = s_aot

        blocks = [(h, ib) for h in range(4) for ib in range(8)]
        jmax = [4 * ib + 4 for (h, ib) in blocks]
        cum = [0]
        for n in range(len(blocks)):
            cum.append(cum[-1] + jmax[n])
        sc0, exp0 = s_sc.v, s_exp.v
        s_sl = [sem("s_sl0"), sem("s_sl1")]

        qk_all, negb_all = me.s_qk.v, me.s_negb.v

        @block.gpsimd
        def _(g):
            g.wait_ge(me.s_qk.h, qk_all)
            g.wait_ge(me.s_negb.h, negb_all)
            me.sem_w.inc(g.dma_start(out=me.masks[:, :], in_=me.mask_d[:, :]), 16)
        w_mask = me.sem_w.v

        @block.sync
        def _(sy):
            sy.wait_ge(me.s_qk.h, qk_all)
            sy.wait_ge(me.s_negb.h, negb_all)
            me.sem_x.inc(sy.dma_start(out=me.sjs[:, :], in_=me.sjh_d[:, :]), 16)
            for n, (h, ib) in enumerate(blocks):
                if n >= 2:
                    sy.wait_ge(s_cib.h, n - 1)
                s_sl[n % 2].inc(sy.dma_start(
                    out=me.slbuf[n % 2][:, :],
                    in_=me.slneg_d[0:1, S * h + 512 * ib:S * h + 512 * (ib + 1)]), 16)
        sjs_mark = me.sem_x.v

        @block.tensor
        def _(t):
            t.wait_ge(me.s_tpev.h, me.p1_tpev_total)
            t.wait_ge(me.s_c1.h, 2)
            t.wait_ge(me.sem_w.h, w_mask)
            for n, (h, ib) in enumerate(blocks):
                jm = jmax[n]
                t.wait_ge(s_cib.h, n + 1)
                if n >= 1:
                    t.wait_ge(s_norm.h, n)
                for jt in range(jm):
                    t.wait_ge(s_exp.h, exp0 + cum[n] + max(0, jt - 1))
                    bank = pf[jt % 2]
                    partial = jt >= 4 * ib
                    t.matmul(bank[:, :], me.kT[h][:, 128 * jt:128 * (jt + 1)],
                             me.qT[h][:, 512 * ib:512 * (ib + 1)],
                             start=True, stop=False)
                    ins = t.matmul(bank[:, :], me.e1b[:, :],
                                   me.cibuf[n % 2][:, :],
                                   start=False, stop=not partial)
                    if partial:
                        k = jt - 4 * ib
                        ins = t.matmul(bank[:, :], me.idb[:, :],
                                       me.masks[:, 512 * k:512 * (k + 1)],
                                       start=False, stop=True)
                    s_sc.inc(ins)
                    if jt >= 1:
                        t.wait_ge(s_exp.h, exp0 + cum[n] + jt)
                        for isub in range(4):
                            ins = t.matmul(
                                pav[isub][:, 0:129],
                                me.strip[:, 512 * (jt - 1) + 128 * isub:
                                         512 * (jt - 1) + 128 * (isub + 1)],
                                me.Vt[h][:, 129 * (jt - 1):129 * jt],
                                start=(jt - 1 == 0), stop=False)
                t.wait_ge(s_exp.h, exp0 + cum[n] + jm)
                for isub in range(4):
                    ins = t.matmul(
                        pav[isub][:, 0:129],
                        me.strip[:, 512 * (jm - 1) + 128 * isub:
                                 512 * (jm - 1) + 128 * (isub + 1)],
                        me.Vt[h][:, 129 * (jm - 1):129 * jm],
                        start=(jm - 1 == 0), stop=True)
                    if isub == 3:
                        s_av.inc(ins)
                t.wait_ge(s_norm.h, n + 1)
                if n >= 2:
                    t.wait_ge(s_aot.h, n - 1)
                bank = pbb[n % 2]
                for isub in range(4):
                    ins = t.transpose(bank[:, 128 * isub:128 * (isub + 1)],
                                      ao_sb[:, 128 * isub:128 * (isub + 1)],
                                      me.idb[:, :])
                    if isub == 3:
                        s_aotp.inc(ins)

        @block.scalar
        def _(s):
            s.wait_ge(me.s_c1.h, 2)
            s.wait_ge(me.sem_x.h, sjs_mark)
            for n, (h, ib) in enumerate(blocks):
                s.wait_ge(s_av.h, n)
                if n == 0:
                    s.wait_ge(me.s_negb.h, 4)
                s.wait_ge(s_sl[n % 2].h, 16 * (n // 2 + 1))
                s_cib.inc(s.activation(
                    me.cibuf[n % 2][0:1, :], me.slbuf[n % 2][:, :],
                    AF.Identity, bias=me.bnd[:, 16 + h:17 + h], scale=1.0))
                for jt in range(jmax[n]):
                    s.wait_ge(s_sc.h, sc0 + cum[n] + jt + 1)
                    s_exp.inc(s.activation(
                        me.strip[:, 512 * jt:512 * (jt + 1)], pf[jt % 2][:, :],
                        AF.Exp, bias=me.sjs[:, 32 * h + jt:32 * h + jt + 1],
                        scale=1.0))

        @block.vector
        def _(v):
            v.wait_ge(me.s_qk.h, qk_all)
            v.memset(me.cibuf[0][:, :], 0.0)
            me.s_c1.inc(v.memset(me.cibuf[1][:, :], 0.0))
            for n, (h, ib) in enumerate(blocks):
                v.wait_ge(s_av.h, n + 1)
                for isub in range(4):
                    v.reciprocal(rcp[:, isub:isub + 1], pav[isub][:, 128:129])
                v.drain()
                if n >= 1:
                    v.wait_ge(s_aotp.h, n)
                for isub in range(4):
                    ins = v.tensor_scalar(
                        ao_sb[:, 128 * isub:128 * (isub + 1)],
                        pav[isub][:, 0:128], rcp[:, isub:isub + 1], None,
                        op0=ALU.mult)
                    if isub == 3:
                        s_norm.inc(ins)
                v.wait_ge(s_aotp.h, n + 1)
                bank = pbb[n % 2]
                s_aot.inc(v.tensor_copy(
                    me.aoT[h][:, 512 * ib:512 * (ib + 1)], bank[:, 0:512]))

    def phase4(self):
        nc, block, me = self.nc, self.block, self
        sem = self.sem
        s_cx = [sem(f"s_cx{h}") for h in range(4)]
        s_cc = sem("s_cc")
        self.s_at = [sem(f"s_at{k}") for k in range(4)]

        @block.sync
        def _(sy):
            for h in range(4):
                sy.wait_ge(me.s_aot.h, 8 * (h + 1))
                s_cx[h].inc(sy.dma_start(
                    out=me.ccin.ap()[128 * h:128 * (h + 1), :],
                    in_=me.aoT[h][:, :]), 16)

        @block.gpsimd
        def _(g):
            for k in range(4):
                g.wait_ge(s_cx[k].h, 16)
                s_cc.inc(g.collective_compute(
                    "AllGather", ALU.bypass,
                    replica_groups=[[0, 1, 2, 3], [4, 5, 6, 7]],
                    ins=[me.ccin.ap()[128 * k:128 * (k + 1), :].opt()],
                    outs=[me.ccout.ap()[512 * k:512 * (k + 1), :].opt()]), 1)
        self.s_cc = s_cc

    def phase5(self):
        nc, block, me = self.nc, self.block, self
        es5 = self.es.enter_context(contextlib.ExitStack())
        self.es5 = es5
        sb5 = lambda n, s, d: es5.enter_context(nc.sbuf_tensor(n, s, d))
        ps5 = lambda n, s, d: es5.enter_context(nc.psum_tensor(n, s, d))
        sem = self.sem
        atf = sb5("atf", [128, NCH * 1024], BF16)
        wo_sb = sb5("wo_sb", [128, NCH * 2048], BF16)
        bob = sb5("bob", [128, D], BF16)
        xb5 = [sb5(f"xb5_{i}", [128, D], F32) for i in range(2)]
        y1 = sb5("y1", [128, D], F32)
        x1t = [sb5(f"x1t{i}", [128, D], F32) for i in range(2)]
        pw = [ps5(f"pw{i}", [128, 512], F32) for i in range(4)]

        s_wo = sem("s_wo")
        s_qk5 = sem("s_qk5")
        s_st5, s_rs5, s_nb5 = me.s_stat, me.s_rstd, me.s_nb
        st5_0, rs5_0, nb5_0 = s_st5.v, s_rs5.v, s_nb5.v
        s_y1 = sem("s_y1")
        s_x1 = sem("s_x1")
        s_x1s = sem("s_x1s")
        s_xb5 = [sem("s_xb5a"), sem("s_xb5b")]
        s_bz = sem("s_bz")
        self.s_x1s = s_x1s

        atf3 = atf[:, :].rearrange("p (c t) -> p c t", c=NCH)
        wo3 = wo_sb[:, :].rearrange("p (c t) -> p c t", c=NCH)

        @block.gpsimd
        def _(g):
            view = me.ccout.ap().rearrange("r (q c) -> (r q) c", q=4)
            for k in range(4):
                g.wait_ge(me.s_cc.h, k + 1)
                for r in range(4):
                    ch = 4 * r + k
                    me.s_at[k].inc(g.indirect_dma_start(
                        out=atf3[:, ch, :], out_offset=None, in_=view,
                        in_offset=IndirectOffsetOnAxis(ap=me.gidx_s[:, ch:ch + 1], axis=0)), 16)
            for oc in range(4):
                srcw = me.wo[:, 512 * oc:512 * (oc + 1)].rearrange(
                    "(c p) q -> p c q", p=128)
                dstw = wo3[:, :, 512 * oc:512 * (oc + 1)]
                s_wo.inc(g.dma_start(out=dstw, in_=srcw), 16)
            g.wait_ge(s_bz.h, 1)
            s_wo.inc(g.dma_start(out=bob[0:1, :], in_=me.bo_row[:, :]), 16)

        @block.sync
        def _(sy):
            for i in range(8):
                if i >= 2:
                    sy.wait_ge(s_x1.h, i - 1)
                s_xb5[i % 2].inc(sy.dma_start(
                    out=xb5[i % 2][:, :],
                    in_=me.xsl[128 * i:128 * (i + 1), :]), 16)
                sy.wait_ge(s_x1.h, i + 1)
                s_x1s.inc(sy.dma_start(
                    out=me.x1dram[128 * i:128 * (i + 1), :],
                    in_=x1t[i % 2][:, :]), 16)

        @block.tensor
        def _(t):
            t.wait_ge(s_wo.h, 80)
            for i in range(8):
                if i >= 1:
                    t.wait_ge(s_y1.h, 4 * i)
                    t.wait_ge(s_st5.h, st5_0 + i)
                for oc in range(4):
                    for k in range(4):
                        if i == 0 and oc == 0:
                            t.wait_ge(me.s_at[k].h, 64)
                        for r in range(4):
                            ch = 4 * r + k
                            t.matmul(pw[oc][:, :], atf3[:, ch, 128 * i:128 * (i + 1)],
                                     wo3[:, ch, 512 * oc:512 * (oc + 1)],
                                     start=(k == 0 and r == 0), stop=False)
                    ins = t.matmul(pw[oc][:, :], me.e1b[:, :],
                                   bob[:, 512 * oc:512 * (oc + 1)],
                                   start=False, stop=True)
                    s_qk5.inc(ins)

        @block.vector
        def _(v):
            s_bz.inc(v.memset(bob[:, :], 0.0))
            for i in range(8):
                v.wait_ge(s_qk5.h, 4 * i + 4)
                for oc in range(4):
                    v.bn_stats(me.stats[:, 6 * oc:6 * oc + 6], pw[oc][:, :])
                v.drain()
                s_st5.inc(v.bn_aggr(me.mv[i % 2][:, :], me.stats[:, :]))
                v.wait_ge(s_rs5.h, rs5_0 + i + 1)
                v.reciprocal(me.rstd[i % 2][:, :], me.rstd[i % 2][:, :])
                v.drain()
                s_nb5.inc(v.tensor_scalar(
                    me.nbias[i % 2][:, :], me.rstd[i % 2][:, :],
                    me.mv[i % 2][:, 0:1], -1.0, op0=ALU.mult, op1=ALU.mult))
                v.wait_ge(s_y1.h, 4 * i + 4)
                v.wait_ge(s_xb5[i % 2].h, 16 * (i // 2 + 1))
                if i >= 2:
                    v.wait_ge(s_x1s.h, 16 * i)
                s_x1.inc(v.tensor_tensor(x1t[i % 2][:, :], y1[:, :],
                                         xb5[i % 2][:, :], op=ALU.add))

        @block.scalar
        def _(s):
            for i in range(8):
                s.wait_ge(s_st5.h, st5_0 + i + 1)
                s_rs5.inc(s.activation(
                    me.rstd[i % 2][:, :], me.mv[i % 2][:, 1:2],
                    AF.Sqrt, bias=me.epsv[:, 0:1], scale=1.0))
                s.wait_ge(s_nb5.h, nb5_0 + i + 1)
                for oc in range(4):
                    s_y1.inc(s.activation(
                        y1[:, 512 * oc:512 * (oc + 1)], pw[oc][:, :], AF.Identity,
                        bias=me.nbias[i % 2][:, 0:1], scale=me.rstd[i % 2][:, 0:1]))

    def phase6(self):
        nc, block, me = self.nc, self.block, self
        es6 = self.es.enter_context(contextlib.ExitStack())
        sb6 = lambda n, s, d: es6.enter_context(nc.sbuf_tensor(n, s, d))
        sem = self.sem


        wvT = [sb6(f"wvT{ib}", [128, 4 * 8 * 128], BF16) for ib in range(2)]
        wv_sb = [sb6(f"wv{ib}", [128, 512], BF16) for ib in range(2)]
        wf = sb6("wf", [128, 8 * 8], F32)
        wfT = sb6("wfT", [128, 1024], BF16)

        esL = es6.enter_context(contextlib.ExitStack())
        sbL = lambda n, s, d: esL.enter_context(nc.sbuf_tensor(n, s, d))
        ln2T = sbL("ln2T", [128, NCH * 1024], BF16)
        uall1 = sbL("uall1", [128, 8 * 512], BF16)
        uall3 = sbL("uall3", [128, 8 * 512], BF16)
        u1sb = [uall1[0:32, :], uall1[32:64, :]]
        u3sb = [uall3[0:32, :], uall3[32:64, :]]
        web = sbL("web", [128, 8 * 1024], BF16)
        ln2T3 = ln2T[:, :].rearrange("p (c t) -> p c t", c=NCH)
        web3 = web[:, :].rearrange("p (e t) -> p e t", e=8)

        esa = es6.enter_context(contextlib.ExitStack())
        sba = lambda n, s, d: esa.enter_context(nc.sbuf_tensor(n, s, d))
        psa = lambda n, s, d: esa.enter_context(nc.psum_tensor(n, s, d))
        xb6_ = sba("xb6", [128, D], F32)
        xb6 = [xb6_, xb6_]
        ln6_ = sba("ln6", [128, D], BF16)
        ln6 = [ln6_, ln6_]
        aubuf = sba("aubuf", [128, NCH * 256], BF16)
        gw_sb = sba("gw_sb", [128, NCH * 8], BF16)
        lgs = sba("lgs", [128, 8], F32)
        vals = sba("vals", [128, 8], F32)
        dbuf = sba("dbuf", [128, 4], F32)
        eqa = sba("eqa", [128, 8], F32)
        eqb = sba("eqb", [128, 8], F32)
        pga = [psa(f"pga{i}", [128, 512], F32) for i in range(2)]
        pgb = [psa(f"pgb{i}", [128, 1024], BF16) for i in range(2)]
        pgc = psa("pgc", [128, 512], F32)

        s_x6 = [sem("s_x6a"), sem("s_x6b")]
        s_st6, s_rs6, s_nb6 = me.s_stat, me.s_rstd, me.s_nb
        st6_0, rs6_0, nb6_0 = s_st6.v, s_rs6.v, s_nb6.v
        s_ln6 = sem("s_ln6")
        s_tp6 = sem("s_tp6"); s_te6 = sem("s_te6")
        s_au = sem("s_au")
        s_gw = sem("s_gw")
        s_lg = sem("s_lg")
        s_gv = sem("s_gv")
        s_gs = sem("s_gs")
        s_wfv = sem("s_wfv")
        s_wft = sem("s_wft")
        s_wfe = sem("s_wfe")
        s_web = sem("s_web")
        s_webe = sem("s_webe")
        s_um = sem("s_um")
        s_ue = sem("s_ue")

        au3v = aubuf[:, :].rearrange("p (c t) -> p c t", c=NCH)
        gw3 = gw_sb[:, :].rearrange("p (c t) -> p c t", c=NCH)

        @block.gpsimd
        def _(g):
            g.wait_ge(me.s_x1s.h, 16 * 8)
            srcg = me.gwT[:, :].rearrange("(c p) q -> p c q", p=128)
            s_gw.inc(g.dma_start(out=gw3, in_=srcg), 16)
            srca = me.au1[:, :].rearrange("(c p) q -> p c q", p=128)
            s_au.inc(g.dma_start(out=au3v, in_=srca), 16)

        @block.sync
        def _(sy):
            sy.wait_ge(me.s_x1s.h, 16 * 8)
            for it in range(8):
                if it >= 1:
                    sy.wait_ge(s_ln6.h, it)
                s_x6[it % 2].inc(sy.dma_start(
                    out=xb6[it % 2][:, :],
                    in_=me.x1dram[128 * it:128 * (it + 1), :]), 16)

        @block.vector
        def _(v):
            v.drain()
            v.memset(wfT[:, :], 0.0)
            v.memset(uall1[:, :], 0.0)
            v.memset(uall3[:, :], 0.0)
            for it in range(8):
                v.wait_ge(s_x6[it % 2].h, 16 * (it // 2 + 1))
                xt = xb6[it % 2]
                for q in range(4):
                    v.bn_stats(me.stats[:, 6 * q:6 * q + 6],
                               xt[:, 512 * q:512 * (q + 1)])
                v.drain()
                s_st6.inc(v.bn_aggr(me.mv[it % 2][:, :], me.stats[:, :]))
                if it >= 1:
                    for half in range(2):
                        nep = 2 * (it - 1) + half
                        v.wait_ge(s_tp6.h, nep + 1)
                        bank = pgb[nep % 2]
                        dst = ln2T3[:, 8 * half:8 * half + 8,
                                    128 * (it - 1):128 * it]
                        srcb = bank[:, 0:1024].rearrange("p (c t) -> p c t", c=8)
                        s_te6.inc(v.tensor_copy(dst, srcb))
                v.wait_ge(s_rs6.h, rs6_0 + it + 1)
                v.reciprocal(me.rstd[it % 2][:, :], me.rstd[it % 2][:, :])
                v.drain()
                s_nb6.inc(v.tensor_scalar(
                    me.nbias[it % 2][:, :], me.rstd[it % 2][:, :],
                    me.mv[it % 2][:, 0:1], -1.0, op0=ALU.mult, op1=ALU.mult))
                if it >= 1:
                    self.emit_gate_v(v, it - 1, lgs, vals, dbuf, eqa, eqb, wf,
                                     pgc, s_lg, s_gv, s_gs, s_wfv)
            for half in range(2):
                nep = 2 * 7 + half
                v.wait_ge(s_tp6.h, nep + 1)
                bank = pgb[nep % 2]
                dst = ln2T3[:, 8 * half:8 * half + 8, 128 * 7:128 * 8]
                srcb = bank[:, 0:1024].rearrange("p (c t) -> p c t", c=8)
                s_te6.inc(v.tensor_copy(dst, srcb))
            self.emit_gate_v(v, 7, lgs, vals, dbuf, eqa, eqb, wf,
                             pgc, s_lg, s_gv, s_gs, s_wfv)
            for it in range(8):
                v.wait_ge(s_wft.h, it + 1)
                s_wfe.inc(v.tensor_copy(wfT[0:8, 128 * it:128 * (it + 1)],
                                        pga[it % 2][0:8, 0:128]))
            for e in range(8):
                for hf in range(2):
                    n = 2 * e + hf
                    v.wait_ge(s_web.h, n + 1)
                    s_webe.inc(v.tensor_copy(
                        web3[:, e, 512 * hf:512 * (hf + 1)],
                        pga[n % 2][:, :]))
            for n in range(32):
                v.wait_ge(s_um.h, n + 1)
                path, ib, slot = n // 16, (n // 8) % 2, n % 8
                usb = (u1sb if path == 0 else u3sb)[ib]
                s_ue.inc(v.tensor_scalar(
                    usb[0:16, 512 * slot:512 * (slot + 1)],
                    pga[n % 2][0:16, :], LORA, None, op0=ALU.mult))

        @block.scalar
        def _(s):
            for it in range(8):
                s.wait_ge(s_st6.h, st6_0 + it + 1)
                s_rs6.inc(s.activation(
                    me.rstd[it % 2][:, :], me.mv[it % 2][:, 1:2],
                    AF.Sqrt, bias=me.epsv[:, 0:1], scale=1.0))
                s.wait_ge(s_nb6.h, nb6_0 + it + 1)
                if it >= 1:
                    s.wait_ge(s_tp6.h, 2 * (it - 1) + 2)
                s_ln6.inc(s.activation(
                    ln6[it % 2][:, :], xb6[it % 2][:, :], AF.Identity,
                    bias=me.nbias[it % 2][:, 0:1], scale=me.rstd[it % 2][:, 0:1]))
                if it >= 1:
                    s.wait_ge(s_gv.h, it)
                    s_gs.inc(s.activation(dbuf[:, 1:2], dbuf[:, 0:1],
                                          AF.Sigmoid, bias=0.0, scale=1.0))
            s.wait_ge(s_gv.h, 8)
            s_gs.inc(s.activation(dbuf[:, 1:2], dbuf[:, 0:1],
                                  AF.Sigmoid, bias=0.0, scale=1.0))

        @block.tensor
        def _(t):
            t.wait_ge(s_gw.h, 16)
            t.wait_ge(s_au.h, 16)
            for it in range(8):
                t.wait_ge(s_ln6.h, it + 1)
                for half in range(2):
                    nep = 2 * it + half
                    if nep >= 2:
                        t.wait_ge(s_te6.h, nep - 1)
                    bank = pgb[nep % 2]
                    for c8 in range(8):
                        ch = 8 * half + c8
                        ins = t.transpose(bank[:, 128 * c8:128 * (c8 + 1)],
                                          ln6[it % 2][:, 128 * ch:128 * (ch + 1)],
                                          me.idb[:, :])
                        if c8 == 7:
                            s_tp6.inc(ins)
                t.wait_ge(s_te6.h, 2 * it + 2)
                if it >= 1:
                    t.wait_ge(s_gv.h, it)
                for ch in range(NCH):
                    ins = t.matmul(pgc[:, 0:8], ln2T3[:, ch, 128 * it:128 * (it + 1)],
                                   gw3[:, ch, :], start=(ch == 0), stop=(ch == NCH - 1))
                    if ch == NCH - 1:
                        s_lg.inc(ins)
            for it in range(8):
                t.wait_ge(s_wfv.h, it + 1)
                if it >= 2:
                    t.wait_ge(s_wfe.h, it - 1)
                s_wft.inc(t.transpose(pga[it % 2][0:8, 0:128],
                                      wf[:, 8 * it:8 * (it + 1)], me.idf[:, :]))
            t.wait_ge(s_wfe.h, 8)
            for e in range(8):
                for hf in range(2):
                    n = 2 * e + hf
                    if n >= 2:
                        t.wait_ge(s_webe.h, n - 1)
                    s_web.inc(t.matmul(
                        pga[n % 2][:, :], me.onehots[:, 128 * e:128 * (e + 1)],
                        wfT[:, 512 * hf:512 * (hf + 1)], start=True, stop=True))
            t.wait_ge(s_webe.h, 16)
            for n in range(32):
                path, ib, slot = n // 16, (n // 8) % 2, n % 8
                if n == 16:
                    t.wait_ge(s_au.h, 32)
                if n >= 2:
                    t.wait_ge(s_ue.h, n - 1)
                for ch in range(NCH):
                    ins = t.matmul(
                        pga[n % 2][0:32, :],
                        au3v[:, ch, 32 * slot:32 * (slot + 1)],
                        ln2T3[:, ch, 512 * ib:512 * (ib + 1)],
                        start=(ch == 0), stop=(ch == NCH - 1))
                    if ch == NCH - 1:
                        s_um.inc(ins)

        @block.gpsimd
        def _(g):
            g.wait_ge(s_um.h, 16)
            srca = me.au3[:, :].rearrange("(c p) q -> p c q", p=128)
            s_au.inc(g.dma_start(out=au3v, in_=srca), 16)

        self.esa = esa
        self.s_ue, self.s_webe = s_ue, s_webe
        self.p6_shared = dict(wvT=wvT, wv_sb=wv_sb, wf=wf,
                              ln2T3=ln2T3, u1sb=u1sb, u3sb=u3sb, web3=web3,
                              uall1=uall1, uall3=uall3, esL=esL, es6=es6)

    def emit_gate_v(self, v, it, lgs, vals, dbuf, eqa, eqb, wf, pgc,
                    s_lg, s_gv, s_gs, s_wfv):
        me = self
        v.wait_ge(s_lg.h, it + 1)
        v.tensor_copy(lgs[:, :], pgc[:, 0:8])
        v.drain()
        v.max(vals[:, :], lgs[:, :])
        v.drain()
        s_gv.inc(v.tensor_scalar(dbuf[:, 0:1], vals[:, 0:1], vals[:, 1:2],
                                 None, op0=ALU.subtract))
        v.wait_ge(s_gs.h, it + 1)
        v.tensor_scalar(dbuf[:, 2:3], dbuf[:, 1:2], -1.0, 1.0,
                        op0=ALU.mult, op1=ALU.add)
        v.tensor_scalar(eqa[:, :], lgs[:, :], vals[:, 0:1], None, op0=ALU.is_equal)
        v.tensor_scalar(eqb[:, :], lgs[:, :], vals[:, 1:2], None, op0=ALU.is_equal)
        v.drain()
        v.tensor_tensor(lgs[:, :], eqb[:, :], eqa[:, :], op=ALU.mult)
        v.drain()
        v.tensor_tensor(eqb[:, :], eqb[:, :], lgs[:, :], op=ALU.subtract)
        v.tensor_scalar(eqa[:, :], eqa[:, :], dbuf[:, 1:2], None, op0=ALU.mult)
        v.drain()
        s_wfv.inc(v.scalar_tensor_tensor(wf[:, 8 * it:8 * (it + 1)], eqb[:, :],
                                         dbuf[:, 2:3], eqa[:, :],
                                         op0=ALU.mult, op1=ALU.add))

    def phase6b(self):
        nc, block, me = self.nc, self.block, self
        sh = self.p6_shared
        es6 = sh["es6"]
        ln2T3, web3 = sh["ln2T3"], sh["web3"]
        uall1, uall3 = sh["uall1"], sh["uall3"]
        sem = self.sem
        self.pv = [es6.enter_context(nc.psum_tensor(f"pv{ib}", [128, 512], F32))
                   for ib in range(2)]
        pv = self.pv
        esb = es6.enter_context(contextlib.ExitStack())
        sbB = lambda n, s, d: esb.enter_context(nc.sbuf_tensor(n, s, d))
        psB = lambda n, s, d: esb.enter_context(nc.psum_tensor(n, s, d))
        self.esb = esb
        w1b = sbB("w1b", [128, NCH * 128], BF16)
        w3b = sbB("w3b", [128, NCH * 128], BF16)
        a2b = [sbB(f"a2b{i}", [128, 128], BF16) for i in range(2)]
        b1b = [[sbB(f"b1b{i}_{ib}", [128, 8 * 128], BF16) for ib in range(2)]
               for i in range(2)]
        b3b = [[sbB(f"b3b{i}_{ib}", [128, 8 * 128], BF16) for ib in range(2)]
               for i in range(2)]
        bs1_ = [sbB(f"bs1_{ib}", [128, 512], BF16) for ib in range(2)]
        bs3_ = [sbB(f"bs3_{ib}", [128, 512], BF16) for ib in range(2)]
        bs1 = [bs1_, bs1_]
        bs3 = [bs3_, bs3_]
        s1 = [sbB(f"s1_{i}", [128, 512], BF16) for i in range(2)]
        act = [sbB(f"act_{i}", [128, 512], BF16) for i in range(4)]
        wact = [sbB(f"wact_{i}", [128, 512], BF16) for i in range(4)]
        HB1 = [psB(f"hb1_{i}", [128, 512], F32) for i in range(2)]
        HB3 = [psB(f"hb3_{i}", [128, 512], F32) for i in range(2)]
        pmac = [psB(f"pmac{ib}", [128, 512], F32) for ib in range(2)]

        s_wt = [sem("s_wt0"), sem("s_wt1")]
        s_b1 = sem("s_b1"); s_b3 = sem("s_b3")
        s_bev1 = sem("s_bev1"); s_bev3 = sem("s_bev3")
        s_h1e = sem("s_h1e"); s_h3e = sem("s_h3e")
        s_silu = sem("s_silu"); s_act = sem("s_act"); s_wact = sem("s_wact")
        s_vm = sem("s_vm"); s_imm = sem("s_imm")
        s_mev = [sem("s_mev0"), sem("s_mev1")]
        s_mst = sem("s_mst")
        mstage = [sbB(f"mstage{i}", [128, 512], BF16) for i in range(2)]
        self.s_mst = s_mst
        s_bbz = sem("s_bbz")
        self.s_vm, self.s_mev = s_vm, s_mev
        self.s_vm_total = 16 * NFT
        ue_total = 32

        pool1_free = []
        pool3_free = []

        @block.gpsimd
        def _(g):
            g.wait_ge(me.s_ue.h, ue_total)
            g.wait_ge(s_bbz.h, 1)
            for ft in range(NFT):
                par = ft % 2
                s = s_wt[par]
                if ft >= 1:
                    g.wait_ge(s_b1.h, 2 * ft)
                    g.wait_ge(s_b3.h, 2 * ft)
                if ft >= 2:
                    g.wait_ge(s_h1e.h, 16 * (ft - 1))
                    g.wait_ge(s_h3e.h, 16 * (ft - 1))
                    g.wait_ge(s_vm.h, 16 * (ft - 1))
                src1 = me.w1T[:, 128 * ft:128 * (ft + 1)].rearrange("(c p) q -> p c q", p=128)
                s.inc(g.dma_start(out=w1b[:, :].rearrange("p (c q) -> p c q", c=NCH), in_=src1), 16)
                src3 = me.w3T[:, 128 * ft:128 * (ft + 1)].rearrange("(c p) q -> p c q", p=128)
                s.inc(g.dma_start(out=w3b[:, :].rearrange("p (c q) -> p c q", c=NCH), in_=src3), 16)
                s.inc(g.dma_start(out=a2b[par][:, :], in_=me.a2cat[128 * ft:128 * (ft + 1), :]), 16)
                s.inc(g.dma_start(out=b1b[par][0][0:16, :], in_=me.b1pk[ft:ft + 1].rearrange("o p q -> (o p) q")), 16)
                s.inc(g.dma_start(out=b1b[par][1][32:48, :], in_=me.b1pk[ft:ft + 1].rearrange("o p q -> (o p) q")), 16)
                s.inc(g.dma_start(out=b3b[par][0][0:16, :], in_=me.b3pk[ft:ft + 1].rearrange("o p q -> (o p) q")), 16)
                s.inc(g.dma_start(out=b3b[par][1][32:48, :], in_=me.b3pk[ft:ft + 1].rearrange("o p q -> (o p) q")), 16)

        @block.sync
        def _(sy):
            for ft in range(NFT):
                for ib in range(2):
                    n = 2 * ft + ib
                    sy.wait_ge(s_mev[ib].h, ft + 1)
                    s_mst.inc(sy.dma_start(
                        out=me.maccd[ib][ft:ft + 1].rearrange("o p q -> (o p) q"),
                        in_=mstage[n % 2][:, :]), 16)

        LAG = 3

        @block.tensor
        def _(t):
            t.wait_ge(me.s_ue.h, ue_total)
            t.wait_ge(me.s_webe.h, 16)
            w1v = w1b[:, :].rearrange("p (c q) -> p c q", c=NCH)
            w3v = w3b[:, :].rearrange("p (c q) -> p c q", c=NCH)
            pending = []

            def emit_vi(step):
                ft_, e_, ib_, G_ = step
                t.wait_ge(s_act.h, G_ + 1)
                for isub in range(4):
                    off = 16 * (8 * isub + e_)
                    ins = t.matmul(
                        pv[ib_][:, off:off + 16],
                        act[G_ % 4][:, 128 * isub:128 * (isub + 1)],
                        a2b[ft_ % 2][:, 16 * e_:16 * (e_ + 1)],
                        start=(ft_ == 0 and e_ == 0 and isub == 0),
                        stop=(ft_ == NFT - 1 and e_ == 7 and isub == 3),
                        skip_group_check=True)
                    if isub == 3:
                        s_vm.inc(ins)
                t.wait_ge(s_wact.h, G_ + 1)
                if e_ == 0:
                    t.wait_ge(s_mev[ib_].h, ft_)
                s_imm.inc(t.matmul(
                    pmac[ib_][:, :], me.idb[:, :], wact[G_ % 4][:, :],
                    start=(e_ == 0), stop=(e_ == 7)))

            for ft in range(NFT):
                par = ft % 2
                t.wait_ge(s_wt[par].h, 112 * (ft // 2 + 1))
                for ib in range(2):
                    u = len(pool1_free)
                    if u >= 2:
                        t.wait_ge(*pool1_free[u - 2])
                    bank = HB1[u % 2]
                    for ch in range(NCH):
                        ins = t.matmul(bank[:, :], w1v[:, ch, :],
                                       ln2T3[:, ch, 512 * ib:512 * (ib + 1)],
                                       start=(ch == 0), stop=(ch == NCH - 1))
                    s_b1.inc(ins)
                    pool1_free.append((s_bev1.h, 2 * ft + ib + 1))
                    u = len(pool3_free)
                    if u >= 2:
                        t.wait_ge(*pool3_free[u - 2])
                    bank = HB3[u % 2]
                    for ch in range(NCH):
                        ins = t.matmul(bank[:, :], w3v[:, ch, :],
                                       ln2T3[:, ch, 512 * ib:512 * (ib + 1)],
                                       start=(ch == 0), stop=(ch == NCH - 1))
                    s_b3.inc(ins)
                    pool3_free.append((s_bev3.h, 2 * ft + ib + 1))
                for e in range(8):
                    for ib in range(2):
                        G = 16 * ft + 2 * e + ib
                        u = len(pool1_free)
                        if u >= 2:
                            t.wait_ge(*pool1_free[u - 2])
                        t.wait_ge(s_bev1.h, 2 * ft + ib + 1)
                        bank = HB1[u % 2]
                        t.matmul(bank[:, :], me.idb[:, :], bs1[par][ib][:, :],
                                 start=True, stop=False)
                        s_h1e.inc(t.matmul(bank[:, :], b1b[par][ib][:, 128 * e:128 * (e + 1)],
                                           uall1[:, 512 * e:512 * (e + 1)],
                                           start=False, stop=True))
                        pool1_free.append((s_silu.h, G + 1))
                        u = len(pool3_free)
                        if u >= 2:
                            t.wait_ge(*pool3_free[u - 2])
                        t.wait_ge(s_bev3.h, 2 * ft + ib + 1)
                        bank = HB3[u % 2]
                        t.matmul(bank[:, :], me.idb[:, :], bs3[par][ib][:, :],
                                 start=True, stop=False)
                        s_h3e.inc(t.matmul(bank[:, :], b3b[par][ib][:, 128 * e:128 * (e + 1)],
                                           uall3[:, 512 * e:512 * (e + 1)],
                                           start=False, stop=True))
                        pool3_free.append((s_act.h, G + 1))
                        pending.append((ft, e, ib, G))
                        while len(pending) > LAG:
                            emit_vi(pending.pop(0))
            while pending:
                emit_vi(pending.pop(0))

        @block.scalar
        def _(s):
            u1 = 0
            for ft in range(NFT):
                par = ft % 2
                for ib in range(2):
                    s.wait_ge(s_b1.h, 2 * ft + ib + 1)
                    bank = HB1[u1 % 2]
                    s_bev1.inc(s.copy(bs1[par][ib][:, :], bank[:, :]))
                    u1 += 1
                    s.wait_ge(s_b3.h, 2 * ft + ib + 1)
                    bank = HB3[(u1 - 1) % 2]
                    s_bev3.inc(s.copy(bs3[par][ib][:, :], bank[:, :]))
                for e in range(8):
                    for ib in range(2):
                        G = 16 * ft + 2 * e + ib
                        s.wait_ge(s_h1e.h, G + 1)
                        if G >= 2:
                            s.wait_ge(s_act.h, G - 1)
                        bank = HB1[(18 * ft + 2 + 2 * e + ib) % 2]
                        s_silu.inc(s.activation(s1[G % 2][:, :], bank[:, :],
                                                AF.Silu, bias=0.0, scale=1.0))
                u1 = 18 * (ft + 1)

        @block.vector
        def _(v):
            v.drain()
            ins = None
            for par in range(2):
                for ibv in range(2):
                    v.memset(b1b[par][ibv][:, :], 0.0)
                    ins = v.memset(b3b[par][ibv][:, :], 0.0)
            s_bbz.inc(ins)
            for ft in range(NFT):
                for e in range(8):
                    for ib in range(2):
                        G = 16 * ft + 2 * e + ib
                        v.wait_ge(s_silu.h, G + 1)
                        v.wait_ge(s_h3e.h, G + 1)
                        if G >= 4:
                            v.wait_ge(s_vm.h, G - 3)
                        bank = HB3[(18 * ft + 2 + 2 * e + ib) % 2]
                        s_act.inc(v.tensor_tensor(
                            act[G % 4][:, :], s1[G % 2][:, :], bank[:, :],
                            op=ALU.mult))
                    for ib in range(2):
                        G = 16 * ft + 2 * e + ib
                        if G >= 4:
                            v.wait_ge(s_imm.h, G - 3)
                        if SIM_SAFE[0]:
                            v.drain()
                        s_wact.inc(v.tensor_tensor(
                            wact[G % 4][:, :], act[G % 4][:, :],
                            web3[:, e, 512 * ib:512 * (ib + 1)], op=ALU.mult))
                        if e == 1 and ft >= 1:
                            v.wait_ge(s_imm.h, 16 * (ft - 1) + 15 + ib)
                            n = 2 * (ft - 1) + ib
                            if n >= 2:
                                v.wait_ge(s_mst.h, 16 * n)
                            s_mev[ib].inc(v.tensor_copy(
                                mstage[n % 2][:, :], pmac[ib][:, :]))
            for ib in range(2):
                v.wait_ge(s_imm.h, 16 * (NFT - 1) + 15 + ib)
                n = 2 * (NFT - 1) + ib
                v.wait_ge(s_mst.h, 16 * n)
                s_mev[ib].inc(v.tensor_copy(mstage[n % 2][:, :], pmac[ib][:, :]))

    def phase6c(self):
        nc, block, me = self.nc, self.block, self
        sh = self.p6_shared
        es6, esL = sh["es6"], sh["esL"]
        wvT, wv_sb, wf = sh["wvT"], sh["wv_sb"], sh["wf"]
        pv = self.pv
        self.esb.close()
        esL.close()
        sem = self.sem
        esc = es6.enter_context(contextlib.ExitStack())
        sbC = lambda n, s, d: esc.enter_context(nc.sbuf_tensor(n, s, d))
        psC = lambda n, s, d: esc.enter_context(nc.psum_tensor(n, s, d))
        b2full = sbC("b2full", [128, E * D], BF16)
        y2v = [sbC(f"y2v{i}", [128, D], F32) for i in range(4)]
        x1b = [sbC(f"x1b{i}", [128, D], F32) for i in range(2)]
        x2t_ = sbC("x2t", [128, D], F32)
        x2t = [x2t_, x2t_]
        CH6 = 8
        w2b = [sbC(f"w2b{i}", [128, CH6 * 512], BF16) for i in range(2)]
        mb = [sbC(f"mb{i}", [128, CH6 * 512], BF16) for i in range(2)]
        po = [psC(f"po{i}", [128, 512], F32) for i in range(4)]
        pbc = psC("pbc", [16, 1024], BF16)

        s_b2 = sem("s_b2")
        s_zz = sem("s_zz")
        s_wv = [sem("s_wv0"), sem("s_wv1")]
        s_wvt = sem("s_wvt"); s_wve = sem("s_wve")
        s_w2 = [sem("s_w2a"), sem("s_w2b")]
        s_m2 = [sem("s_m2a"), sem("s_m2b")]
        s_w2c = sem("s_w2c")
        blocks6 = []
        ft0 = 0
        while ft0 < NFT:
            blocks6.append((ft0, min(NFT, ft0 + CH6)))
            ft0 += CH6
        s_o2 = sem("s_o2")
        s_ev6 = sem("s_ev6")
        s_st6c, s_rs6c, s_nb6c = me.s_stat, me.s_rstd, me.s_nb
        st6c_0, rs6c_0, nb6c_0 = s_st6c.v, s_rs6c.v, s_nb6c.v
        s_y6 = sem("s_y6")
        s_x2 = sem("s_x2")
        s_xc = [sem("s_xc0"), sem("s_xc1")]
        s_outs = sem("s_outs")
        self.s_outs = s_outs

        @block.vector
        def _(v):
            v.drain()
            v.memset(b2full[:, :], 0.0)
            v.memset(wvT[0][:, :], 0.0)
            s_zz.inc(v.memset(wvT[1][:, :], 0.0))
            for ib in range(2):
                v.wait_ge(me.s_vm.h, me.s_vm_total)
                for isub in range(4):
                    for e in range(8):
                        off = 16 * (8 * isub + e)
                        col = 8 * (4 * ib + isub) + e
                        ins = v.tensor_scalar(
                            wv_sb[ib][:, off:off + 16], pv[ib][:, off:off + 16],
                            wf[:, col:col + 1], 2.0, op0=ALU.mult, op1=ALU.mult)
                        if isub == 3 and e == 7:
                            s_wv[ib].inc(ins)
            for k in range(8):
                ib, isub = k // 4, k % 4
                v.wait_ge(s_wvt.h, k + 1)
                s_wve.inc(v.tensor_copy(
                    wvT[ib][0:16, 1024 * isub:1024 * (isub + 1)], pbc[0:16, :]))

        @block.tensor
        def _(t):
            for k in range(8):
                ib, isub = k // 4, k % 4
                t.wait_ge(s_wv[ib].h, 1)
                if k >= 1:
                    t.wait_ge(s_wve.h, k)
                for e in range(8):
                    off = 16 * (8 * isub + e)
                    ins = t.transpose(pbc[0:16, 128 * e:128 * (e + 1)],
                                      wv_sb[ib][:, off:off + 16], me.idb[:, :])
                    if e == 7:
                        s_wvt.inc(ins)
            t.wait_ge(s_b2.h, 16)
            t.wait_ge(s_wve.h, 8)
            n2 = 0
            for grp in range(2):
                for oc in range(4):
                    noc = 4 * grp + oc
                    if noc >= 1:
                        t.wait_ge(s_ev6.h, 4 * noc)
                    for isub in range(4):
                        for e in range(8):
                            t.matmul(po[isub][:, :],
                                     wvT[grp][:, 1024 * isub + 128 * e:1024 * isub + 128 * (e + 1)],
                                     b2full[:, 2048 * e + 512 * oc:2048 * e + 512 * (oc + 1)],
                                     start=(e == 0), stop=False)
                    for (f0, f1) in blocks6:
                        t.wait_ge(s_w2[n2 % 2].h, 16 * (n2 // 2 + 1))
                        t.wait_ge(s_m2[n2 % 2].h, 16 * (n2 // 2 + 1))
                        for ft in range(f0, f1):
                            for isub in range(4):
                                ins = t.matmul(
                                    po[isub][:, :],
                                    mb[n2 % 2][:, 512 * (ft - f0) + 128 * isub:
                                               512 * (ft - f0) + 128 * (isub + 1)],
                                    w2b[n2 % 2][:, 512 * (ft - f0):512 * (ft - f0 + 1)],
                                    start=False, stop=(ft == NFT - 1))
                                if isub == 3 and ft == f1 - 1:
                                    if ft == NFT - 1:
                                        s_o2.inc(ins)
                                    else:
                                        s_w2c.inc(ins)
                        n2 += 1

        @block.gpsimd
        def _(g):
            g.wait_ge(me.s_mst.h, 16 * 2 * NFT)
            g.wait_ge(s_zz.h, 1)
            s_b2.inc(g.dma_start(out=b2full[0:16, :], in_=me.b2cat[:, :]), 16)
            n2 = 0
            nb = len(blocks6)
            for grp in range(2):
                for oc in range(4):
                    for bi, (f0, f1) in enumerate(blocks6):
                        if n2 >= 2:
                            prev = n2 - 2
                            bi_prev = prev % nb
                            if bi_prev == nb - 1:
                                g.wait_ge(s_o2.h, prev // nb + 1)
                            else:
                                g.wait_ge(s_w2c.h, prev + 1 - (prev // nb))
                        src = me.w2T[128 * f0:128 * f1,
                                     512 * oc:512 * (oc + 1)].rearrange(
                                         "(c p) q -> p c q", p=128)
                        dst = w2b[n2 % 2][:, 0:512 * (f1 - f0)].rearrange(
                            "p (c q) -> p c q", c=f1 - f0)
                        s_w2[n2 % 2].inc(g.dma_start(out=dst, in_=src), 16)
                        n2 += 1


        @block.scalar
        def _(s):
            for grp in range(2):
                if grp >= 1:
                    s.wait_ge(s_x2.h, 4 * grp)
                for oc in range(4):
                    noc = 4 * grp + oc
                    s.wait_ge(s_o2.h, 4 * grp + oc + 1)
                    for isub in range(4):
                        s_ev6.inc(s.copy(y2v[isub][:, 512 * oc:512 * (oc + 1)],
                                         po[isub][:, :]))
                for isub in range(4):
                    i128 = 4 * grp + isub
                    s.wait_ge(s_st6c.h, st6c_0 + 4 * grp + isub + 1)
                    s_rs6c.inc(s.activation(
                        me.rstd[isub % 2][:, :], me.mv[isub % 2][:, 1:2],
                        AF.Sqrt, bias=me.epsv[:, 0:1], scale=1.0))
                    s.wait_ge(s_nb6c.h, nb6c_0 + 4 * grp + isub + 1)
                    s_y6.inc(s.activation(
                        y2v[isub][:, :], y2v[isub][:, :], AF.Identity,
                        bias=me.nbias[isub % 2][:, 0:1],
                        scale=me.rstd[isub % 2][:, 0:1]))

        @block.vector
        def _(v):
            for grp in range(2):
                v.wait_ge(s_ev6.h, 16 * (grp + 1))
                for isub in range(4):
                    i128 = 4 * grp + isub
                    for q in range(4):
                        v.bn_stats(me.stats[:, 6 * q:6 * q + 6],
                                   y2v[isub][:, 512 * q:512 * (q + 1)])
                    v.drain()
                    s_st6c.inc(v.bn_aggr(me.mv[isub % 2][:, :], me.stats[:, :]))
                    v.wait_ge(s_rs6c.h, rs6c_0 + 4 * grp + isub + 1)
                    v.reciprocal(me.rstd[isub % 2][:, :], me.rstd[isub % 2][:, :])
                    v.drain()
                    s_nb6c.inc(v.tensor_scalar(
                        me.nbias[isub % 2][:, :], me.rstd[isub % 2][:, :],
                        me.mv[isub % 2][:, 0:1], -1.0, op0=ALU.mult, op1=ALU.mult))
                    v.wait_ge(s_y6.h, 4 * grp + isub + 1)
                    v.wait_ge(s_xc[i128 % 2].h, 16 * (i128 // 2 + 1))
                    if i128 >= 1:
                        v.wait_ge(s_outs.h, 16 * i128)
                    s_x2.inc(v.tensor_tensor(x2t[i128 % 2][:, :], y2v[isub][:, :],
                                             x1b[i128 % 2][:, :], op=ALU.add))

        @block.sync
        def _(sy):
            sy.wait_ge(me.s_mst.h, 16 * 2 * NFT)
            n2 = 0
            nb = len(blocks6)
            for grp in range(2):
                for oc in range(4):
                    for bi, (f0, f1) in enumerate(blocks6):
                        if n2 >= 2:
                            prev = n2 - 2
                            bi_prev = prev % nb
                            if bi_prev == nb - 1:
                                sy.wait_ge(s_o2.h, prev // nb + 1)
                            else:
                                sy.wait_ge(s_w2c.h, prev + 1 - (prev // nb))
                        dstm = mb[n2 % 2][:, 0:512 * (f1 - f0)].rearrange(
                            "p (c q) -> p c q", c=f1 - f0)
                        srcm = me.maccd[grp][f0:f1].rearrange("o p q -> p o q")
                        s_m2[n2 % 2].inc(sy.dma_start(out=dstm, in_=srcm), 16)
                        n2 += 1
                for isub in range(4):
                    i128 = 4 * grp + isub
                    if i128 >= 2:
                        sy.wait_ge(s_x2.h, i128 - 1)
                    s_xc[i128 % 2].inc(sy.dma_start(
                        out=x1b[i128 % 2][:, :],
                        in_=me.x1dram[128 * i128:128 * (i128 + 1), :]), 16)
                    sy.wait_ge(s_x2.h, i128 + 1)
                    s_outs.inc(sy.dma_start(
                        out=me.out[128 * i128:128 * (i128 + 1), :],
                        in_=x2t[i128 % 2][:, :]), 16)
            sy.wait_ge(s_outs.h, 16 * 8)

    def build(self, upto="p3"):
        self.declare_dram()
        self.es = contextlib.ExitStack()
        nc = self.nc
        self.sb = lambda n, s, d: self.es.enter_context(nc.sbuf_tensor(n, s, d))
        self.sem = lambda n: Sem(self.es, nc, n)
        with self.es:
            self.phase0_alloc()
            self.esA = self.es.enter_context(contextlib.ExitStack())
            self.sbA = lambda n, s, d: self.esA.enter_context(nc.sbuf_tensor(n, s, d))
            self.psA = lambda n, s, d: self.esA.enter_context(nc.psum_tensor(n, s, d))
            self.block = self.es.enter_context(nc.Block())
            self.phase0()
            self.phase1()
            if upto == "p1":
                self.finish_dev_p1()
                return nc
            self.phase2()
            self.es1.close()
            self.phase3()
            if upto == "p3":
                self.finish_dev_p3()
                return nc
            self.phase4()
            self.esA.close()
            self.phase5()
            if upto == "p5":
                self.finish_dev_p5()
                return nc
            self.es5.close()
            self.phase6()
            self.esa.close()
            self.phase6b()
            self.phase6c()
            self.finish_dbg_all()
        return nc

    def finish_dbg_all(self):
        me, block = self, self.block
        dx1 = self.dbgout("x1", [TOK, D])
        dwf = self.dbgout("wf", [128, 64])
        dm0 = dm1 = None
        if not (dx1 or dwf):
            return
        sh = self.p6_shared

        @block.sync
        def _(sy):
            sy.wait_ge(me.s_outs.h, 16 * 8)
            n = 0
            if dx1 is not None:
                me.sem_x.inc(sy.dma_start(out=dx1[:, :], in_=me.x1dram[:, :]), 16)
            if dwf is not None:
                me.sem_x.inc(sy.dma_start(out=dwf[:, :], in_=sh["wf"][:, :]), 16)

            sy.wait_ge(me.sem_x.h, me.sem_x.v)

    def finish_dev_p5(self):
        me, block = self, self.block
        dx1 = self.dbgout("x1", [TOK, D])

        @block.sync
        def _(sy):
            sy.wait_ge(me.s_x1s.h, 16 * 8)
            if dx1 is not None:
                me.sem_x.inc(sy.dma_start(out=dx1[:, :], in_=me.x1dram[:, :]), 16)
            sy.wait_ge(me.sem_x.h, me.sem_x.v)

    def finish_dev_p1(self):
        me, block = self, self.block
        dq = self.dbgout("qT", [4 * 128, S], BF16)
        dk = self.dbgout("kT", [4 * 128, S], BF16)
        dv = self.dbgout("Vt", [4 * 128, 32 * 129], BF16)

        @block.sync
        def _(sy):
            sy.wait_ge(me.s_ev.h, me.p1_ev_total)
            sy.wait_ge(me.s_tpev.h, me.p1_tpev_total)
            for h in range(4):
                if dq is not None:
                    me.sem_x.inc(sy.dma_start(out=dq[128 * h:128 * (h + 1), :],
                                              in_=me.qT[h][:, :]), 16)
                if dk is not None:
                    me.sem_x.inc(sy.dma_start(out=dk[128 * h:128 * (h + 1), :],
                                              in_=me.kT[h][:, :]), 16)
                if dv is not None:
                    me.sem_x.inc(sy.dma_start(out=dv[128 * h:128 * (h + 1), :],
                                              in_=me.Vt[h][:, :]), 16)

            sy.wait_ge(me.sem_x.h, me.sem_x.v)

    def finish_dev_p3(self):
        me, block = self, self.block
        da = self.dbgout("aoT", [4 * 128, S], BF16)
        db = self.dbgout("bnd", [1, 20])

        @block.sync
        def _(sy):
            sy.wait_ge(me.s_aot.h, 32)
            for h in range(4):
                if da is not None:
                    me.sem_x.inc(sy.dma_start(out=da[128 * h:128 * (h + 1), :],
                                              in_=me.aoT[h][:, :]), 16)
            if db is not None:
                me.sem_x.inc(sy.dma_start(out=db[:, :], in_=me.bnd[:, 0:20]), 16)
            sy.wait_ge(me.sem_x.h, me.sem_x.v)


def build_program(dbg_names=(), upto="p1"):
    b = Builder(dbg_names)
    nc = b.build(upto=upto)
    return nc, b.dbg


def host_inputs(core_id, I):
    c = core_id
    b, hq = c // 4, c % 4
    heads = [4 * hq + h for h in range(4)]
    C = np.ascontiguousarray
    Wqkv = np.asarray(I["L0_attn_Wqkv"], dtype=np.float32)
    bqkv = np.asarray(I["L0_attn_bqkv"], dtype=np.float32)
    wq = np.concatenate([Wqkv[:, 128 * h:128 * (h + 1)] for h in heads], axis=1)
    wk = np.concatenate([Wqkv[:, D + 128 * h:D + 128 * (h + 1)] for h in heads], axis=1)
    wv = np.concatenate([Wqkv[:, 2 * D + 128 * h:2 * D + 128 * (h + 1)] for h in heads], axis=1)
    wqkv = C(np.concatenate([wq, wk, wv], axis=1))
    bq = np.concatenate([bqkv[128 * h:128 * (h + 1)] for h in heads])
    bk = np.concatenate([bqkv[D + 128 * h:D + 128 * (h + 1)] for h in heads])
    bv = np.concatenate([bqkv[2 * D + 128 * h:2 * D + 128 * (h + 1)] for h in heads])
    bqkv_col = C(np.concatenate([bq, bk, bv]).reshape(12, 128).T)

    a1 = np.asarray(I["L0_ffn_a1"], np.float32)
    a3 = np.asarray(I["L0_ffn_a3"], np.float32)
    a2 = np.asarray(I["L0_ffn_a2"], np.float32)
    b1 = np.asarray(I["L0_ffn_b1"], np.float32)
    b3 = np.asarray(I["L0_ffn_b3"], np.float32)
    b2 = np.asarray(I["L0_ffn_b2"], np.float32)

    def aucat(a):
        z = np.zeros((D, 16), np.float32)
        cols = []
        for t in range(8):
            cols.append(a[t].T)
            cols.append(z)
        return C(np.concatenate(cols, axis=1))

    def bpk(bm):
        bT = bm.transpose(0, 2, 1)
        r = bT.reshape(8, 16, NFT, 128).transpose(2, 1, 0, 3).reshape(NFT, 16, 8 * 128)
        return C(r)

    a2c = C(np.concatenate([a2[e].T for e in range(E)], axis=1))
    b2c = C(b2.transpose(2, 0, 1).reshape(16, E * D))

    identf = np.eye(128, dtype=np.float32)
    onehot = np.zeros((128, 8 * 128), np.float32)
    for e in range(E):
        onehot[e, 128 * e:128 * (e + 1)] = 1.0
    e1row = np.zeros((128, 128), np.float32); e1row[0, :] = 1.0
    onecol = np.zeros((128, 128), np.float32); onecol[:, 0] = 1.0
    slneg = np.zeros((1, 4 * S), np.float32)
    for h in range(4):
        slneg[0, S * h:S * (h + 1)] = -SLOPES[heads[h]] * np.arange(S, dtype=np.float32)
    sjh = np.zeros((128, 4 * 32), np.float32)
    for h in range(4):
        sl = SLOPES[heads[h]]
        for jt in range(32):
            sjh[:, 32 * h + jt] = sl * (128 * jt + np.arange(128))
    mask = np.zeros((128, 4 * 512), np.float32)
    for k in range(4):
        jj = np.arange(128)[:, None]
        ii = np.arange(512)[None, :]
        mask[:, 512 * k:512 * (k + 1)] = np.where(jj + 128 * k <= ii, 0.0, NEGBIG)
    svec = np.full((32, 1), LORA, np.float32)
    gidx = np.zeros((128, NCH), np.uint32)
    for ch in range(NCH):
        row = 512 * (ch % 4) + 128 * (ch // 4) + np.arange(128)
        gidx[:, ch] = 4 * row + hq

    x = np.asarray(I["L0_attn_x"], np.float32)
    return {
        "xb": C(x[b]),
        "xsl": C(x[b][1024 * hq:1024 * (hq + 1)]),
        "wqkv": wqkv, "bqkv_col": bqkv_col,
        "wo": C(np.asarray(I["L0_attn_Wo"], np.float32)),
        "bo_row": C(np.asarray(I["L0_attn_bo"], np.float32).reshape(1, D)),
        "gwT": C(np.asarray(I["L0_ffn_gate_w"], np.float32).T),
        "w1T": C(np.asarray(I["L0_ffn_w1"], np.float32).T),
        "w3T": C(np.asarray(I["L0_ffn_w3"], np.float32).T),
        "w2T": C(np.asarray(I["L0_ffn_w2"], np.float32).T),
        "au1": aucat(a1), "au3": aucat(a3),
        "b1pk": bpk(b1), "b3pk": bpk(b3),
        "a2cat": a2c, "b2cat": b2c,
        "identf": identf, "onehot": onehot,
        "e1row_d": e1row, "onecol_d": onecol,
        "slneg_d": slneg, "sjh_d": sjh, "mask_d": mask, "svec_d": svec,
        "gidx_d": gidx,
    }


def kernel(**inputs):
    nc, _ = build_program(upto="all")
    in_maps = [host_inputs(c, inputs) for c in range(8)]
    res = run_bass_kernel_spmd(nc, in_maps, core_ids=list(range(8)))
    y = np.zeros((B, S, D), np.float32)
    for c in range(8):
        b, hq = c // 4, c % 4
        y[b, 1024 * hq:1024 * (hq + 1), :] = res.results[c]["out"]
    return y


# version 09
# speedup vs baseline: 1.1727x; 1.1727x over previous
import math
import contextlib
import numpy as np
import ml_dtypes
import concourse.bass as bass
import concourse.mybir as mybir
from concourse.bass import IndirectOffsetOnAxis
from concourse.bass_utils import run_bass_kernel_spmd

F32 = mybir.dt.float32
BF16 = mybir.dt.bfloat16
U32 = mybir.dt.uint32
AF = mybir.ActivationFunctionType
ALU = mybir.AluOpType
AX = mybir.AxisListType
bf16 = ml_dtypes.bfloat16

B, S, D = 2, 4096, 2048
H, HD = 16, 128
FF, E, R = 5632, 8, 16
TOK, IBS, NIB = 1024, 512, 2
NFT = FF // 128
NCH = D // 128
NPT = S // 128
SCALE = 1.0 / math.sqrt(HD)
LORA = 2.0
SLOPES = [2.0 ** (-k / 2.0) for k in range(H)]
NEGBIG = -1.0e30
SIM_SAFE = [False]


class Sem:
    def __init__(self, es, nc, name):
        self.h = es.enter_context(nc.semaphore(name))
        self.v = 0

    def inc(self, instr, n=1):
        instr.then_inc(self.h, n)
        self.v += n
        return instr


class Builder:
    def __init__(self, dbg_names=()):
        self.nc = bass.Bass("TRN2", target_bir_lowering=False, dynamic_dma_scratch_size=16384)
        self.dbg_names = set(dbg_names)
        self.dbg = {}

    def ap(self, *a, **k):
        return self.nc.dram_tensor(*a, **k).ap()

    def dbgout(self, name, shape, dt=F32):
        if name in self.dbg_names:
            self.dbg[name] = self.ap("dbg_" + name, shape, dt, kind="ExternalOutput")
        return self.dbg.get(name)

    def declare_dram(self):
        ap = self.ap
        self.xb = ap("xb", [S, D], F32, kind="ExternalInput")
        self.xsl = ap("xsl", [TOK, D], F32, kind="ExternalInput")
        self.wqkv = ap("wqkv", [D, 1536], F32, kind="ExternalInput")
        self.bqkv_col = ap("bqkv_col", [128, 12], F32, kind="ExternalInput")
        self.wo = ap("wo", [D, D], F32, kind="ExternalInput")
        self.bo_row = ap("bo_row", [1, D], F32, kind="ExternalInput")
        self.gwT = ap("gwT", [D, E], F32, kind="ExternalInput")
        self.w1T = ap("w1T", [D, FF], F32, kind="ExternalInput")
        self.w3T = ap("w3T", [D, FF], F32, kind="ExternalInput")
        self.w2T = ap("w2T", [FF, D], F32, kind="ExternalInput")
        self.au1 = ap("au1", [D, 256], F32, kind="ExternalInput")
        self.au3 = ap("au3", [D, 256], F32, kind="ExternalInput")
        self.b1pk = ap("b1pk", [NFT, 16, 8 * 128], F32, kind="ExternalInput")
        self.b3pk = ap("b3pk", [NFT, 16, 8 * 128], F32, kind="ExternalInput")
        self.a2cat = ap("a2cat", [FF, 128], F32, kind="ExternalInput")
        self.b2cat = ap("b2cat", [16, E * D], F32, kind="ExternalInput")
        self.identf = ap("identf", [128, 128], F32, kind="ExternalInput")
        self.onehot = ap("onehot", [128, 8 * 128], F32, kind="ExternalInput")
        self.e1row_d = ap("e1row_d", [128, 128], F32, kind="ExternalInput")
        self.onecol_d = ap("onecol_d", [128, 128], F32, kind="ExternalInput")
        self.slneg_d = ap("slneg_d", [1, 4 * S], F32, kind="ExternalInput")
        self.sjh_d = ap("sjh_d", [128, 4 * 32], F32, kind="ExternalInput")
        self.mask_d = ap("mask_d", [128, 4 * 512], F32, kind="ExternalInput")
        self.svec_d = ap("svec_d", [32, 1], F32, kind="ExternalInput")
        self.gidx_d = ap("gidx_d", [128, NCH], U32, kind="ExternalInput")
        self.out = ap("out", [TOK, D], F32, kind="ExternalOutput")
        self.x1dram = ap("x1dram", [TOK, D], F32, kind="Internal")
        self.maccd = [ap(f"maccd{ib}", [NFT, 128, 512], BF16, kind="Internal")
                      for ib in range(2)]
        self.ccin = self.nc.dram_tensor("ccin", [512, S], BF16, kind="Internal")
        self.ccout = self.nc.dram_tensor("ccout", [D, S], BF16, kind="Internal")

    def phase0_alloc(self):
        sb, sem = self.sb, self.sem
        self.idb = sb("idb", [128, 128], BF16)
        self.idf = sb("idf", [128, 128], F32)
        self.e1b = sb("e1b", [128, 128], BF16)
        self.oc1b = sb("oc1b", [128, 128], BF16)
        self.svec = sb("svec", [32, 1], F32)
        self.bqcol = sb("bqcol", [128, 12], F32)
        self.bqcs = sb("bqcs", [128, 4], F32)
        self.epsv = sb("epsv", [128, 1], F32)
        self.onehots = sb("onehots", [128, 8 * 128], BF16)
        self.stats = sb("stats", [128, 24], F32)
        self.gidx_s = sb("gidx_s", [128, NCH], U32)
        self.mv = [sb(f"mv{i}", [128, 2], F32) for i in range(2)]
        self.rstd = [sb(f"rstd{i}", [128, 1], F32) for i in range(2)]
        self.nbias = [sb(f"nbias{i}", [128, 1], F32) for i in range(2)]

        self.sem_x = sem("sem_x")
        self.sem_w = sem("sem_w")
        self.s_c0 = sem("s_c0")
        self.s_c1 = sem("s_c1")

    def phase0(self):
        block = self.block
        me = self

        @block.gpsimd
        def _(g):
            me.sem_w.inc(g.dma_start(out=me.idb[:, :], in_=me.identf[:, :]), 16)
            me.sem_w.inc(g.dma_start(out=me.e1b[:, :], in_=me.e1row_d[:, :]), 16)
            me.sem_w.inc(g.dma_start(out=me.oc1b[:, :], in_=me.onecol_d[:, :]), 16)
            me.sem_w.inc(g.dma_start(out=me.onehots[:, :], in_=me.onehot[:, :]), 16)

        @block.sync
        def _(sy):
            me.s_c0.inc(sy.dma_start(out=me.idf[:, :], in_=me.identf[:, :]), 16)
            me.s_c0.inc(sy.dma_start(out=me.svec[:, :], in_=me.svec_d[:, :]), 16)
            me.s_c0.inc(sy.dma_start(out=me.bqcol[:, :], in_=me.bqkv_col[:, :]), 16)
            me.s_c0.inc(sy.dma_start(out=me.gidx_s[:, :], in_=me.gidx_d[:, :]), 16)

        @block.vector
        def _(v):
            v.memset(me.epsv[:, :], 1e-5)
            v.wait_ge(me.s_c0.h, 64)
            me.s_c1.inc(v.tensor_scalar(me.bqcs[:, :], me.bqcol[:, 0:4], SCALE,
                                        None, op0=ALU.mult))

    def phase1(self):
        nc, block, me = self.nc, self.block, self
        sbA = self.sbA

        self.qT = [sbA(f"qT{h}", [128, S], BF16) for h in range(4)]
        self.kT = [sbA(f"kT{h}", [128, S], BF16) for h in range(4)]
        self.Vt = [sbA(f"Vt{h}", [128, 32 * 129], BF16) for h in range(4)]
        self.cmx8 = sbA("cmx8", [1, 8], F32)
        self.bnd = sbA("bnd", [1, 24], F32)
        es1 = self.esA.enter_context(contextlib.ExitStack())
        sb1 = lambda n, s, d: es1.enter_context(nc.sbuf_tensor(n, s, d))
        ps1 = lambda n, s, d: es1.enter_context(nc.psum_tensor(n, s, d))
        self.es1 = es1
        xbuf = [sb1(f"xbuf{i}", [128, D], F32) for i in range(2)]
        lnx = [sb1(f"lnx{i}", [128, D], BF16) for i in range(2)]
        lnxT = [sb1(f"lnxT{i}", [128, NCH * 512], BF16) for i in range(2)]
        wqkv_sb = sb1("wqkv_sb", [128, NCH * 1536], BF16)
        vtmp = [sb1(f"vtmp{i}", [128, 512], BF16) for i in range(2)]
        self.sqbuf = lnx[0]
        pf = [self.psA(f"pf{i}", [128, 512], F32) for i in range(2)]
        pbb = [self.psA(f"pbb{i}", [128, 1024], BF16) for i in range(2)]
        self.p1_pf, self.p1_pbb = pf, pbb

        sem = self.sem
        s_stat = sem("s_stat"); s_rstd = sem("s_rstd"); s_nb = sem("s_nb")
        self.s_stat, self.s_rstd, self.s_nb = s_stat, s_rstd, s_nb
        s_lnx = sem("s_lnx")
        s_tp = sem("s_tp"); s_tpev = sem("s_tpev")
        s_qk = sem("s_qk"); s_ev = sem("s_ev")
        self.s_qk, self.s_ev, self.s_tp, self.s_tpev = s_qk, s_ev, s_tp, s_tpev

        @block.gpsimd
        def _(g):
            src = me.wqkv[:, :].rearrange("(c p) q -> p c q", p=128)
            dst = wqkv_sb[:, :].rearrange("p (c q) -> p c q", c=NCH)
            me.sem_w.inc(g.dma_start(out=dst, in_=src), 16)
        w_qkv = self.sem_w.v

        @block.vector
        def _(v):
            for h in range(4):
                vv = me.Vt[h][:, :].rearrange("p (t c) -> p t c", c=129)
                v.memset(vv[:, :, 128:129], 1.0)

        x_base = self.sem_x.v

        @block.sync
        def _(sy):
            for pt in range(NPT):
                if pt >= 2:
                    sy.wait_ge(s_lnx.h, pt - 1)
                me.sem_x.inc(sy.dma_start(
                    out=xbuf[pt % 2][:, :],
                    in_=me.xb[128 * pt:128 * (pt + 1), :]), 16)

        lnxT3 = [lnxT[i][:, :].rearrange("p (c t) -> p c t", c=NCH) for i in range(2)]
        wq3 = wqkv_sb[:, :].rearrange("p (c q) -> p c q", c=NCH)
        pvt = [ps1(f"pvt{i}", [128, 1024], BF16) for i in range(2)]
        s_vtev = sem("s_vtev")
        self.s_vtev = s_vtev
        tp_pos = {}
        tp_n = [0]

        @block.tensor
        def _(t):
            def tile_tp(pt):
                t.wait_ge(s_lnx.h, pt + 1)
                for half in range(2):
                    ep = 2 * pt + half
                    if ep >= 2:
                        t.wait_ge(s_tpev.h, ep - 1)
                    bank = pbb[ep % 2]
                    for c8 in range(8):
                        ch = 8 * half + c8
                        ins = t.transpose(bank[:, 128 * c8:128 * (c8 + 1)],
                                          lnx[pt % 2][:, 128 * ch:128 * (ch + 1)],
                                          me.idb[:, :])
                        if c8 == 7:
                            s_tp.inc(ins)
                            tp_n[0] += 1
                            tp_pos[('t', pt, half)] = tp_n[0]

            def vtp(a, h):
                m = 4 * a + h
                t.wait_ge(s_ev.h, 12 * a + 3 * h + 3)
                if m >= 2:
                    t.wait_ge(s_vtev.h, m - 1)
                bank = pvt[m % 2]
                for ts4 in range(4):
                    ins = t.transpose(bank[:, 128 * ts4:128 * (ts4 + 1)],
                                      vtmp[h % 2][:, 128 * ts4:128 * (ts4 + 1)],
                                      me.idb[:, :])
                    if ts4 == 3:
                        s_tp.inc(ins)
                        tp_n[0] += 1
                        tp_pos[('v', m)] = tp_n[0]

            t.wait_ge(me.sem_w.h, w_qkv)
            for pt in range(4):
                tile_tp(pt)
            for a in range(8):
                t.wait_ge(s_tpev.h, 8 * (a + 1))
                for g in range(12):
                    h, P = g // 3, g % 3
                    grp = 12 * a + g
                    if grp >= 2:
                        t.wait_ge(s_ev.h, grp - 1)
                    bank = pf[grp % 2]
                    col0 = 512 * P + 128 * h
                    for ch in range(NCH):
                        ins = t.matmul(bank[:, :], wq3[:, ch, col0:col0 + 128],
                                       lnxT3[a % 2][:, ch, :],
                                       start=(ch == 0), stop=(ch == NCH - 1))
                        if ch == NCH - 1:
                            s_qk.inc(ins)
                    if g in (2, 5, 8, 11):
                        vtp(a, g // 3)
                    if a <= 6 and g in (3, 5, 7, 10):
                        k = {3: 0, 5: 1, 7: 2, 10: 3}[g]
                        tile_tp(4 * (a + 1) + k)

        @block.scalar
        def _(s):
            def ln_evac(pt):
                a, j = pt // 4, pt % 4
                for half in range(2):
                    ep = 2 * pt + half
                    s.wait_ge(s_tp.h, tp_pos[('t', pt, half)])
                    if half == 0 and a >= 2:
                        s.wait_ge(s_qk.h, 12 * (a - 1))
                    bank = pbb[ep % 2]
                    dst = lnxT3[a % 2][:, 8 * half:8 * half + 8, 128 * j:128 * (j + 1)]
                    srcb = bank[:, 0:1024].rearrange("p (c t) -> p c t", c=8)
                    s_tpev.inc(s.activation(dst, srcb, AF.Identity,
                                            bias=0.0, scale=1.0))

            for pt in range(NPT):
                if pt >= 1:
                    ln_evac(pt - 1)
                s.wait_ge(s_stat.h, pt + 1)
                s_rstd.inc(s.activation(
                    me.rstd[pt % 2][:, :], me.mv[pt % 2][:, 1:2],
                    AF.Sqrt, bias=me.epsv[:, 0:1], scale=1.0))
                s.wait_ge(s_nb.h, pt + 1)
                if pt >= 2:
                    s.wait_ge(s_tp.h, tp_pos[('t', pt - 2, 1)])
                s_lnx.inc(s.activation(
                    lnx[pt % 2][:, :], xbuf[pt % 2][:, :], AF.Identity,
                    bias=me.nbias[pt % 2][:, 0:1], scale=me.rstd[pt % 2][:, 0:1]))
            ln_evac(NPT - 1)

        @block.vector
        def _(v):
            def qkv_evacs(b, hj):
                m = 4 * b + hj
                for P in range(3):
                    grp = 12 * b + 3 * hj + P
                    v.wait_ge(s_qk.h, grp + 1)
                    bank = pf[grp % 2]
                    if P == 0:
                        ins = v.tensor_scalar(
                            me.qT[hj][:, 512 * b:512 * (b + 1)], bank[:, :],
                            me.bqcol[:, hj:hj + 1], SCALE,
                            op0=ALU.add, op1=ALU.mult)
                    elif P == 1:
                        ins = v.tensor_scalar(
                            me.kT[hj][:, 512 * b:512 * (b + 1)], bank[:, :],
                            me.bqcol[:, 4 + hj:5 + hj], None, op0=ALU.add)
                    else:
                        if m >= 2:
                            v.wait_ge(s_tp.h, tp_pos[('v', m - 2)])
                        ins = v.tensor_scalar(
                            vtmp[hj % 2][:, :], bank[:, :],
                            me.bqcol[:, 8 + hj:9 + hj], None, op0=ALU.add)
                    s_ev.inc(ins)
                v.wait_ge(s_tp.h, tp_pos[('v', m)])
                vt3 = me.Vt[hj][:, :].rearrange("p (t c) -> p t c", c=129)
                dst = vt3[:, 4 * b:4 * b + 4, 0:128]
                srcb = pvt[m % 2][:, 0:512].rearrange("p (t c) -> p t c", c=128)
                s_vtev.inc(v.tensor_copy(dst, srcb))

            for pt in range(NPT):
                a, j = pt // 4, pt % 4
                v.wait_ge(me.sem_x.h, x_base + 16 * min(pt + 2, NPT))
                if pt >= 2:
                    v.wait_ge(s_rstd.h, pt - 1)
                xt = xbuf[pt % 2]
                for q in range(4):
                    v.bn_stats(me.stats[:, 6 * q:6 * q + 6],
                               xt[:, 512 * q:512 * (q + 1)])
                v.drain()
                s_stat.inc(v.bn_aggr(me.mv[pt % 2][:, :], me.stats[:, :]))
                v.wait_ge(s_rstd.h, pt + 1)
                v.reciprocal(me.rstd[pt % 2][:, :], me.rstd[pt % 2][:, :])
                v.drain()
                s_nb.inc(v.tensor_scalar(
                    me.nbias[pt % 2][:, :], me.rstd[pt % 2][:, :],
                    me.mv[pt % 2][:, 0:1], -1.0, op0=ALU.mult, op1=ALU.mult))
                if pt >= 4:
                    qkv_evacs(a - 1, j)
            for j in range(4):
                qkv_evacs(7, j)

        self.p1_qk_total = s_qk.v
        self.p1_ev_total = s_ev.v
        self.p1_tp_total = s_tp.v
        self.p1_tpev_total = s_tpev.v
        self.p1_vtev_total = s_vtev.v
        self.s_lnx = s_lnx

    def phase2(self):
        block, me = self.block, self
        sem = self.sem
        s_sq = sem("s_sq")
        s_bsum = sem("s_bsum")
        s_red = sem("s_red")
        s_bnd = sem("s_bnd")
        s_sqr = sem("s_sqr")
        s_negb = sem("s_negb")
        self.s_negb = s_negb
        pf = self.p1_pf

        @block.vector
        def _(v):
            v.wait_ge(me.s_ev.h, me.p1_ev_total)
            for strip in range(8):
                h, which = strip // 2, strip % 2
                srcq = me.qT[h] if which == 0 else me.kT[h]
                for hh in range(2):
                    if strip + hh > 0:
                        v.wait_ge(s_bsum.h, 8 * strip + 4 * hh)
                    s_sq.inc(v.tensor_tensor(
                        me.sqbuf[:, :], srcq[:, 2048 * hh:2048 * (hh + 1)],
                        srcq[:, 2048 * hh:2048 * (hh + 1)], op=ALU.mult))
                    for c in range(4 * hh, 4 * hh + 4):
                        v.wait_ge(s_bsum.h, 8 * strip + c + 1)
                        s_red.inc(v.reduce_max(me.cmx8[:, c:c + 1],
                                               pf[c % 2][0:1, :], axis=AX.X))
                v.drain()
                s_bnd.inc(v.reduce_max(me.bnd[:, strip:strip + 1],
                                       me.cmx8[:, :], axis=AX.X))
                v.drain()

        @block.tensor
        def _(t):
            for strip in range(8):
                for c in range(8):
                    hh = c // 4
                    t.wait_ge(s_sq.h, 2 * strip + hh + 1)
                    n = 8 * strip + c
                    if n >= 2:
                        t.wait_ge(s_red.h, n - 1)
                    s_bsum.inc(t.matmul(pf[c % 2][:, :], me.oc1b[:, :],
                                        me.sqbuf[:, 512 * (c % 4):512 * (c % 4 + 1)],
                                        start=True, stop=True))

        @block.scalar
        def _(s):
            for h in range(4):
                s.wait_ge(s_bnd.h, 2 * h + 2)
                s_sqr.inc(s.activation(me.bnd[:, 8 + 2 * h:9 + 2 * h],
                                       me.bnd[:, 2 * h:2 * h + 1],
                                       AF.Sqrt, bias=0.0, scale=1.0))
                s_sqr.inc(s.activation(me.bnd[:, 9 + 2 * h:10 + 2 * h],
                                       me.bnd[:, 2 * h + 1:2 * h + 2],
                                       AF.Sqrt, bias=0.0, scale=1.0))

        @block.vector
        def _(v):
            for h in range(4):
                v.wait_ge(s_sqr.h, 2 * h + 2)
                s_negb.inc(v.tensor_scalar(
                    me.bnd[:, 16 + h:17 + h], me.bnd[:, 8 + 2 * h:9 + 2 * h],
                    me.bnd[:, 9 + 2 * h:10 + 2 * h], -1.0,
                    op0=ALU.mult, op1=ALU.mult))

    def phase3(self):
        nc, block, me = self.nc, self.block, self
        sbA = self.sbA
        es3 = self.esA.enter_context(contextlib.ExitStack())
        ps3 = lambda n, s, d: es3.enter_context(nc.psum_tensor(n, s, d))
        self.slbuf = [sbA(f"slbuf{i}", [1, 512], F32) for i in range(2)]
        self.sjs = sbA("sjs", [128, 4 * 32], F32)
        self.masks = sbA("masks", [128, 4 * 512], BF16)
        self.cibuf = [sbA(f"cibuf{i}", [128, 512], BF16) for i in range(2)]
        self.strip = sbA("strip", [128, 32 * 512], BF16)
        self.aoT = [sbA(f"aoT{h}", [128, S], BF16) for h in range(4)]
        ao_sb = sbA("ao_sb", [128, 512], BF16)
        rcp = sbA("rcp", [128, 4], F32)
        pav = [ps3(f"pav{i}", [128, 512], F32) for i in range(4)]
        pf = self.p1_pf
        pbb = self.p1_pbb

        sem = self.sem
        s_sc = sem("s_sc"); s_exp = sem("s_exp"); s_cib = sem("s_cib")
        s_av = sem("s_av"); s_norm = sem("s_norm")
        s_aotp = sem("s_aotp"); s_aot = sem("s_aot")
        self.s_aot = s_aot

        blocks = [(h, ib) for h in range(4) for ib in range(8)]
        jmax = [4 * ib + 4 for (h, ib) in blocks]
        cum = [0]
        for n in range(len(blocks)):
            cum.append(cum[-1] + jmax[n])
        sc0, exp0 = s_sc.v, s_exp.v
        s_sl = [sem("s_sl0"), sem("s_sl1")]

        qk_all, negb_all = me.s_qk.v, me.s_negb.v

        @block.gpsimd
        def _(g):
            g.wait_ge(me.s_qk.h, qk_all)
            g.wait_ge(me.s_negb.h, negb_all)
            me.sem_w.inc(g.dma_start(out=me.masks[:, :], in_=me.mask_d[:, :]), 16)
        w_mask = me.sem_w.v

        @block.sync
        def _(sy):
            sy.wait_ge(me.s_qk.h, qk_all)
            sy.wait_ge(me.s_negb.h, negb_all)
            me.sem_x.inc(sy.dma_start(out=me.sjs[:, :], in_=me.sjh_d[:, :]), 16)
            for n, (h, ib) in enumerate(blocks):
                if n >= 2:
                    sy.wait_ge(s_cib.h, n - 1)
                s_sl[n % 2].inc(sy.dma_start(
                    out=me.slbuf[n % 2][:, :],
                    in_=me.slneg_d[0:1, S * h + 512 * ib:S * h + 512 * (ib + 1)]), 16)
        sjs_mark = me.sem_x.v

        @block.tensor
        def _(t):
            t.wait_ge(me.s_tpev.h, me.p1_tpev_total)
            t.wait_ge(me.s_vtev.h, me.p1_vtev_total)
            t.wait_ge(me.s_c1.h, 2)
            t.wait_ge(me.sem_w.h, w_mask)
            for n, (h, ib) in enumerate(blocks):
                jm = jmax[n]
                t.wait_ge(s_cib.h, n + 1)
                if n >= 1:
                    t.wait_ge(s_norm.h, n)
                for jt in range(jm):
                    t.wait_ge(s_exp.h, exp0 + cum[n] + max(0, jt - 1))
                    bank = pf[jt % 2]
                    partial = jt >= 4 * ib
                    t.matmul(bank[:, :], me.kT[h][:, 128 * jt:128 * (jt + 1)],
                             me.qT[h][:, 512 * ib:512 * (ib + 1)],
                             start=True, stop=False)
                    ins = t.matmul(bank[:, :], me.e1b[:, :],
                                   me.cibuf[n % 2][:, :],
                                   start=False, stop=not partial)
                    if partial:
                        k = jt - 4 * ib
                        ins = t.matmul(bank[:, :], me.idb[:, :],
                                       me.masks[:, 512 * k:512 * (k + 1)],
                                       start=False, stop=True)
                    s_sc.inc(ins)
                    if jt >= 1:
                        t.wait_ge(s_exp.h, exp0 + cum[n] + jt)
                        for isub in range(4):
                            ins = t.matmul(
                                pav[isub][:, 0:129],
                                me.strip[:, 512 * (jt - 1) + 128 * isub:
                                         512 * (jt - 1) + 128 * (isub + 1)],
                                me.Vt[h][:, 129 * (jt - 1):129 * jt],
                                start=(jt - 1 == 0), stop=False)
                t.wait_ge(s_exp.h, exp0 + cum[n] + jm)
                for isub in range(4):
                    ins = t.matmul(
                        pav[isub][:, 0:129],
                        me.strip[:, 512 * (jm - 1) + 128 * isub:
                                 512 * (jm - 1) + 128 * (isub + 1)],
                        me.Vt[h][:, 129 * (jm - 1):129 * jm],
                        start=(jm - 1 == 0), stop=True)
                    if isub == 3:
                        s_av.inc(ins)
                t.wait_ge(s_norm.h, n + 1)
                if n >= 2:
                    t.wait_ge(s_aot.h, n - 1)
                bank = pbb[n % 2]
                for isub in range(4):
                    ins = t.transpose(bank[:, 128 * isub:128 * (isub + 1)],
                                      ao_sb[:, 128 * isub:128 * (isub + 1)],
                                      me.idb[:, :])
                    if isub == 3:
                        s_aotp.inc(ins)

        @block.scalar
        def _(s):
            s.wait_ge(me.s_c1.h, 2)
            s.wait_ge(me.sem_x.h, sjs_mark)
            for n, (h, ib) in enumerate(blocks):
                s.wait_ge(s_av.h, n)
                if n == 0:
                    s.wait_ge(me.s_negb.h, 4)
                s.wait_ge(s_sl[n % 2].h, 16 * (n // 2 + 1))
                s_cib.inc(s.activation(
                    me.cibuf[n % 2][0:1, :], me.slbuf[n % 2][:, :],
                    AF.Identity, bias=me.bnd[:, 16 + h:17 + h], scale=1.0))
                for jt in range(jmax[n]):
                    s.wait_ge(s_sc.h, sc0 + cum[n] + jt + 1)
                    s_exp.inc(s.activation(
                        me.strip[:, 512 * jt:512 * (jt + 1)], pf[jt % 2][:, :],
                        AF.Exp, bias=me.sjs[:, 32 * h + jt:32 * h + jt + 1],
                        scale=1.0))

        @block.vector
        def _(v):
            v.wait_ge(me.s_qk.h, qk_all)
            v.memset(me.cibuf[0][:, :], 0.0)
            me.s_c1.inc(v.memset(me.cibuf[1][:, :], 0.0))
            for n, (h, ib) in enumerate(blocks):
                v.wait_ge(s_av.h, n + 1)
                for isub in range(4):
                    v.reciprocal(rcp[:, isub:isub + 1], pav[isub][:, 128:129])
                v.drain()
                if n >= 1:
                    v.wait_ge(s_aotp.h, n)
                for isub in range(4):
                    ins = v.tensor_scalar(
                        ao_sb[:, 128 * isub:128 * (isub + 1)],
                        pav[isub][:, 0:128], rcp[:, isub:isub + 1], None,
                        op0=ALU.mult)
                    if isub == 3:
                        s_norm.inc(ins)
                v.wait_ge(s_aotp.h, n + 1)
                bank = pbb[n % 2]
                s_aot.inc(v.tensor_copy(
                    me.aoT[h][:, 512 * ib:512 * (ib + 1)], bank[:, 0:512]))

    def phase4(self):
        nc, block, me = self.nc, self.block, self
        sem = self.sem
        s_cx = [sem(f"s_cx{h}") for h in range(4)]
        s_cc = sem("s_cc")
        self.s_at = [sem(f"s_at{k}") for k in range(4)]

        @block.sync
        def _(sy):
            for h in range(4):
                sy.wait_ge(me.s_aot.h, 8 * (h + 1))
                s_cx[h].inc(sy.dma_start(
                    out=me.ccin.ap()[128 * h:128 * (h + 1), :],
                    in_=me.aoT[h][:, :]), 16)

        @block.gpsimd
        def _(g):
            for k in range(4):
                g.wait_ge(s_cx[k].h, 16)
                s_cc.inc(g.collective_compute(
                    "AllGather", ALU.bypass,
                    replica_groups=[[0, 1, 2, 3], [4, 5, 6, 7]],
                    ins=[me.ccin.ap()[128 * k:128 * (k + 1), :].opt()],
                    outs=[me.ccout.ap()[512 * k:512 * (k + 1), :].opt()]), 1)
        self.s_cc = s_cc

    def phase5(self):
        nc, block, me = self.nc, self.block, self
        es5 = self.es.enter_context(contextlib.ExitStack())
        self.es5 = es5
        sb5 = lambda n, s, d: es5.enter_context(nc.sbuf_tensor(n, s, d))
        ps5 = lambda n, s, d: es5.enter_context(nc.psum_tensor(n, s, d))
        sem = self.sem
        atf = sb5("atf", [128, NCH * 1024], BF16)
        wo_sb = sb5("wo_sb", [128, NCH * 2048], BF16)
        bob = sb5("bob", [128, D], BF16)
        xb5 = [sb5(f"xb5_{i}", [128, D], F32) for i in range(2)]
        y1 = sb5("y1", [128, D], F32)
        x1t = [sb5(f"x1t{i}", [128, D], F32) for i in range(2)]
        pw = [ps5(f"pw{i}", [128, 512], F32) for i in range(4)]

        s_wo = sem("s_wo")
        s_qk5 = sem("s_qk5")
        s_st5, s_rs5, s_nb5 = me.s_stat, me.s_rstd, me.s_nb
        st5_0, rs5_0, nb5_0 = s_st5.v, s_rs5.v, s_nb5.v
        s_y1 = sem("s_y1")
        s_x1 = sem("s_x1")
        s_x1s = sem("s_x1s")
        s_xb5 = [sem("s_xb5a"), sem("s_xb5b")]
        s_bz = sem("s_bz")
        self.s_x1s = s_x1s

        atf3 = atf[:, :].rearrange("p (c t) -> p c t", c=NCH)
        wo3 = wo_sb[:, :].rearrange("p (c t) -> p c t", c=NCH)

        @block.gpsimd
        def _(g):
            view = me.ccout.ap().rearrange("r (q c) -> (r q) c", q=4)
            for k in range(4):
                g.wait_ge(me.s_cc.h, k + 1)
                for r in range(4):
                    ch = 4 * r + k
                    me.s_at[k].inc(g.indirect_dma_start(
                        out=atf3[:, ch, :], out_offset=None, in_=view,
                        in_offset=IndirectOffsetOnAxis(ap=me.gidx_s[:, ch:ch + 1], axis=0)), 16)
            for oc in range(4):
                srcw = me.wo[:, 512 * oc:512 * (oc + 1)].rearrange(
                    "(c p) q -> p c q", p=128)
                dstw = wo3[:, :, 512 * oc:512 * (oc + 1)]
                s_wo.inc(g.dma_start(out=dstw, in_=srcw), 16)
            g.wait_ge(s_bz.h, 1)
            s_wo.inc(g.dma_start(out=bob[0:1, :], in_=me.bo_row[:, :]), 16)

        @block.sync
        def _(sy):
            for i in range(8):
                if i >= 2:
                    sy.wait_ge(s_x1.h, i - 1)
                s_xb5[i % 2].inc(sy.dma_start(
                    out=xb5[i % 2][:, :],
                    in_=me.xsl[128 * i:128 * (i + 1), :]), 16)
                sy.wait_ge(s_x1.h, i + 1)
                s_x1s.inc(sy.dma_start(
                    out=me.x1dram[128 * i:128 * (i + 1), :],
                    in_=x1t[i % 2][:, :]), 16)

        @block.tensor
        def _(t):
            t.wait_ge(s_wo.h, 80)
            for i in range(8):
                if i >= 1:
                    t.wait_ge(s_y1.h, 4 * i)
                    t.wait_ge(s_st5.h, st5_0 + i)
                for oc in range(4):
                    for k in range(4):
                        if i == 0 and oc == 0:
                            t.wait_ge(me.s_at[k].h, 64)
                        for r in range(4):
                            ch = 4 * r + k
                            t.matmul(pw[oc][:, :], atf3[:, ch, 128 * i:128 * (i + 1)],
                                     wo3[:, ch, 512 * oc:512 * (oc + 1)],
                                     start=(k == 0 and r == 0), stop=False)
                    ins = t.matmul(pw[oc][:, :], me.e1b[:, :],
                                   bob[:, 512 * oc:512 * (oc + 1)],
                                   start=False, stop=True)
                    s_qk5.inc(ins)

        @block.vector
        def _(v):
            s_bz.inc(v.memset(bob[:, :], 0.0))
            for i in range(8):
                v.wait_ge(s_qk5.h, 4 * i + 4)
                for oc in range(4):
                    v.bn_stats(me.stats[:, 6 * oc:6 * oc + 6], pw[oc][:, :])
                v.drain()
                s_st5.inc(v.bn_aggr(me.mv[i % 2][:, :], me.stats[:, :]))
                v.wait_ge(s_rs5.h, rs5_0 + i + 1)
                v.reciprocal(me.rstd[i % 2][:, :], me.rstd[i % 2][:, :])
                v.drain()
                s_nb5.inc(v.tensor_scalar(
                    me.nbias[i % 2][:, :], me.rstd[i % 2][:, :],
                    me.mv[i % 2][:, 0:1], -1.0, op0=ALU.mult, op1=ALU.mult))
                v.wait_ge(s_y1.h, 4 * i + 4)
                v.wait_ge(s_xb5[i % 2].h, 16 * (i // 2 + 1))
                if i >= 2:
                    v.wait_ge(s_x1s.h, 16 * i)
                s_x1.inc(v.tensor_tensor(x1t[i % 2][:, :], y1[:, :],
                                         xb5[i % 2][:, :], op=ALU.add))

        @block.scalar
        def _(s):
            for i in range(8):
                s.wait_ge(s_st5.h, st5_0 + i + 1)
                s_rs5.inc(s.activation(
                    me.rstd[i % 2][:, :], me.mv[i % 2][:, 1:2],
                    AF.Sqrt, bias=me.epsv[:, 0:1], scale=1.0))
                s.wait_ge(s_nb5.h, nb5_0 + i + 1)
                for oc in range(4):
                    s_y1.inc(s.activation(
                        y1[:, 512 * oc:512 * (oc + 1)], pw[oc][:, :], AF.Identity,
                        bias=me.nbias[i % 2][:, 0:1], scale=me.rstd[i % 2][:, 0:1]))

    def phase6(self):
        nc, block, me = self.nc, self.block, self
        es6 = self.es.enter_context(contextlib.ExitStack())
        sb6 = lambda n, s, d: es6.enter_context(nc.sbuf_tensor(n, s, d))
        sem = self.sem


        wvT = [sb6(f"wvT{ib}", [128, 4 * 8 * 128], BF16) for ib in range(2)]
        wv_sb = [sb6(f"wv{ib}", [128, 512], BF16) for ib in range(2)]
        wf = sb6("wf", [128, 8 * 8], F32)
        wfT = sb6("wfT", [128, 1024], BF16)

        esL = es6.enter_context(contextlib.ExitStack())
        sbL = lambda n, s, d: esL.enter_context(nc.sbuf_tensor(n, s, d))
        ln2T = sbL("ln2T", [128, NCH * 1024], BF16)
        uall1 = sbL("uall1", [128, 8 * 512], BF16)
        uall3 = sbL("uall3", [128, 8 * 512], BF16)
        u1sb = [uall1[0:32, :], uall1[32:64, :]]
        u3sb = [uall3[0:32, :], uall3[32:64, :]]
        web = sbL("web", [128, 8 * 1024], BF16)
        ln2T3 = ln2T[:, :].rearrange("p (c t) -> p c t", c=NCH)
        web3 = web[:, :].rearrange("p (e t) -> p e t", e=8)

        esa = es6.enter_context(contextlib.ExitStack())
        sba = lambda n, s, d: esa.enter_context(nc.sbuf_tensor(n, s, d))
        psa = lambda n, s, d: esa.enter_context(nc.psum_tensor(n, s, d))
        xb6_ = sba("xb6", [128, D], F32)
        xb6 = [xb6_, xb6_]
        ln6_ = sba("ln6", [128, D], BF16)
        ln6 = [ln6_, ln6_]
        aubuf = sba("aubuf", [128, NCH * 256], BF16)
        gw_sb = sba("gw_sb", [128, NCH * 8], BF16)
        lgs = sba("lgs", [128, 8], F32)
        vals = sba("vals", [128, 8], F32)
        dbuf = sba("dbuf", [128, 4], F32)
        eqa = sba("eqa", [128, 8], F32)
        eqb = sba("eqb", [128, 8], F32)
        pga = [psa(f"pga{i}", [128, 512], F32) for i in range(2)]
        pgb = [psa(f"pgb{i}", [128, 1024], BF16) for i in range(2)]
        pgc = psa("pgc", [128, 512], F32)

        s_x6 = [sem("s_x6a"), sem("s_x6b")]
        s_st6, s_rs6, s_nb6 = me.s_stat, me.s_rstd, me.s_nb
        st6_0, rs6_0, nb6_0 = s_st6.v, s_rs6.v, s_nb6.v
        s_ln6 = sem("s_ln6")
        s_tp6 = sem("s_tp6"); s_te6 = sem("s_te6")
        s_au = sem("s_au")
        s_gw = sem("s_gw")
        s_lg = sem("s_lg")
        s_gv = sem("s_gv")
        s_gs = sem("s_gs")
        s_wfv = sem("s_wfv")
        s_wft = sem("s_wft")
        s_wfe = sem("s_wfe")
        s_web = sem("s_web")
        s_webe = sem("s_webe")
        s_um = sem("s_um")
        s_ue = sem("s_ue")

        au3v = aubuf[:, :].rearrange("p (c t) -> p c t", c=NCH)
        gw3 = gw_sb[:, :].rearrange("p (c t) -> p c t", c=NCH)

        @block.gpsimd
        def _(g):
            g.wait_ge(me.s_x1s.h, 16 * 8)
            srcg = me.gwT[:, :].rearrange("(c p) q -> p c q", p=128)
            s_gw.inc(g.dma_start(out=gw3, in_=srcg), 16)
            srca = me.au1[:, :].rearrange("(c p) q -> p c q", p=128)
            s_au.inc(g.dma_start(out=au3v, in_=srca), 16)

        @block.sync
        def _(sy):
            sy.wait_ge(me.s_x1s.h, 16 * 8)
            for it in range(8):
                if it >= 1:
                    sy.wait_ge(s_ln6.h, it)
                s_x6[it % 2].inc(sy.dma_start(
                    out=xb6[it % 2][:, :],
                    in_=me.x1dram[128 * it:128 * (it + 1), :]), 16)

        @block.vector
        def _(v):
            v.drain()
            v.memset(wfT[:, :], 0.0)
            v.memset(uall1[:, :], 0.0)
            v.memset(uall3[:, :], 0.0)
            for it in range(8):
                v.wait_ge(s_x6[it % 2].h, 16 * (it // 2 + 1))
                xt = xb6[it % 2]
                for q in range(4):
                    v.bn_stats(me.stats[:, 6 * q:6 * q + 6],
                               xt[:, 512 * q:512 * (q + 1)])
                v.drain()
                s_st6.inc(v.bn_aggr(me.mv[it % 2][:, :], me.stats[:, :]))
                if it >= 1:
                    for half in range(2):
                        nep = 2 * (it - 1) + half
                        v.wait_ge(s_tp6.h, nep + 1)
                        bank = pgb[nep % 2]
                        dst = ln2T3[:, 8 * half:8 * half + 8,
                                    128 * (it - 1):128 * it]
                        srcb = bank[:, 0:1024].rearrange("p (c t) -> p c t", c=8)
                        s_te6.inc(v.tensor_copy(dst, srcb))
                v.wait_ge(s_rs6.h, rs6_0 + it + 1)
                v.reciprocal(me.rstd[it % 2][:, :], me.rstd[it % 2][:, :])
                v.drain()
                s_nb6.inc(v.tensor_scalar(
                    me.nbias[it % 2][:, :], me.rstd[it % 2][:, :],
                    me.mv[it % 2][:, 0:1], -1.0, op0=ALU.mult, op1=ALU.mult))
                if it >= 1:
                    self.emit_gate_v(v, it - 1, lgs, vals, dbuf, eqa, eqb, wf,
                                     pgc, s_lg, s_gv, s_gs, s_wfv)
            for half in range(2):
                nep = 2 * 7 + half
                v.wait_ge(s_tp6.h, nep + 1)
                bank = pgb[nep % 2]
                dst = ln2T3[:, 8 * half:8 * half + 8, 128 * 7:128 * 8]
                srcb = bank[:, 0:1024].rearrange("p (c t) -> p c t", c=8)
                s_te6.inc(v.tensor_copy(dst, srcb))
            self.emit_gate_v(v, 7, lgs, vals, dbuf, eqa, eqb, wf,
                             pgc, s_lg, s_gv, s_gs, s_wfv)
            for it in range(8):
                v.wait_ge(s_wft.h, it + 1)
                s_wfe.inc(v.tensor_copy(wfT[0:8, 128 * it:128 * (it + 1)],
                                        pga[it % 2][0:8, 0:128]))
            for e in range(8):
                for hf in range(2):
                    n = 2 * e + hf
                    v.wait_ge(s_web.h, n + 1)
                    s_webe.inc(v.tensor_copy(
                        web3[:, e, 512 * hf:512 * (hf + 1)],
                        pga[n % 2][:, :]))
            for n in range(32):
                v.wait_ge(s_um.h, n + 1)
                path, ib, slot = n // 16, (n // 8) % 2, n % 8
                usb = (u1sb if path == 0 else u3sb)[ib]
                s_ue.inc(v.tensor_scalar(
                    usb[0:16, 512 * slot:512 * (slot + 1)],
                    pga[n % 2][0:16, :], LORA, None, op0=ALU.mult))

        @block.scalar
        def _(s):
            for it in range(8):
                s.wait_ge(s_st6.h, st6_0 + it + 1)
                s_rs6.inc(s.activation(
                    me.rstd[it % 2][:, :], me.mv[it % 2][:, 1:2],
                    AF.Sqrt, bias=me.epsv[:, 0:1], scale=1.0))
                s.wait_ge(s_nb6.h, nb6_0 + it + 1)
                if it >= 1:
                    s.wait_ge(s_tp6.h, 2 * (it - 1) + 2)
                s_ln6.inc(s.activation(
                    ln6[it % 2][:, :], xb6[it % 2][:, :], AF.Identity,
                    bias=me.nbias[it % 2][:, 0:1], scale=me.rstd[it % 2][:, 0:1]))
                if it >= 1:
                    s.wait_ge(s_gv.h, it)
                    s_gs.inc(s.activation(dbuf[:, 1:2], dbuf[:, 0:1],
                                          AF.Sigmoid, bias=0.0, scale=1.0))
            s.wait_ge(s_gv.h, 8)
            s_gs.inc(s.activation(dbuf[:, 1:2], dbuf[:, 0:1],
                                  AF.Sigmoid, bias=0.0, scale=1.0))

        @block.tensor
        def _(t):
            t.wait_ge(s_gw.h, 16)
            t.wait_ge(s_au.h, 16)
            for it in range(8):
                t.wait_ge(s_ln6.h, it + 1)
                for half in range(2):
                    nep = 2 * it + half
                    if nep >= 2:
                        t.wait_ge(s_te6.h, nep - 1)
                    bank = pgb[nep % 2]
                    for c8 in range(8):
                        ch = 8 * half + c8
                        ins = t.transpose(bank[:, 128 * c8:128 * (c8 + 1)],
                                          ln6[it % 2][:, 128 * ch:128 * (ch + 1)],
                                          me.idb[:, :])
                        if c8 == 7:
                            s_tp6.inc(ins)
                t.wait_ge(s_te6.h, 2 * it + 2)
                if it >= 1:
                    t.wait_ge(s_gv.h, it)
                for ch in range(NCH):
                    ins = t.matmul(pgc[:, 0:8], ln2T3[:, ch, 128 * it:128 * (it + 1)],
                                   gw3[:, ch, :], start=(ch == 0), stop=(ch == NCH - 1))
                    if ch == NCH - 1:
                        s_lg.inc(ins)
            for it in range(8):
                t.wait_ge(s_wfv.h, it + 1)
                if it >= 2:
                    t.wait_ge(s_wfe.h, it - 1)
                s_wft.inc(t.transpose(pga[it % 2][0:8, 0:128],
                                      wf[:, 8 * it:8 * (it + 1)], me.idf[:, :]))
            t.wait_ge(s_wfe.h, 8)
            for e in range(8):
                for hf in range(2):
                    n = 2 * e + hf
                    if n >= 2:
                        t.wait_ge(s_webe.h, n - 1)
                    s_web.inc(t.matmul(
                        pga[n % 2][:, :], me.onehots[:, 128 * e:128 * (e + 1)],
                        wfT[:, 512 * hf:512 * (hf + 1)], start=True, stop=True))
            t.wait_ge(s_webe.h, 16)
            for n in range(32):
                path, ib, slot = n // 16, (n // 8) % 2, n % 8
                if n == 16:
                    t.wait_ge(s_au.h, 32)
                if n >= 2:
                    t.wait_ge(s_ue.h, n - 1)
                for ch in range(NCH):
                    ins = t.matmul(
                        pga[n % 2][0:32, :],
                        au3v[:, ch, 32 * slot:32 * (slot + 1)],
                        ln2T3[:, ch, 512 * ib:512 * (ib + 1)],
                        start=(ch == 0), stop=(ch == NCH - 1))
                    if ch == NCH - 1:
                        s_um.inc(ins)

        @block.gpsimd
        def _(g):
            g.wait_ge(s_um.h, 16)
            srca = me.au3[:, :].rearrange("(c p) q -> p c q", p=128)
            s_au.inc(g.dma_start(out=au3v, in_=srca), 16)

        self.esa = esa
        self.s_ue, self.s_webe = s_ue, s_webe
        self.p6_shared = dict(wvT=wvT, wv_sb=wv_sb, wf=wf,
                              ln2T3=ln2T3, u1sb=u1sb, u3sb=u3sb, web3=web3,
                              uall1=uall1, uall3=uall3, esL=esL, es6=es6)

    def emit_gate_v(self, v, it, lgs, vals, dbuf, eqa, eqb, wf, pgc,
                    s_lg, s_gv, s_gs, s_wfv):
        me = self
        v.wait_ge(s_lg.h, it + 1)
        v.tensor_copy(lgs[:, :], pgc[:, 0:8])
        v.drain()
        v.max(vals[:, :], lgs[:, :])
        v.drain()
        s_gv.inc(v.tensor_scalar(dbuf[:, 0:1], vals[:, 0:1], vals[:, 1:2],
                                 None, op0=ALU.subtract))
        v.wait_ge(s_gs.h, it + 1)
        v.tensor_scalar(dbuf[:, 2:3], dbuf[:, 1:2], -1.0, 1.0,
                        op0=ALU.mult, op1=ALU.add)
        v.tensor_scalar(eqa[:, :], lgs[:, :], vals[:, 0:1], None, op0=ALU.is_equal)
        v.tensor_scalar(eqb[:, :], lgs[:, :], vals[:, 1:2], None, op0=ALU.is_equal)
        v.drain()
        v.tensor_tensor(lgs[:, :], eqb[:, :], eqa[:, :], op=ALU.mult)
        v.drain()
        v.tensor_tensor(eqb[:, :], eqb[:, :], lgs[:, :], op=ALU.subtract)
        v.tensor_scalar(eqa[:, :], eqa[:, :], dbuf[:, 1:2], None, op0=ALU.mult)
        v.drain()
        s_wfv.inc(v.scalar_tensor_tensor(wf[:, 8 * it:8 * (it + 1)], eqb[:, :],
                                         dbuf[:, 2:3], eqa[:, :],
                                         op0=ALU.mult, op1=ALU.add))

    def phase6b(self):
        nc, block, me = self.nc, self.block, self
        sh = self.p6_shared
        es6 = sh["es6"]
        ln2T3, web3 = sh["ln2T3"], sh["web3"]
        uall1, uall3 = sh["uall1"], sh["uall3"]
        sem = self.sem
        self.pv = [es6.enter_context(nc.psum_tensor(f"pv{ib}", [128, 512], F32))
                   for ib in range(2)]
        pv = self.pv
        esb = es6.enter_context(contextlib.ExitStack())
        sbB = lambda n, s, d: esb.enter_context(nc.sbuf_tensor(n, s, d))
        psB = lambda n, s, d: esb.enter_context(nc.psum_tensor(n, s, d))
        self.esb = esb
        w1b = sbB("w1b", [128, NCH * 128], BF16)
        w3b = sbB("w3b", [128, NCH * 128], BF16)
        a2b = [sbB(f"a2b{i}", [128, 128], BF16) for i in range(2)]
        b1b = [[sbB(f"b1b{i}_{ib}", [128, 8 * 128], BF16) for ib in range(2)]
               for i in range(2)]
        b3b = [[sbB(f"b3b{i}_{ib}", [128, 8 * 128], BF16) for ib in range(2)]
               for i in range(2)]
        bs1_ = [sbB(f"bs1_{ib}", [128, 512], BF16) for ib in range(2)]
        bs3_ = [sbB(f"bs3_{ib}", [128, 512], BF16) for ib in range(2)]
        bs1 = [bs1_, bs1_]
        bs3 = [bs3_, bs3_]
        s1 = [sbB(f"s1_{i}", [128, 512], BF16) for i in range(2)]
        act = [sbB(f"act_{i}", [128, 512], BF16) for i in range(4)]
        wact = [sbB(f"wact_{i}", [128, 512], BF16) for i in range(4)]
        HB1 = [psB(f"hb1_{i}", [128, 512], F32) for i in range(2)]
        HB3 = [psB(f"hb3_{i}", [128, 512], F32) for i in range(2)]
        pmac = [psB(f"pmac{ib}", [128, 512], F32) for ib in range(2)]

        s_wt = [sem("s_wt0"), sem("s_wt1")]
        s_b1 = sem("s_b1"); s_b3 = sem("s_b3")
        s_bev1 = sem("s_bev1"); s_bev3 = sem("s_bev3")
        s_h1e = sem("s_h1e"); s_h3e = sem("s_h3e")
        s_silu = sem("s_silu"); s_act = sem("s_act"); s_wact = sem("s_wact")
        s_vm = sem("s_vm"); s_imm = sem("s_imm")
        s_mev = [sem("s_mev0"), sem("s_mev1")]
        s_mst = sem("s_mst")
        mstage = [sbB(f"mstage{i}", [128, 512], BF16) for i in range(2)]
        self.s_mst = s_mst
        s_bbz = sem("s_bbz")
        self.s_vm, self.s_mev = s_vm, s_mev
        self.s_vm_total = 16 * NFT
        ue_total = 32

        pool1_free = []
        pool3_free = []

        @block.gpsimd
        def _(g):
            g.wait_ge(me.s_ue.h, ue_total)
            g.wait_ge(s_bbz.h, 1)
            for ft in range(NFT):
                par = ft % 2
                s = s_wt[par]
                if ft >= 1:
                    g.wait_ge(s_b1.h, 2 * ft)
                    g.wait_ge(s_b3.h, 2 * ft)
                if ft >= 2:
                    g.wait_ge(s_h1e.h, 16 * (ft - 1))
                    g.wait_ge(s_h3e.h, 16 * (ft - 1))
                    g.wait_ge(s_vm.h, 16 * (ft - 1))
                src1 = me.w1T[:, 128 * ft:128 * (ft + 1)].rearrange("(c p) q -> p c q", p=128)
                s.inc(g.dma_start(out=w1b[:, :].rearrange("p (c q) -> p c q", c=NCH), in_=src1), 16)
                src3 = me.w3T[:, 128 * ft:128 * (ft + 1)].rearrange("(c p) q -> p c q", p=128)
                s.inc(g.dma_start(out=w3b[:, :].rearrange("p (c q) -> p c q", c=NCH), in_=src3), 16)
                s.inc(g.dma_start(out=a2b[par][:, :], in_=me.a2cat[128 * ft:128 * (ft + 1), :]), 16)
                s.inc(g.dma_start(out=b1b[par][0][0:16, :], in_=me.b1pk[ft:ft + 1].rearrange("o p q -> (o p) q")), 16)
                s.inc(g.dma_start(out=b1b[par][1][32:48, :], in_=me.b1pk[ft:ft + 1].rearrange("o p q -> (o p) q")), 16)
                s.inc(g.dma_start(out=b3b[par][0][0:16, :], in_=me.b3pk[ft:ft + 1].rearrange("o p q -> (o p) q")), 16)
                s.inc(g.dma_start(out=b3b[par][1][32:48, :], in_=me.b3pk[ft:ft + 1].rearrange("o p q -> (o p) q")), 16)

        @block.sync
        def _(sy):
            for ft in range(NFT):
                for ib in range(2):
                    n = 2 * ft + ib
                    sy.wait_ge(s_mev[ib].h, ft + 1)
                    s_mst.inc(sy.dma_start(
                        out=me.maccd[ib][ft:ft + 1].rearrange("o p q -> (o p) q"),
                        in_=mstage[n % 2][:, :]), 16)

        LAG = 3

        @block.tensor
        def _(t):
            t.wait_ge(me.s_ue.h, ue_total)
            t.wait_ge(me.s_webe.h, 16)
            w1v = w1b[:, :].rearrange("p (c q) -> p c q", c=NCH)
            w3v = w3b[:, :].rearrange("p (c q) -> p c q", c=NCH)
            pending = []

            def emit_vi(step):
                ft_, e_, ib_, G_ = step
                t.wait_ge(s_act.h, G_ + 1)
                for isub in range(4):
                    off = 16 * (8 * isub + e_)
                    ins = t.matmul(
                        pv[ib_][:, off:off + 16],
                        act[G_ % 4][:, 128 * isub:128 * (isub + 1)],
                        a2b[ft_ % 2][:, 16 * e_:16 * (e_ + 1)],
                        start=(ft_ == 0 and e_ == 0 and isub == 0),
                        stop=(ft_ == NFT - 1 and e_ == 7 and isub == 3),
                        skip_group_check=True)
                    if isub == 3:
                        s_vm.inc(ins)
                t.wait_ge(s_wact.h, G_ + 1)
                if e_ == 0:
                    t.wait_ge(s_mev[ib_].h, ft_)
                s_imm.inc(t.matmul(
                    pmac[ib_][:, :], me.idb[:, :], wact[G_ % 4][:, :],
                    start=(e_ == 0), stop=(e_ == 7)))

            for ft in range(NFT):
                par = ft % 2
                t.wait_ge(s_wt[par].h, 112 * (ft // 2 + 1))
                for ib in range(2):
                    u = len(pool1_free)
                    if u >= 2:
                        t.wait_ge(*pool1_free[u - 2])
                    bank = HB1[u % 2]
                    for ch in range(NCH):
                        ins = t.matmul(bank[:, :], w1v[:, ch, :],
                                       ln2T3[:, ch, 512 * ib:512 * (ib + 1)],
                                       start=(ch == 0), stop=(ch == NCH - 1))
                    s_b1.inc(ins)
                    pool1_free.append((s_bev1.h, 2 * ft + ib + 1))
                    u = len(pool3_free)
                    if u >= 2:
                        t.wait_ge(*pool3_free[u - 2])
                    bank = HB3[u % 2]
                    for ch in range(NCH):
                        ins = t.matmul(bank[:, :], w3v[:, ch, :],
                                       ln2T3[:, ch, 512 * ib:512 * (ib + 1)],
                                       start=(ch == 0), stop=(ch == NCH - 1))
                    s_b3.inc(ins)
                    pool3_free.append((s_bev3.h, 2 * ft + ib + 1))
                for e in range(8):
                    for ib in range(2):
                        G = 16 * ft + 2 * e + ib
                        u = len(pool1_free)
                        if u >= 2:
                            t.wait_ge(*pool1_free[u - 2])
                        t.wait_ge(s_bev1.h, 2 * ft + ib + 1)
                        bank = HB1[u % 2]
                        t.matmul(bank[:, :], me.idb[:, :], bs1[par][ib][:, :],
                                 start=True, stop=False)
                        s_h1e.inc(t.matmul(bank[:, :], b1b[par][ib][:, 128 * e:128 * (e + 1)],
                                           uall1[:, 512 * e:512 * (e + 1)],
                                           start=False, stop=True))
                        pool1_free.append((s_silu.h, G + 1))
                        u = len(pool3_free)
                        if u >= 2:
                            t.wait_ge(*pool3_free[u - 2])
                        t.wait_ge(s_bev3.h, 2 * ft + ib + 1)
                        bank = HB3[u % 2]
                        t.matmul(bank[:, :], me.idb[:, :], bs3[par][ib][:, :],
                                 start=True, stop=False)
                        s_h3e.inc(t.matmul(bank[:, :], b3b[par][ib][:, 128 * e:128 * (e + 1)],
                                           uall3[:, 512 * e:512 * (e + 1)],
                                           start=False, stop=True))
                        pool3_free.append((s_act.h, G + 1))
                        pending.append((ft, e, ib, G))
                        while len(pending) > LAG:
                            emit_vi(pending.pop(0))
            while pending:
                emit_vi(pending.pop(0))

        @block.scalar
        def _(s):
            u1 = 0
            for ft in range(NFT):
                par = ft % 2
                for ib in range(2):
                    s.wait_ge(s_b1.h, 2 * ft + ib + 1)
                    bank = HB1[u1 % 2]
                    s_bev1.inc(s.copy(bs1[par][ib][:, :], bank[:, :]))
                    u1 += 1
                    s.wait_ge(s_b3.h, 2 * ft + ib + 1)
                    bank = HB3[(u1 - 1) % 2]
                    s_bev3.inc(s.copy(bs3[par][ib][:, :], bank[:, :]))
                for e in range(8):
                    for ib in range(2):
                        G = 16 * ft + 2 * e + ib
                        s.wait_ge(s_h1e.h, G + 1)
                        if G >= 2:
                            s.wait_ge(s_act.h, G - 1)
                        bank = HB1[(18 * ft + 2 + 2 * e + ib) % 2]
                        s_silu.inc(s.activation(s1[G % 2][:, :], bank[:, :],
                                                AF.Silu, bias=0.0, scale=1.0))
                u1 = 18 * (ft + 1)

        @block.vector
        def _(v):
            v.drain()
            ins = None
            for par in range(2):
                for ibv in range(2):
                    v.memset(b1b[par][ibv][:, :], 0.0)
                    ins = v.memset(b3b[par][ibv][:, :], 0.0)
            s_bbz.inc(ins)
            for ft in range(NFT):
                for e in range(8):
                    for ib in range(2):
                        G = 16 * ft + 2 * e + ib
                        v.wait_ge(s_silu.h, G + 1)
                        v.wait_ge(s_h3e.h, G + 1)
                        if G >= 4:
                            v.wait_ge(s_vm.h, G - 3)
                        bank = HB3[(18 * ft + 2 + 2 * e + ib) % 2]
                        s_act.inc(v.tensor_tensor(
                            act[G % 4][:, :], s1[G % 2][:, :], bank[:, :],
                            op=ALU.mult))
                    for ib in range(2):
                        G = 16 * ft + 2 * e + ib
                        if G >= 4:
                            v.wait_ge(s_imm.h, G - 3)
                        if SIM_SAFE[0]:
                            v.drain()
                        s_wact.inc(v.tensor_tensor(
                            wact[G % 4][:, :], act[G % 4][:, :],
                            web3[:, e, 512 * ib:512 * (ib + 1)], op=ALU.mult))
                        if e == 1 and ft >= 1:
                            v.wait_ge(s_imm.h, 16 * (ft - 1) + 15 + ib)
                            n = 2 * (ft - 1) + ib
                            if n >= 2:
                                v.wait_ge(s_mst.h, 16 * n)
                            s_mev[ib].inc(v.tensor_copy(
                                mstage[n % 2][:, :], pmac[ib][:, :]))
            for ib in range(2):
                v.wait_ge(s_imm.h, 16 * (NFT - 1) + 15 + ib)
                n = 2 * (NFT - 1) + ib
                v.wait_ge(s_mst.h, 16 * n)
                s_mev[ib].inc(v.tensor_copy(mstage[n % 2][:, :], pmac[ib][:, :]))

    def phase6c(self):
        nc, block, me = self.nc, self.block, self
        sh = self.p6_shared
        es6, esL = sh["es6"], sh["esL"]
        wvT, wv_sb, wf = sh["wvT"], sh["wv_sb"], sh["wf"]
        pv = self.pv
        self.esb.close()
        esL.close()
        sem = self.sem
        esc = es6.enter_context(contextlib.ExitStack())
        sbC = lambda n, s, d: esc.enter_context(nc.sbuf_tensor(n, s, d))
        psC = lambda n, s, d: esc.enter_context(nc.psum_tensor(n, s, d))
        b2full = sbC("b2full", [128, E * D], BF16)
        y2v = [sbC(f"y2v{i}", [128, D], F32) for i in range(4)]
        x1b = [sbC(f"x1b{i}", [128, D], F32) for i in range(2)]
        x2t_ = sbC("x2t", [128, D], F32)
        x2t = [x2t_, x2t_]
        CH6 = 8
        w2b = [sbC(f"w2b{i}", [128, CH6 * 512], BF16) for i in range(2)]
        mb = [sbC(f"mb{i}", [128, CH6 * 512], BF16) for i in range(2)]
        po = [psC(f"po{i}", [128, 512], F32) for i in range(4)]
        pbc = psC("pbc", [16, 1024], BF16)

        s_b2 = sem("s_b2")
        s_zz = sem("s_zz")
        s_wv = [sem("s_wv0"), sem("s_wv1")]
        s_wvt = sem("s_wvt"); s_wve = sem("s_wve")
        s_w2 = [sem("s_w2a"), sem("s_w2b")]
        s_m2 = [sem("s_m2a"), sem("s_m2b")]
        s_w2c = sem("s_w2c")
        blocks6 = []
        ft0 = 0
        while ft0 < NFT:
            blocks6.append((ft0, min(NFT, ft0 + CH6)))
            ft0 += CH6
        s_o2 = sem("s_o2")
        s_ev6 = sem("s_ev6")
        s_st6c, s_rs6c, s_nb6c = me.s_stat, me.s_rstd, me.s_nb
        st6c_0, rs6c_0, nb6c_0 = s_st6c.v, s_rs6c.v, s_nb6c.v
        s_y6 = sem("s_y6")
        s_x2 = sem("s_x2")
        s_xc = [sem("s_xc0"), sem("s_xc1")]
        s_outs = sem("s_outs")
        self.s_outs = s_outs

        @block.vector
        def _(v):
            v.drain()
            v.memset(b2full[:, :], 0.0)
            v.memset(wvT[0][:, :], 0.0)
            s_zz.inc(v.memset(wvT[1][:, :], 0.0))
            for ib in range(2):
                v.wait_ge(me.s_vm.h, me.s_vm_total)
                for isub in range(4):
                    for e in range(8):
                        off = 16 * (8 * isub + e)
                        col = 8 * (4 * ib + isub) + e
                        ins = v.tensor_scalar(
                            wv_sb[ib][:, off:off + 16], pv[ib][:, off:off + 16],
                            wf[:, col:col + 1], 2.0, op0=ALU.mult, op1=ALU.mult)
                        if isub == 3 and e == 7:
                            s_wv[ib].inc(ins)
            for k in range(8):
                ib, isub = k // 4, k % 4
                v.wait_ge(s_wvt.h, k + 1)
                s_wve.inc(v.tensor_copy(
                    wvT[ib][0:16, 1024 * isub:1024 * (isub + 1)], pbc[0:16, :]))

        @block.tensor
        def _(t):
            for k in range(8):
                ib, isub = k // 4, k % 4
                t.wait_ge(s_wv[ib].h, 1)
                if k >= 1:
                    t.wait_ge(s_wve.h, k)
                for e in range(8):
                    off = 16 * (8 * isub + e)
                    ins = t.transpose(pbc[0:16, 128 * e:128 * (e + 1)],
                                      wv_sb[ib][:, off:off + 16], me.idb[:, :])
                    if e == 7:
                        s_wvt.inc(ins)
            t.wait_ge(s_b2.h, 16)
            t.wait_ge(s_wve.h, 8)
            n2 = 0
            for grp in range(2):
                for oc in range(4):
                    noc = 4 * grp + oc
                    if noc >= 1:
                        t.wait_ge(s_ev6.h, 4 * noc)
                    for isub in range(4):
                        for e in range(8):
                            t.matmul(po[isub][:, :],
                                     wvT[grp][:, 1024 * isub + 128 * e:1024 * isub + 128 * (e + 1)],
                                     b2full[:, 2048 * e + 512 * oc:2048 * e + 512 * (oc + 1)],
                                     start=(e == 0), stop=False)
                    for (f0, f1) in blocks6:
                        t.wait_ge(s_w2[n2 % 2].h, 16 * (n2 // 2 + 1))
                        t.wait_ge(s_m2[n2 % 2].h, 16 * (n2 // 2 + 1))
                        for ft in range(f0, f1):
                            for isub in range(4):
                                ins = t.matmul(
                                    po[isub][:, :],
                                    mb[n2 % 2][:, 512 * (ft - f0) + 128 * isub:
                                               512 * (ft - f0) + 128 * (isub + 1)],
                                    w2b[n2 % 2][:, 512 * (ft - f0):512 * (ft - f0 + 1)],
                                    start=False, stop=(ft == NFT - 1))
                                if isub == 3 and ft == f1 - 1:
                                    if ft == NFT - 1:
                                        s_o2.inc(ins)
                                    else:
                                        s_w2c.inc(ins)
                        n2 += 1

        @block.gpsimd
        def _(g):
            g.wait_ge(me.s_mst.h, 16 * 2 * NFT)
            g.wait_ge(s_zz.h, 1)
            s_b2.inc(g.dma_start(out=b2full[0:16, :], in_=me.b2cat[:, :]), 16)
            n2 = 0
            nb = len(blocks6)
            for grp in range(2):
                for oc in range(4):
                    for bi, (f0, f1) in enumerate(blocks6):
                        if n2 >= 2:
                            prev = n2 - 2
                            bi_prev = prev % nb
                            if bi_prev == nb - 1:
                                g.wait_ge(s_o2.h, prev // nb + 1)
                            else:
                                g.wait_ge(s_w2c.h, prev + 1 - (prev // nb))
                        src = me.w2T[128 * f0:128 * f1,
                                     512 * oc:512 * (oc + 1)].rearrange(
                                         "(c p) q -> p c q", p=128)
                        dst = w2b[n2 % 2][:, 0:512 * (f1 - f0)].rearrange(
                            "p (c q) -> p c q", c=f1 - f0)
                        s_w2[n2 % 2].inc(g.dma_start(out=dst, in_=src), 16)
                        n2 += 1


        @block.scalar
        def _(s):
            for grp in range(2):
                if grp >= 1:
                    s.wait_ge(s_x2.h, 4 * grp)
                for oc in range(4):
                    noc = 4 * grp + oc
                    s.wait_ge(s_o2.h, 4 * grp + oc + 1)
                    for isub in range(4):
                        s_ev6.inc(s.copy(y2v[isub][:, 512 * oc:512 * (oc + 1)],
                                         po[isub][:, :]))
                for isub in range(4):
                    i128 = 4 * grp + isub
                    s.wait_ge(s_st6c.h, st6c_0 + 4 * grp + isub + 1)
                    s_rs6c.inc(s.activation(
                        me.rstd[isub % 2][:, :], me.mv[isub % 2][:, 1:2],
                        AF.Sqrt, bias=me.epsv[:, 0:1], scale=1.0))
                    s.wait_ge(s_nb6c.h, nb6c_0 + 4 * grp + isub + 1)
                    s_y6.inc(s.activation(
                        y2v[isub][:, :], y2v[isub][:, :], AF.Identity,
                        bias=me.nbias[isub % 2][:, 0:1],
                        scale=me.rstd[isub % 2][:, 0:1]))

        @block.vector
        def _(v):
            for grp in range(2):
                v.wait_ge(s_ev6.h, 16 * (grp + 1))
                for isub in range(4):
                    i128 = 4 * grp + isub
                    for q in range(4):
                        v.bn_stats(me.stats[:, 6 * q:6 * q + 6],
                                   y2v[isub][:, 512 * q:512 * (q + 1)])
                    v.drain()
                    s_st6c.inc(v.bn_aggr(me.mv[isub % 2][:, :], me.stats[:, :]))
                    v.wait_ge(s_rs6c.h, rs6c_0 + 4 * grp + isub + 1)
                    v.reciprocal(me.rstd[isub % 2][:, :], me.rstd[isub % 2][:, :])
                    v.drain()
                    s_nb6c.inc(v.tensor_scalar(
                        me.nbias[isub % 2][:, :], me.rstd[isub % 2][:, :],
                        me.mv[isub % 2][:, 0:1], -1.0, op0=ALU.mult, op1=ALU.mult))
                    v.wait_ge(s_y6.h, 4 * grp + isub + 1)
                    v.wait_ge(s_xc[i128 % 2].h, 16 * (i128 // 2 + 1))
                    if i128 >= 1:
                        v.wait_ge(s_outs.h, 16 * i128)
                    s_x2.inc(v.tensor_tensor(x2t[i128 % 2][:, :], y2v[isub][:, :],
                                             x1b[i128 % 2][:, :], op=ALU.add))

        @block.sync
        def _(sy):
            sy.wait_ge(me.s_mst.h, 16 * 2 * NFT)
            n2 = 0
            nb = len(blocks6)
            for grp in range(2):
                for oc in range(4):
                    for bi, (f0, f1) in enumerate(blocks6):
                        if n2 >= 2:
                            prev = n2 - 2
                            bi_prev = prev % nb
                            if bi_prev == nb - 1:
                                sy.wait_ge(s_o2.h, prev // nb + 1)
                            else:
                                sy.wait_ge(s_w2c.h, prev + 1 - (prev // nb))
                        dstm = mb[n2 % 2][:, 0:512 * (f1 - f0)].rearrange(
                            "p (c q) -> p c q", c=f1 - f0)
                        srcm = me.maccd[grp][f0:f1].rearrange("o p q -> p o q")
                        s_m2[n2 % 2].inc(sy.dma_start(out=dstm, in_=srcm), 16)
                        n2 += 1
                for isub in range(4):
                    i128 = 4 * grp + isub
                    if i128 >= 2:
                        sy.wait_ge(s_x2.h, i128 - 1)
                    s_xc[i128 % 2].inc(sy.dma_start(
                        out=x1b[i128 % 2][:, :],
                        in_=me.x1dram[128 * i128:128 * (i128 + 1), :]), 16)
                    sy.wait_ge(s_x2.h, i128 + 1)
                    s_outs.inc(sy.dma_start(
                        out=me.out[128 * i128:128 * (i128 + 1), :],
                        in_=x2t[i128 % 2][:, :]), 16)
            sy.wait_ge(s_outs.h, 16 * 8)

    def build(self, upto="p3"):
        self.declare_dram()
        self.es = contextlib.ExitStack()
        nc = self.nc
        self.sb = lambda n, s, d: self.es.enter_context(nc.sbuf_tensor(n, s, d))
        self.sem = lambda n: Sem(self.es, nc, n)
        with self.es:
            self.phase0_alloc()
            self.esA = self.es.enter_context(contextlib.ExitStack())
            self.sbA = lambda n, s, d: self.esA.enter_context(nc.sbuf_tensor(n, s, d))
            self.psA = lambda n, s, d: self.esA.enter_context(nc.psum_tensor(n, s, d))
            self.block = self.es.enter_context(nc.Block())
            self.phase0()
            self.phase1()
            if upto == "p1":
                self.finish_dev_p1()
                return nc
            self.phase2()
            self.es1.close()
            self.phase3()
            if upto == "p3":
                self.finish_dev_p3()
                return nc
            self.phase4()
            self.esA.close()
            self.phase5()
            if upto == "p5":
                self.finish_dev_p5()
                return nc
            self.es5.close()
            self.phase6()
            self.esa.close()
            self.phase6b()
            self.phase6c()
            self.finish_dbg_all()
        return nc

    def finish_dbg_all(self):
        me, block = self, self.block
        dx1 = self.dbgout("x1", [TOK, D])
        dwf = self.dbgout("wf", [128, 64])
        dm0 = dm1 = None
        if not (dx1 or dwf):
            return
        sh = self.p6_shared

        @block.sync
        def _(sy):
            sy.wait_ge(me.s_outs.h, 16 * 8)
            n = 0
            if dx1 is not None:
                me.sem_x.inc(sy.dma_start(out=dx1[:, :], in_=me.x1dram[:, :]), 16)
            if dwf is not None:
                me.sem_x.inc(sy.dma_start(out=dwf[:, :], in_=sh["wf"][:, :]), 16)

            sy.wait_ge(me.sem_x.h, me.sem_x.v)

    def finish_dev_p5(self):
        me, block = self, self.block
        dx1 = self.dbgout("x1", [TOK, D])

        @block.sync
        def _(sy):
            sy.wait_ge(me.s_x1s.h, 16 * 8)
            if dx1 is not None:
                me.sem_x.inc(sy.dma_start(out=dx1[:, :], in_=me.x1dram[:, :]), 16)
            sy.wait_ge(me.sem_x.h, me.sem_x.v)

    def finish_dev_p1(self):
        me, block = self, self.block
        dq = self.dbgout("qT", [4 * 128, S], BF16)
        dk = self.dbgout("kT", [4 * 128, S], BF16)
        dv = self.dbgout("Vt", [4 * 128, 32 * 129], BF16)

        @block.sync
        def _(sy):
            sy.wait_ge(me.s_ev.h, me.p1_ev_total)
            sy.wait_ge(me.s_tpev.h, me.p1_tpev_total)
            for h in range(4):
                if dq is not None:
                    me.sem_x.inc(sy.dma_start(out=dq[128 * h:128 * (h + 1), :],
                                              in_=me.qT[h][:, :]), 16)
                if dk is not None:
                    me.sem_x.inc(sy.dma_start(out=dk[128 * h:128 * (h + 1), :],
                                              in_=me.kT[h][:, :]), 16)
                if dv is not None:
                    me.sem_x.inc(sy.dma_start(out=dv[128 * h:128 * (h + 1), :],
                                              in_=me.Vt[h][:, :]), 16)

            sy.wait_ge(me.sem_x.h, me.sem_x.v)

    def finish_dev_p3(self):
        me, block = self, self.block
        da = self.dbgout("aoT", [4 * 128, S], BF16)
        db = self.dbgout("bnd", [1, 20])

        @block.sync
        def _(sy):
            sy.wait_ge(me.s_aot.h, 32)
            for h in range(4):
                if da is not None:
                    me.sem_x.inc(sy.dma_start(out=da[128 * h:128 * (h + 1), :],
                                              in_=me.aoT[h][:, :]), 16)
            if db is not None:
                me.sem_x.inc(sy.dma_start(out=db[:, :], in_=me.bnd[:, 0:20]), 16)
            sy.wait_ge(me.sem_x.h, me.sem_x.v)


def build_program(dbg_names=(), upto="p1"):
    b = Builder(dbg_names)
    nc = b.build(upto=upto)
    return nc, b.dbg


def host_inputs(core_id, I):
    c = core_id
    b, hq = c // 4, c % 4
    heads = [4 * hq + h for h in range(4)]
    C = np.ascontiguousarray
    Wqkv = np.asarray(I["L0_attn_Wqkv"], dtype=np.float32)
    bqkv = np.asarray(I["L0_attn_bqkv"], dtype=np.float32)
    wq = np.concatenate([Wqkv[:, 128 * h:128 * (h + 1)] for h in heads], axis=1)
    wk = np.concatenate([Wqkv[:, D + 128 * h:D + 128 * (h + 1)] for h in heads], axis=1)
    wv = np.concatenate([Wqkv[:, 2 * D + 128 * h:2 * D + 128 * (h + 1)] for h in heads], axis=1)
    wqkv = C(np.concatenate([wq, wk, wv], axis=1))
    bq = np.concatenate([bqkv[128 * h:128 * (h + 1)] for h in heads])
    bk = np.concatenate([bqkv[D + 128 * h:D + 128 * (h + 1)] for h in heads])
    bv = np.concatenate([bqkv[2 * D + 128 * h:2 * D + 128 * (h + 1)] for h in heads])
    bqkv_col = C(np.concatenate([bq, bk, bv]).reshape(12, 128).T)

    a1 = np.asarray(I["L0_ffn_a1"], np.float32)
    a3 = np.asarray(I["L0_ffn_a3"], np.float32)
    a2 = np.asarray(I["L0_ffn_a2"], np.float32)
    b1 = np.asarray(I["L0_ffn_b1"], np.float32)
    b3 = np.asarray(I["L0_ffn_b3"], np.float32)
    b2 = np.asarray(I["L0_ffn_b2"], np.float32)

    def aucat(a):
        z = np.zeros((D, 16), np.float32)
        cols = []
        for t in range(8):
            cols.append(a[t].T)
            cols.append(z)
        return C(np.concatenate(cols, axis=1))

    def bpk(bm):
        bT = bm.transpose(0, 2, 1)
        r = bT.reshape(8, 16, NFT, 128).transpose(2, 1, 0, 3).reshape(NFT, 16, 8 * 128)
        return C(r)

    a2c = C(np.concatenate([a2[e].T for e in range(E)], axis=1))
    b2c = C(b2.transpose(2, 0, 1).reshape(16, E * D))

    identf = np.eye(128, dtype=np.float32)
    onehot = np.zeros((128, 8 * 128), np.float32)
    for e in range(E):
        onehot[e, 128 * e:128 * (e + 1)] = 1.0
    e1row = np.zeros((128, 128), np.float32); e1row[0, :] = 1.0
    onecol = np.zeros((128, 128), np.float32); onecol[:, 0] = 1.0
    slneg = np.zeros((1, 4 * S), np.float32)
    for h in range(4):
        slneg[0, S * h:S * (h + 1)] = -SLOPES[heads[h]] * np.arange(S, dtype=np.float32)
    sjh = np.zeros((128, 4 * 32), np.float32)
    for h in range(4):
        sl = SLOPES[heads[h]]
        for jt in range(32):
            sjh[:, 32 * h + jt] = sl * (128 * jt + np.arange(128))
    mask = np.zeros((128, 4 * 512), np.float32)
    for k in range(4):
        jj = np.arange(128)[:, None]
        ii = np.arange(512)[None, :]
        mask[:, 512 * k:512 * (k + 1)] = np.where(jj + 128 * k <= ii, 0.0, NEGBIG)
    svec = np.full((32, 1), LORA, np.float32)
    gidx = np.zeros((128, NCH), np.uint32)
    for ch in range(NCH):
        row = 512 * (ch % 4) + 128 * (ch // 4) + np.arange(128)
        gidx[:, ch] = 4 * row + hq

    x = np.asarray(I["L0_attn_x"], np.float32)
    return {
        "xb": C(x[b]),
        "xsl": C(x[b][1024 * hq:1024 * (hq + 1)]),
        "wqkv": wqkv, "bqkv_col": bqkv_col,
        "wo": C(np.asarray(I["L0_attn_Wo"], np.float32)),
        "bo_row": C(np.asarray(I["L0_attn_bo"], np.float32).reshape(1, D)),
        "gwT": C(np.asarray(I["L0_ffn_gate_w"], np.float32).T),
        "w1T": C(np.asarray(I["L0_ffn_w1"], np.float32).T),
        "w3T": C(np.asarray(I["L0_ffn_w3"], np.float32).T),
        "w2T": C(np.asarray(I["L0_ffn_w2"], np.float32).T),
        "au1": aucat(a1), "au3": aucat(a3),
        "b1pk": bpk(b1), "b3pk": bpk(b3),
        "a2cat": a2c, "b2cat": b2c,
        "identf": identf, "onehot": onehot,
        "e1row_d": e1row, "onecol_d": onecol,
        "slneg_d": slneg, "sjh_d": sjh, "mask_d": mask, "svec_d": svec,
        "gidx_d": gidx,
    }


def kernel(**inputs):
    nc, _ = build_program(upto="all")
    in_maps = [host_inputs(c, inputs) for c in range(8)]
    res = run_bass_kernel_spmd(nc, in_maps, core_ids=list(range(8)))
    y = np.zeros((B, S, D), np.float32)
    for c in range(8):
        b, hq = c // 4, c % 4
        y[b, 1024 * hq:1024 * (hq + 1), :] = res.results[c]["out"]
    return y


# version 11 (final)
# speedup vs baseline: 1.2328x; 1.0512x over previous
import math
import contextlib
import numpy as np
import ml_dtypes
import concourse.bass as bass
import concourse.mybir as mybir
from concourse.bass import IndirectOffsetOnAxis
from concourse.bass_utils import run_bass_kernel_spmd

F32 = mybir.dt.float32
BF16 = mybir.dt.bfloat16
U32 = mybir.dt.uint32
AF = mybir.ActivationFunctionType
ALU = mybir.AluOpType
AX = mybir.AxisListType
bf16 = ml_dtypes.bfloat16

B, S, D = 2, 4096, 2048
H, HD = 16, 128
FF, E, R = 5632, 8, 16
TOK, IBS, NIB = 1024, 512, 2
NFT = FF // 128
NCH = D // 128
NPT = S // 128
SCALE = 1.0 / math.sqrt(HD)
LORA = 2.0
SLOPES = [2.0 ** (-k / 2.0) for k in range(H)]
NEGBIG = -1.0e30
SIM_SAFE = [False]


class Sem:
    def __init__(self, es, nc, name):
        self.h = es.enter_context(nc.semaphore(name))
        self.v = 0

    def inc(self, instr, n=1):
        instr.then_inc(self.h, n)
        self.v += n
        return instr


class Builder:
    def __init__(self, dbg_names=()):
        self.nc = bass.Bass("TRN2", target_bir_lowering=False, dynamic_dma_scratch_size=16384)
        self.dbg_names = set(dbg_names)
        self.dbg = {}

    def ap(self, *a, **k):
        return self.nc.dram_tensor(*a, **k).ap()

    def dbgout(self, name, shape, dt=F32):
        if name in self.dbg_names:
            self.dbg[name] = self.ap("dbg_" + name, shape, dt, kind="ExternalOutput")
        return self.dbg.get(name)

    def declare_dram(self):
        ap = self.ap
        self.xb = ap("xb", [S, D], F32, kind="ExternalInput")
        self.xsl = ap("xsl", [TOK, D], F32, kind="ExternalInput")
        self.wqkv = ap("wqkv", [D, 1536], F32, kind="ExternalInput")
        self.bqkv_col = ap("bqkv_col", [128, 12], F32, kind="ExternalInput")
        self.wo = ap("wo", [D, D], F32, kind="ExternalInput")
        self.bo_row = ap("bo_row", [1, D], F32, kind="ExternalInput")
        self.gwT = ap("gwT", [D, E], F32, kind="ExternalInput")
        self.w1T = ap("w1T", [D, FF], F32, kind="ExternalInput")
        self.w3T = ap("w3T", [D, FF], F32, kind="ExternalInput")
        self.w2T = ap("w2T", [FF, D], F32, kind="ExternalInput")
        self.au1 = ap("au1", [D, 256], F32, kind="ExternalInput")
        self.au3 = ap("au3", [D, 256], F32, kind="ExternalInput")
        self.b1pk = ap("b1pk", [NFT, 16, 8 * 128], F32, kind="ExternalInput")
        self.b3pk = ap("b3pk", [NFT, 16, 8 * 128], F32, kind="ExternalInput")
        self.a2cat = ap("a2cat", [FF, 128], F32, kind="ExternalInput")
        self.b2cat = ap("b2cat", [16, E * D], F32, kind="ExternalInput")
        self.identf = ap("identf", [128, 128], F32, kind="ExternalInput")
        self.onehot = ap("onehot", [128, 8 * 128], F32, kind="ExternalInput")
        self.e1row_d = ap("e1row_d", [128, 128], F32, kind="ExternalInput")
        self.onecol_d = ap("onecol_d", [128, 128], F32, kind="ExternalInput")
        self.slneg_d = ap("slneg_d", [1, 4 * S], F32, kind="ExternalInput")
        self.sjh_d = ap("sjh_d", [128, 4 * 32], F32, kind="ExternalInput")
        self.mask_d = ap("mask_d", [128, 4 * 512], F32, kind="ExternalInput")
        self.svec_d = ap("svec_d", [32, 1], F32, kind="ExternalInput")
        self.gidx_d = ap("gidx_d", [128, NCH], U32, kind="ExternalInput")
        self.out = ap("out", [TOK, D], F32, kind="ExternalOutput")
        self.x1dram = ap("x1dram", [TOK, D], F32, kind="Internal")
        self.maccd = [ap(f"maccd{ib}", [NFT, 128, 512], BF16, kind="Internal")
                      for ib in range(2)]
        self.ccin = self.nc.dram_tensor("ccin", [2048, 1024], BF16, kind="Internal")
        self.ccout = self.nc.dram_tensor("ccout", [8192, 1024], BF16, kind="Internal")

    def phase0_alloc(self):
        sb, sem = self.sb, self.sem
        self.idb = sb("idb", [128, 128], BF16)
        self.idf = sb("idf", [128, 128], F32)
        self.e1b = sb("e1b", [128, 128], BF16)
        self.oc1b = sb("oc1b", [128, 128], BF16)
        self.svec = sb("svec", [32, 1], F32)
        self.bqcol = sb("bqcol", [128, 12], F32)
        self.bqcs = sb("bqcs", [128, 4], F32)
        self.epsv = sb("epsv", [128, 1], F32)
        self.onehots = sb("onehots", [128, 8 * 128], BF16)
        self.stats = sb("stats", [128, 24], F32)
        self.gidx_s = sb("gidx_s", [128, NCH], U32)
        self.mv = [sb(f"mv{i}", [128, 2], F32) for i in range(2)]
        self.rstd = [sb(f"rstd{i}", [128, 1], F32) for i in range(2)]
        self.nbias = [sb(f"nbias{i}", [128, 1], F32) for i in range(2)]

        self.sem_x = sem("sem_x")
        self.sem_w = sem("sem_w")
        self.s_c0 = sem("s_c0")
        self.s_c1 = sem("s_c1")

    def phase0(self):
        block = self.block
        me = self

        @block.gpsimd
        def _(g):
            me.sem_w.inc(g.dma_start(out=me.idb[:, :], in_=me.identf[:, :]), 16)
            me.sem_w.inc(g.dma_start(out=me.e1b[:, :], in_=me.e1row_d[:, :]), 16)
            me.sem_w.inc(g.dma_start(out=me.oc1b[:, :], in_=me.onecol_d[:, :]), 16)
            me.sem_w.inc(g.dma_start(out=me.onehots[:, :], in_=me.onehot[:, :]), 16)

        @block.sync
        def _(sy):
            me.s_c0.inc(sy.dma_start(out=me.idf[:, :], in_=me.identf[:, :]), 16)
            me.s_c0.inc(sy.dma_start(out=me.svec[:, :], in_=me.svec_d[:, :]), 16)
            me.s_c0.inc(sy.dma_start(out=me.bqcol[:, :], in_=me.bqkv_col[:, :]), 16)
            me.s_c0.inc(sy.dma_start(out=me.gidx_s[:, :], in_=me.gidx_d[:, :]), 16)

        @block.vector
        def _(v):
            v.memset(me.epsv[:, :], 1e-5)
            v.wait_ge(me.s_c0.h, 64)
            me.s_c1.inc(v.tensor_scalar(me.bqcs[:, :], me.bqcol[:, 0:4], SCALE,
                                        None, op0=ALU.mult))

    def phase1(self):
        nc, block, me = self.nc, self.block, self
        sbA = self.sbA

        self.qT = [sbA(f"qT{h}", [128, S], BF16) for h in range(4)]
        self.kT = [sbA(f"kT{h}", [128, S], BF16) for h in range(4)]
        self.Vt = [sbA(f"Vt{h}", [128, 32 * 129], BF16) for h in range(4)]
        self.cmx8 = sbA("cmx8", [1, 8], F32)
        self.bnd = sbA("bnd", [1, 24], F32)
        es1 = self.esA.enter_context(contextlib.ExitStack())
        sb1 = lambda n, s, d: es1.enter_context(nc.sbuf_tensor(n, s, d))
        ps1 = lambda n, s, d: es1.enter_context(nc.psum_tensor(n, s, d))
        self.es1 = es1
        xbuf = [sb1(f"xbuf{i}", [128, D], F32) for i in range(2)]
        lnx = [sb1(f"lnx{i}", [128, D], BF16) for i in range(2)]
        lnxT = [sb1(f"lnxT{i}", [128, NCH * 512], BF16) for i in range(2)]
        wqkv_sb = sb1("wqkv_sb", [128, NCH * 1536], BF16)
        vtmp = [sb1(f"vtmp{i}", [128, 512], BF16) for i in range(2)]
        self.sqbuf = lnx[0]
        pf = [self.psA(f"pf{i}", [128, 512], F32) for i in range(2)]
        pbb = [self.psA(f"pbb{i}", [128, 1024], BF16) for i in range(2)]
        self.p1_pf, self.p1_pbb = pf, pbb

        sem = self.sem
        s_stat = sem("s_stat"); s_rstd = sem("s_rstd"); s_nb = sem("s_nb")
        self.s_stat, self.s_rstd, self.s_nb = s_stat, s_rstd, s_nb
        s_lnx = sem("s_lnx")
        s_tp = sem("s_tp"); s_tpev = sem("s_tpev")
        s_qk = sem("s_qk"); s_ev = sem("s_ev")
        self.s_qk, self.s_ev, self.s_tp, self.s_tpev = s_qk, s_ev, s_tp, s_tpev

        @block.gpsimd
        def _(g):
            src = me.wqkv[:, :].rearrange("(c p) q -> p c q", p=128)
            dst = wqkv_sb[:, :].rearrange("p (c q) -> p c q", c=NCH)
            me.sem_w.inc(g.dma_start(out=dst, in_=src), 16)
        w_qkv = self.sem_w.v

        @block.vector
        def _(v):
            for h in range(4):
                vv = me.Vt[h][:, :].rearrange("p (t c) -> p t c", c=129)
                v.memset(vv[:, :, 128:129], 1.0)

        x_base = self.sem_x.v

        @block.sync
        def _(sy):
            for pt in range(NPT):
                if pt >= 2:
                    sy.wait_ge(s_lnx.h, pt - 1)
                me.sem_x.inc(sy.dma_start(
                    out=xbuf[pt % 2][:, :],
                    in_=me.xb[128 * pt:128 * (pt + 1), :]), 16)

        lnxT3 = [lnxT[i][:, :].rearrange("p (c t) -> p c t", c=NCH) for i in range(2)]
        wq3 = wqkv_sb[:, :].rearrange("p (c q) -> p c q", c=NCH)
        pvt = [ps1(f"pvt{i}", [128, 1024], BF16) for i in range(2)]
        s_vtev = sem("s_vtev")
        self.s_vtev = s_vtev
        tp_pos = {}
        tp_n = [0]

        @block.tensor
        def _(t):
            def tile_tp(pt):
                t.wait_ge(s_lnx.h, pt + 1)
                for half in range(2):
                    ep = 2 * pt + half
                    if ep >= 2:
                        t.wait_ge(s_tpev.h, ep - 1)
                    bank = pbb[ep % 2]
                    for c8 in range(8):
                        ch = 8 * half + c8
                        ins = t.transpose(bank[:, 128 * c8:128 * (c8 + 1)],
                                          lnx[pt % 2][:, 128 * ch:128 * (ch + 1)],
                                          me.idb[:, :])
                        if c8 == 7:
                            s_tp.inc(ins)
                            tp_n[0] += 1
                            tp_pos[('t', pt, half)] = tp_n[0]

            def vtp(a, h):
                m = 4 * a + h
                t.wait_ge(s_ev.h, 12 * a + 3 * h + 3)
                if m >= 2:
                    t.wait_ge(s_vtev.h, m - 1)
                bank = pvt[m % 2]
                for ts4 in range(4):
                    ins = t.transpose(bank[:, 128 * ts4:128 * (ts4 + 1)],
                                      vtmp[h % 2][:, 128 * ts4:128 * (ts4 + 1)],
                                      me.idb[:, :])
                    if ts4 == 3:
                        s_tp.inc(ins)
                        tp_n[0] += 1
                        tp_pos[('v', m)] = tp_n[0]

            t.wait_ge(me.sem_w.h, w_qkv)
            for pt in range(4):
                tile_tp(pt)
            for a in range(8):
                t.wait_ge(s_tpev.h, 8 * (a + 1))
                for g in range(12):
                    h, P = g // 3, g % 3
                    grp = 12 * a + g
                    if grp >= 2:
                        t.wait_ge(s_ev.h, grp - 1)
                    bank = pf[grp % 2]
                    col0 = 512 * P + 128 * h
                    for ch in range(NCH):
                        ins = t.matmul(bank[:, :], wq3[:, ch, col0:col0 + 128],
                                       lnxT3[a % 2][:, ch, :],
                                       start=(ch == 0), stop=(ch == NCH - 1))
                        if ch == NCH - 1:
                            s_qk.inc(ins)
                    if g in (2, 5, 8, 11):
                        vtp(a, g // 3)
                    if a <= 6 and g in (3, 5, 7, 10):
                        k = {3: 0, 5: 1, 7: 2, 10: 3}[g]
                        tile_tp(4 * (a + 1) + k)

        @block.scalar
        def _(s):
            def ln_evac(pt):
                a, j = pt // 4, pt % 4
                for half in range(2):
                    ep = 2 * pt + half
                    s.wait_ge(s_tp.h, tp_pos[('t', pt, half)])
                    if half == 0 and a >= 2:
                        s.wait_ge(s_qk.h, 12 * (a - 1))
                    bank = pbb[ep % 2]
                    dst = lnxT3[a % 2][:, 8 * half:8 * half + 8, 128 * j:128 * (j + 1)]
                    srcb = bank[:, 0:1024].rearrange("p (c t) -> p c t", c=8)
                    s_tpev.inc(s.activation(dst, srcb, AF.Identity,
                                            bias=0.0, scale=1.0))

            for pt in range(NPT):
                if pt >= 1:
                    ln_evac(pt - 1)
                s.wait_ge(s_stat.h, pt + 1)
                s_rstd.inc(s.activation(
                    me.rstd[pt % 2][:, :], me.mv[pt % 2][:, 1:2],
                    AF.Sqrt, bias=me.epsv[:, 0:1], scale=1.0))
                s.wait_ge(s_nb.h, pt + 1)
                if pt >= 2:
                    s.wait_ge(s_tp.h, tp_pos[('t', pt - 2, 1)])
                s_lnx.inc(s.activation(
                    lnx[pt % 2][:, :], xbuf[pt % 2][:, :], AF.Identity,
                    bias=me.nbias[pt % 2][:, 0:1], scale=me.rstd[pt % 2][:, 0:1]))
            ln_evac(NPT - 1)

        @block.vector
        def _(v):
            def qkv_evacs(b, hj):
                m = 4 * b + hj
                for P in range(3):
                    grp = 12 * b + 3 * hj + P
                    v.wait_ge(s_qk.h, grp + 1)
                    bank = pf[grp % 2]
                    if P == 0:
                        ins = v.tensor_scalar(
                            me.qT[hj][:, 512 * b:512 * (b + 1)], bank[:, :],
                            me.bqcol[:, hj:hj + 1], SCALE,
                            op0=ALU.add, op1=ALU.mult)
                    elif P == 1:
                        ins = v.tensor_scalar(
                            me.kT[hj][:, 512 * b:512 * (b + 1)], bank[:, :],
                            me.bqcol[:, 4 + hj:5 + hj], None, op0=ALU.add)
                    else:
                        if m >= 2:
                            v.wait_ge(s_tp.h, tp_pos[('v', m - 2)])
                        ins = v.tensor_scalar(
                            vtmp[hj % 2][:, :], bank[:, :],
                            me.bqcol[:, 8 + hj:9 + hj], None, op0=ALU.add)
                    s_ev.inc(ins)
                v.wait_ge(s_tp.h, tp_pos[('v', m)])
                vt3 = me.Vt[hj][:, :].rearrange("p (t c) -> p t c", c=129)
                dst = vt3[:, 4 * b:4 * b + 4, 0:128]
                srcb = pvt[m % 2][:, 0:512].rearrange("p (t c) -> p t c", c=128)
                s_vtev.inc(v.tensor_copy(dst, srcb))

            for pt in range(NPT):
                a, j = pt // 4, pt % 4
                v.wait_ge(me.sem_x.h, x_base + 16 * min(pt + 2, NPT))
                if pt >= 2:
                    v.wait_ge(s_rstd.h, pt - 1)
                xt = xbuf[pt % 2]
                for q in range(4):
                    v.bn_stats(me.stats[:, 6 * q:6 * q + 6],
                               xt[:, 512 * q:512 * (q + 1)])
                v.drain()
                s_stat.inc(v.bn_aggr(me.mv[pt % 2][:, :], me.stats[:, :]))
                v.wait_ge(s_rstd.h, pt + 1)
                v.reciprocal(me.rstd[pt % 2][:, :], me.rstd[pt % 2][:, :])
                v.drain()
                s_nb.inc(v.tensor_scalar(
                    me.nbias[pt % 2][:, :], me.rstd[pt % 2][:, :],
                    me.mv[pt % 2][:, 0:1], -1.0, op0=ALU.mult, op1=ALU.mult))
                if pt >= 4:
                    qkv_evacs(a - 1, j)
            for j in range(4):
                qkv_evacs(7, j)

        self.p1_qk_total = s_qk.v
        self.p1_ev_total = s_ev.v
        self.p1_tp_total = s_tp.v
        self.p1_tpev_total = s_tpev.v
        self.p1_vtev_total = s_vtev.v
        self.s_lnx = s_lnx

    def phase2(self):
        block, me = self.block, self
        sem = self.sem
        s_sq = sem("s_sq")
        s_bsum = sem("s_bsum")
        s_red = sem("s_red")
        s_bnd = sem("s_bnd")
        s_sqr = sem("s_sqr")
        s_negb = sem("s_negb")
        self.s_negb = s_negb
        pf = self.p1_pf

        @block.vector
        def _(v):
            v.wait_ge(me.s_ev.h, me.p1_ev_total)
            for strip in range(8):
                h, which = strip // 2, strip % 2
                srcq = me.qT[h] if which == 0 else me.kT[h]
                for hh in range(2):
                    if strip + hh > 0:
                        v.wait_ge(s_bsum.h, 8 * strip + 4 * hh)
                    s_sq.inc(v.tensor_tensor(
                        me.sqbuf[:, :], srcq[:, 2048 * hh:2048 * (hh + 1)],
                        srcq[:, 2048 * hh:2048 * (hh + 1)], op=ALU.mult))
                    for c in range(4 * hh, 4 * hh + 4):
                        v.wait_ge(s_bsum.h, 8 * strip + c + 1)
                        s_red.inc(v.reduce_max(me.cmx8[:, c:c + 1],
                                               pf[c % 2][0:1, :], axis=AX.X))
                v.drain()
                s_bnd.inc(v.reduce_max(me.bnd[:, strip:strip + 1],
                                       me.cmx8[:, :], axis=AX.X))
                v.drain()

        @block.tensor
        def _(t):
            for strip in range(8):
                for c in range(8):
                    hh = c // 4
                    t.wait_ge(s_sq.h, 2 * strip + hh + 1)
                    n = 8 * strip + c
                    if n >= 2:
                        t.wait_ge(s_red.h, n - 1)
                    s_bsum.inc(t.matmul(pf[c % 2][:, :], me.oc1b[:, :],
                                        me.sqbuf[:, 512 * (c % 4):512 * (c % 4 + 1)],
                                        start=True, stop=True))

        @block.scalar
        def _(s):
            for h in range(4):
                s.wait_ge(s_bnd.h, 2 * h + 2)
                s_sqr.inc(s.activation(me.bnd[:, 8 + 2 * h:9 + 2 * h],
                                       me.bnd[:, 2 * h:2 * h + 1],
                                       AF.Sqrt, bias=0.0, scale=1.0))
                s_sqr.inc(s.activation(me.bnd[:, 9 + 2 * h:10 + 2 * h],
                                       me.bnd[:, 2 * h + 1:2 * h + 2],
                                       AF.Sqrt, bias=0.0, scale=1.0))

        @block.vector
        def _(v):
            for h in range(4):
                v.wait_ge(s_sqr.h, 2 * h + 2)
                s_negb.inc(v.tensor_scalar(
                    me.bnd[:, 16 + h:17 + h], me.bnd[:, 8 + 2 * h:9 + 2 * h],
                    me.bnd[:, 9 + 2 * h:10 + 2 * h], -1.0,
                    op0=ALU.mult, op1=ALU.mult))

    def phase3(self):
        nc, block, me = self.nc, self.block, self
        sbA = self.sbA
        es3 = self.esA.enter_context(contextlib.ExitStack())
        ps3 = lambda n, s, d: es3.enter_context(nc.psum_tensor(n, s, d))
        self.slbuf = [sbA(f"slbuf{i}", [1, 512], F32) for i in range(2)]
        self.sjs = sbA("sjs", [128, 4 * 32], F32)
        self.masks = sbA("masks", [128, 4 * 512], BF16)
        self.cibuf = [sbA(f"cibuf{i}", [128, 512], BF16) for i in range(2)]
        self.strip = sbA("strip", [128, 32 * 512], BF16)
        self.aoT = [sbA(f"aoT{h}", [128, S], BF16) for h in range(4)]
        ao_sb = sbA("ao_sb", [128, 512], BF16)
        rcp = sbA("rcp", [128, 4], F32)
        pav = [ps3(f"pav{i}", [128, 512], F32) for i in range(4)]
        pf = self.p1_pf
        pbb = self.p1_pbb

        sem = self.sem
        s_sc = sem("s_sc"); s_exp = sem("s_exp"); s_cib = sem("s_cib")
        s_av = sem("s_av"); s_norm = sem("s_norm")
        s_aotp = sem("s_aotp"); s_aot = sem("s_aot")
        self.s_aot = s_aot

        blocks = [(h, ib) for h in range(4) for ib in range(8)]
        jmax = [4 * ib + 4 for (h, ib) in blocks]
        cum = [0]
        for n in range(len(blocks)):
            cum.append(cum[-1] + jmax[n])
        sc0, exp0 = s_sc.v, s_exp.v
        s_sl = [sem("s_sl0"), sem("s_sl1")]

        qk_all, negb_all = me.s_qk.v, me.s_negb.v

        @block.gpsimd
        def _(g):
            g.wait_ge(me.s_qk.h, qk_all)
            g.wait_ge(me.s_negb.h, negb_all)
            me.sem_w.inc(g.dma_start(out=me.masks[:, :], in_=me.mask_d[:, :]), 16)
        w_mask = me.sem_w.v

        @block.sync
        def _(sy):
            sy.wait_ge(me.s_qk.h, qk_all)
            sy.wait_ge(me.s_negb.h, negb_all)
            me.sem_x.inc(sy.dma_start(out=me.sjs[:, :], in_=me.sjh_d[:, :]), 16)
            for n, (h, ib) in enumerate(blocks):
                if n >= 2:
                    sy.wait_ge(s_cib.h, n - 1)
                s_sl[n % 2].inc(sy.dma_start(
                    out=me.slbuf[n % 2][:, :],
                    in_=me.slneg_d[0:1, S * h + 512 * ib:S * h + 512 * (ib + 1)]), 16)
        sjs_mark = me.sem_x.v

        @block.tensor
        def _(t):
            t.wait_ge(me.s_tpev.h, me.p1_tpev_total)
            t.wait_ge(me.s_vtev.h, me.p1_vtev_total)
            t.wait_ge(me.s_c1.h, 2)
            t.wait_ge(me.sem_w.h, w_mask)
            for n, (h, ib) in enumerate(blocks):
                jm = jmax[n]
                t.wait_ge(s_cib.h, n + 1)
                if n >= 1:
                    t.wait_ge(s_norm.h, n)
                for jt in range(jm):
                    t.wait_ge(s_exp.h, exp0 + cum[n] + max(0, jt - 1))
                    bank = pf[jt % 2]
                    partial = jt >= 4 * ib
                    t.matmul(bank[:, :], me.kT[h][:, 128 * jt:128 * (jt + 1)],
                             me.qT[h][:, 512 * ib:512 * (ib + 1)],
                             start=True, stop=False)
                    ins = t.matmul(bank[:, :], me.e1b[:, :],
                                   me.cibuf[n % 2][:, :],
                                   start=False, stop=not partial)
                    if partial:
                        k = jt - 4 * ib
                        ins = t.matmul(bank[:, :], me.idb[:, :],
                                       me.masks[:, 512 * k:512 * (k + 1)],
                                       start=False, stop=True)
                    s_sc.inc(ins)
                    if jt >= 1:
                        t.wait_ge(s_exp.h, exp0 + cum[n] + jt)
                        for isub in range(4):
                            ins = t.matmul(
                                pav[isub][:, 0:129],
                                me.strip[:, 512 * (jt - 1) + 128 * isub:
                                         512 * (jt - 1) + 128 * (isub + 1)],
                                me.Vt[h][:, 129 * (jt - 1):129 * jt],
                                start=(jt - 1 == 0), stop=False)
                t.wait_ge(s_exp.h, exp0 + cum[n] + jm)
                for isub in range(4):
                    ins = t.matmul(
                        pav[isub][:, 0:129],
                        me.strip[:, 512 * (jm - 1) + 128 * isub:
                                 512 * (jm - 1) + 128 * (isub + 1)],
                        me.Vt[h][:, 129 * (jm - 1):129 * jm],
                        start=(jm - 1 == 0), stop=True)
                    if isub == 3:
                        s_av.inc(ins)
                t.wait_ge(s_norm.h, n + 1)
                if n >= 2:
                    t.wait_ge(s_aot.h, n - 1)
                bank = pbb[n % 2]
                for isub in range(4):
                    ins = t.transpose(bank[:, 128 * isub:128 * (isub + 1)],
                                      ao_sb[:, 128 * isub:128 * (isub + 1)],
                                      me.idb[:, :])
                    if isub == 3:
                        s_aotp.inc(ins)

        @block.scalar
        def _(s):
            s.wait_ge(me.s_c1.h, 2)
            s.wait_ge(me.sem_x.h, sjs_mark)
            for n, (h, ib) in enumerate(blocks):
                s.wait_ge(s_av.h, n)
                if n == 0:
                    s.wait_ge(me.s_negb.h, 4)
                s.wait_ge(s_sl[n % 2].h, 16 * (n // 2 + 1))
                s_cib.inc(s.activation(
                    me.cibuf[n % 2][0:1, :], me.slbuf[n % 2][:, :],
                    AF.Identity, bias=me.bnd[:, 16 + h:17 + h], scale=1.0))
                for jt in range(jmax[n]):
                    s.wait_ge(s_sc.h, sc0 + cum[n] + jt + 1)
                    s_exp.inc(s.activation(
                        me.strip[:, 512 * jt:512 * (jt + 1)], pf[jt % 2][:, :],
                        AF.Exp, bias=me.sjs[:, 32 * h + jt:32 * h + jt + 1],
                        scale=1.0))

        @block.vector
        def _(v):
            v.wait_ge(me.s_qk.h, qk_all)
            v.memset(me.cibuf[0][:, :], 0.0)
            me.s_c1.inc(v.memset(me.cibuf[1][:, :], 0.0))
            for n, (h, ib) in enumerate(blocks):
                v.wait_ge(s_av.h, n + 1)
                for isub in range(4):
                    v.reciprocal(rcp[:, isub:isub + 1], pav[isub][:, 128:129])
                v.drain()
                if n >= 1:
                    v.wait_ge(s_aotp.h, n)
                for isub in range(4):
                    ins = v.tensor_scalar(
                        ao_sb[:, 128 * isub:128 * (isub + 1)],
                        pav[isub][:, 0:128], rcp[:, isub:isub + 1], None,
                        op0=ALU.mult)
                    if isub == 3:
                        s_norm.inc(ins)
                v.wait_ge(s_aotp.h, n + 1)
                bank = pbb[n % 2]
                s_aot.inc(v.tensor_copy(
                    me.aoT[h][:, 512 * ib:512 * (ib + 1)], bank[:, 0:512]))

    def phase4(self):
        nc, block, me = self.nc, self.block, self
        sem = self.sem
        s_cx = [sem(f"s_cx{h}") for h in range(4)]
        s_cc = sem("s_cc")
        self.s_at = [sem(f"s_at{k}") for k in range(4)]

        self.s_cx = s_cx
        self.s_cc = s_cc

    def phase5(self):
        nc, block, me = self.nc, self.block, self
        es5 = self.es.enter_context(contextlib.ExitStack())
        self.es5 = es5
        sb5 = lambda n, s, d: es5.enter_context(nc.sbuf_tensor(n, s, d))
        ps5 = lambda n, s, d: es5.enter_context(nc.psum_tensor(n, s, d))
        sem = self.sem
        atf = sb5("atf", [128, NCH * 1024], BF16)
        wo_sb = sb5("wo_sb", [128, NCH * 2048], BF16)
        bob = sb5("bob", [128, D], BF16)
        xb5 = [sb5(f"xb5_{i}", [128, D], F32) for i in range(2)]
        y1 = sb5("y1", [128, D], F32)
        x1t = [sb5(f"x1t{i}", [128, D], F32) for i in range(2)]
        pw = [ps5(f"pw{i}", [128, 512], F32) for i in range(4)]

        s_wo = sem("s_wo")
        s_qk5 = sem("s_qk5")
        s_st5, s_rs5, s_nb5 = me.s_stat, me.s_rstd, me.s_nb
        st5_0, rs5_0, nb5_0 = s_st5.v, s_rs5.v, s_nb5.v
        s_y1 = sem("s_y1")
        s_x1 = sem("s_x1")
        s_x1s = sem("s_x1s")
        s_xb5 = [sem("s_xb5a"), sem("s_xb5b")]
        s_bz = sem("s_bz")
        self.s_x1s = s_x1s

        atf3 = atf[:, :].rearrange("p (c t) -> p c t", c=NCH)
        wo3 = wo_sb[:, :].rearrange("p (c t) -> p c t", c=NCH)

        @block.gpsimd
        def _(g):
            for h in range(4):
                for c2 in range(4):
                    q = 4 * h + c2
                    g.wait_ge(me.s_aot.h, 8 * h + 2 * (c2 + 1))
                    me.s_cx[h].inc(g.dma_start(
                        out=me.ccin.ap()[128 * q:128 * (q + 1), :],
                        in_=me.aoT[h][:, 1024 * c2:1024 * (c2 + 1)]), 16)
                    g.wait_ge(me.s_cx[h].h, 16 * (c2 + 1))
                    me.s_cc.inc(g.collective_compute(
                        "AllGather", ALU.bypass,
                        replica_groups=[[0, 1, 2, 3], [4, 5, 6, 7]],
                        ins=[me.ccin.ap()[128 * q:128 * (q + 1), :].opt()],
                        outs=[me.ccout.ap()[512 * q:512 * (q + 1), :].opt()]), 1)
            g.wait_ge(s_bz.h, 1)
            s_wo.inc(g.dma_start(out=bob[0:1, :], in_=me.bo_row[:, :]), 16)
            view = me.ccout.ap()
            for k in range(4):
                srcw = me.wo[:, 512 * k:512 * (k + 1)].rearrange(
                    "(c p) q -> p c q", p=128)
                dstw = wo3[:, :, 512 * k:512 * (k + 1)]
                me.s_at[k].inc(g.dma_start(out=dstw, in_=srcw), 16)
                g.wait_ge(me.s_cc.h, 4 * (k + 1))
                for r in range(4):
                    ch = 4 * r + k
                    me.s_at[k].inc(g.indirect_dma_start(
                        out=atf3[:, ch, :], out_offset=None, in_=view,
                        in_offset=IndirectOffsetOnAxis(ap=me.gidx_s[:, ch:ch + 1], axis=0)), 16)

        @block.sync
        def _(sy):
            sy.wait_ge(me.s_aot.h, 32)
            for i in range(8):
                if i >= 2:
                    sy.wait_ge(s_x1.h, i - 1)
                s_xb5[i % 2].inc(sy.dma_start(
                    out=xb5[i % 2][:, :],
                    in_=me.xsl[128 * i:128 * (i + 1), :]), 16)
                sy.wait_ge(s_x1.h, i + 1)
                s_x1s.inc(sy.dma_start(
                    out=me.x1dram[128 * i:128 * (i + 1), :],
                    in_=x1t[i % 2][:, :]), 16)

        @block.tensor
        def _(t):
            t.wait_ge(s_wo.h, 16)
            for i in range(8):
                if i >= 1:
                    t.wait_ge(s_y1.h, 4 * i)
                    t.wait_ge(s_st5.h, st5_0 + i)
                for oc in range(4):
                    for k in range(4):
                        if i == 0 and oc == 0:
                            t.wait_ge(me.s_at[k].h, 80)
                        for r in range(4):
                            ch = 4 * r + k
                            t.matmul(pw[oc][:, :], atf3[:, ch, 128 * i:128 * (i + 1)],
                                     wo3[:, ch, 512 * oc:512 * (oc + 1)],
                                     start=(k == 0 and r == 0), stop=False)
                    ins = t.matmul(pw[oc][:, :], me.e1b[:, :],
                                   bob[:, 512 * oc:512 * (oc + 1)],
                                   start=False, stop=True)
                    s_qk5.inc(ins)

        @block.vector
        def _(v):
            s_bz.inc(v.memset(bob[:, :], 0.0))
            for i in range(8):
                v.wait_ge(s_qk5.h, 4 * i + 4)
                for oc in range(4):
                    v.bn_stats(me.stats[:, 6 * oc:6 * oc + 6], pw[oc][:, :])
                v.drain()
                s_st5.inc(v.bn_aggr(me.mv[i % 2][:, :], me.stats[:, :]))
                v.wait_ge(s_rs5.h, rs5_0 + i + 1)
                v.reciprocal(me.rstd[i % 2][:, :], me.rstd[i % 2][:, :])
                v.drain()
                s_nb5.inc(v.tensor_scalar(
                    me.nbias[i % 2][:, :], me.rstd[i % 2][:, :],
                    me.mv[i % 2][:, 0:1], -1.0, op0=ALU.mult, op1=ALU.mult))
                v.wait_ge(s_y1.h, 4 * i + 4)
                v.wait_ge(s_xb5[i % 2].h, 16 * (i // 2 + 1))
                if i >= 2:
                    v.wait_ge(s_x1s.h, 16 * i)
                s_x1.inc(v.tensor_tensor(x1t[i % 2][:, :], y1[:, :],
                                         xb5[i % 2][:, :], op=ALU.add))

        @block.scalar
        def _(s):
            for i in range(8):
                s.wait_ge(s_st5.h, st5_0 + i + 1)
                s_rs5.inc(s.activation(
                    me.rstd[i % 2][:, :], me.mv[i % 2][:, 1:2],
                    AF.Sqrt, bias=me.epsv[:, 0:1], scale=1.0))
                s.wait_ge(s_nb5.h, nb5_0 + i + 1)
                for oc in range(4):
                    s_y1.inc(s.activation(
                        y1[:, 512 * oc:512 * (oc + 1)], pw[oc][:, :], AF.Identity,
                        bias=me.nbias[i % 2][:, 0:1], scale=me.rstd[i % 2][:, 0:1]))

    def phase6(self):
        nc, block, me = self.nc, self.block, self
        es6 = self.es.enter_context(contextlib.ExitStack())
        sb6 = lambda n, s, d: es6.enter_context(nc.sbuf_tensor(n, s, d))
        sem = self.sem


        wvT = [sb6(f"wvT{ib}", [128, 4 * 8 * 128], BF16) for ib in range(2)]
        wv_sb = [sb6(f"wv{ib}", [128, 512], BF16) for ib in range(2)]
        wf = sb6("wf", [128, 8 * 8], F32)
        wfT = sb6("wfT", [128, 1024], BF16)

        esL = es6.enter_context(contextlib.ExitStack())
        sbL = lambda n, s, d: esL.enter_context(nc.sbuf_tensor(n, s, d))
        ln2T = sbL("ln2T", [128, NCH * 1024], BF16)
        uall1 = sbL("uall1", [128, 8 * 512], BF16)
        uall3 = sbL("uall3", [128, 8 * 512], BF16)
        u1sb = [uall1[0:32, :], uall1[32:64, :]]
        u3sb = [uall3[0:32, :], uall3[32:64, :]]
        web = sbL("web", [128, 8 * 1024], BF16)
        ln2T3 = ln2T[:, :].rearrange("p (c t) -> p c t", c=NCH)
        web3 = web[:, :].rearrange("p (e t) -> p e t", e=8)

        esa = es6.enter_context(contextlib.ExitStack())
        sba = lambda n, s, d: esa.enter_context(nc.sbuf_tensor(n, s, d))
        psa = lambda n, s, d: esa.enter_context(nc.psum_tensor(n, s, d))
        xb6_ = sba("xb6", [128, D], F32)
        xb6 = [xb6_, xb6_]
        ln6_ = sba("ln6", [128, D], BF16)
        ln6 = [ln6_, ln6_]
        aubuf = sba("aubuf", [128, NCH * 256], BF16)
        gw_sb = sba("gw_sb", [128, NCH * 8], BF16)
        lgs = sba("lgs", [128, 8], F32)
        vals = sba("vals", [128, 8], F32)
        dbuf = sba("dbuf", [128, 4], F32)
        eqa = sba("eqa", [128, 8], F32)
        eqb = sba("eqb", [128, 8], F32)
        pga = [psa(f"pga{i}", [128, 512], F32) for i in range(2)]
        pgb = [psa(f"pgb{i}", [128, 1024], BF16) for i in range(2)]
        pgc = psa("pgc", [128, 512], F32)

        s_x6 = [sem("s_x6a"), sem("s_x6b")]
        s_st6, s_rs6, s_nb6 = me.s_stat, me.s_rstd, me.s_nb
        st6_0, rs6_0, nb6_0 = s_st6.v, s_rs6.v, s_nb6.v
        s_ln6 = sem("s_ln6")
        s_tp6 = sem("s_tp6"); s_te6 = sem("s_te6")
        s_au = sem("s_au")
        s_gw = sem("s_gw")
        s_lg = sem("s_lg")
        s_gv = sem("s_gv")
        s_gs = sem("s_gs")
        s_wfv = sem("s_wfv")
        s_wft = sem("s_wft")
        s_wfe = sem("s_wfe")
        s_web = sem("s_web")
        s_webe = sem("s_webe")
        s_um = sem("s_um")
        s_ue = sem("s_ue")

        au3v = aubuf[:, :].rearrange("p (c t) -> p c t", c=NCH)
        gw3 = gw_sb[:, :].rearrange("p (c t) -> p c t", c=NCH)

        @block.gpsimd
        def _(g):
            g.wait_ge(me.s_x1s.h, 16 * 8)
            srcg = me.gwT[:, :].rearrange("(c p) q -> p c q", p=128)
            s_gw.inc(g.dma_start(out=gw3, in_=srcg), 16)
            srca = me.au1[:, :].rearrange("(c p) q -> p c q", p=128)
            s_au.inc(g.dma_start(out=au3v, in_=srca), 16)

        @block.sync
        def _(sy):
            sy.wait_ge(me.s_x1s.h, 16 * 8)
            for it in range(8):
                if it >= 1:
                    sy.wait_ge(s_ln6.h, it)
                s_x6[it % 2].inc(sy.dma_start(
                    out=xb6[it % 2][:, :],
                    in_=me.x1dram[128 * it:128 * (it + 1), :]), 16)

        @block.vector
        def _(v):
            v.drain()
            v.memset(wfT[:, :], 0.0)
            v.memset(uall1[:, :], 0.0)
            v.memset(uall3[:, :], 0.0)
            for it in range(8):
                v.wait_ge(s_x6[it % 2].h, 16 * (it // 2 + 1))
                xt = xb6[it % 2]
                for q in range(4):
                    v.bn_stats(me.stats[:, 6 * q:6 * q + 6],
                               xt[:, 512 * q:512 * (q + 1)])
                v.drain()
                s_st6.inc(v.bn_aggr(me.mv[it % 2][:, :], me.stats[:, :]))
                if it >= 1:
                    for half in range(2):
                        nep = 2 * (it - 1) + half
                        v.wait_ge(s_tp6.h, nep + 1)
                        bank = pgb[nep % 2]
                        dst = ln2T3[:, 8 * half:8 * half + 8,
                                    128 * (it - 1):128 * it]
                        srcb = bank[:, 0:1024].rearrange("p (c t) -> p c t", c=8)
                        s_te6.inc(v.tensor_copy(dst, srcb))
                v.wait_ge(s_rs6.h, rs6_0 + it + 1)
                v.reciprocal(me.rstd[it % 2][:, :], me.rstd[it % 2][:, :])
                v.drain()
                s_nb6.inc(v.tensor_scalar(
                    me.nbias[it % 2][:, :], me.rstd[it % 2][:, :],
                    me.mv[it % 2][:, 0:1], -1.0, op0=ALU.mult, op1=ALU.mult))
                if it >= 1:
                    self.emit_gate_v(v, it - 1, lgs, vals, dbuf, eqa, eqb, wf,
                                     pgc, s_lg, s_gv, s_gs, s_wfv)
            for half in range(2):
                nep = 2 * 7 + half
                v.wait_ge(s_tp6.h, nep + 1)
                bank = pgb[nep % 2]
                dst = ln2T3[:, 8 * half:8 * half + 8, 128 * 7:128 * 8]
                srcb = bank[:, 0:1024].rearrange("p (c t) -> p c t", c=8)
                s_te6.inc(v.tensor_copy(dst, srcb))
            self.emit_gate_v(v, 7, lgs, vals, dbuf, eqa, eqb, wf,
                             pgc, s_lg, s_gv, s_gs, s_wfv)
            for it in range(8):
                v.wait_ge(s_wft.h, it + 1)
                s_wfe.inc(v.tensor_copy(wfT[0:8, 128 * it:128 * (it + 1)],
                                        pga[it % 2][0:8, 0:128]))
            for e in range(8):
                for hf in range(2):
                    n = 2 * e + hf
                    v.wait_ge(s_web.h, n + 1)
                    s_webe.inc(v.tensor_copy(
                        web3[:, e, 512 * hf:512 * (hf + 1)],
                        pga[n % 2][:, :]))
            for n in range(32):
                v.wait_ge(s_um.h, n + 1)
                path, ib, slot = n // 16, (n // 8) % 2, n % 8
                usb = (u1sb if path == 0 else u3sb)[ib]
                s_ue.inc(v.tensor_scalar(
                    usb[0:16, 512 * slot:512 * (slot + 1)],
                    pga[n % 2][0:16, :], LORA, None, op0=ALU.mult))

        @block.scalar
        def _(s):
            for it in range(8):
                s.wait_ge(s_st6.h, st6_0 + it + 1)
                s_rs6.inc(s.activation(
                    me.rstd[it % 2][:, :], me.mv[it % 2][:, 1:2],
                    AF.Sqrt, bias=me.epsv[:, 0:1], scale=1.0))
                s.wait_ge(s_nb6.h, nb6_0 + it + 1)
                if it >= 1:
                    s.wait_ge(s_tp6.h, 2 * (it - 1) + 2)
                s_ln6.inc(s.activation(
                    ln6[it % 2][:, :], xb6[it % 2][:, :], AF.Identity,
                    bias=me.nbias[it % 2][:, 0:1], scale=me.rstd[it % 2][:, 0:1]))
                if it >= 1:
                    s.wait_ge(s_gv.h, it)
                    s_gs.inc(s.activation(dbuf[:, 1:2], dbuf[:, 0:1],
                                          AF.Sigmoid, bias=0.0, scale=1.0))
            s.wait_ge(s_gv.h, 8)
            s_gs.inc(s.activation(dbuf[:, 1:2], dbuf[:, 0:1],
                                  AF.Sigmoid, bias=0.0, scale=1.0))

        @block.tensor
        def _(t):
            t.wait_ge(s_gw.h, 16)
            t.wait_ge(s_au.h, 16)
            for it in range(8):
                t.wait_ge(s_ln6.h, it + 1)
                for half in range(2):
                    nep = 2 * it + half
                    if nep >= 2:
                        t.wait_ge(s_te6.h, nep - 1)
                    bank = pgb[nep % 2]
                    for c8 in range(8):
                        ch = 8 * half + c8
                        ins = t.transpose(bank[:, 128 * c8:128 * (c8 + 1)],
                                          ln6[it % 2][:, 128 * ch:128 * (ch + 1)],
                                          me.idb[:, :])
                        if c8 == 7:
                            s_tp6.inc(ins)
                t.wait_ge(s_te6.h, 2 * it + 2)
                if it >= 1:
                    t.wait_ge(s_gv.h, it)
                for ch in range(NCH):
                    ins = t.matmul(pgc[:, 0:8], ln2T3[:, ch, 128 * it:128 * (it + 1)],
                                   gw3[:, ch, :], start=(ch == 0), stop=(ch == NCH - 1))
                    if ch == NCH - 1:
                        s_lg.inc(ins)
            for it in range(8):
                t.wait_ge(s_wfv.h, it + 1)
                if it >= 2:
                    t.wait_ge(s_wfe.h, it - 1)
                s_wft.inc(t.transpose(pga[it % 2][0:8, 0:128],
                                      wf[:, 8 * it:8 * (it + 1)], me.idf[:, :]))
            t.wait_ge(s_wfe.h, 8)
            for e in range(8):
                for hf in range(2):
                    n = 2 * e + hf
                    if n >= 2:
                        t.wait_ge(s_webe.h, n - 1)
                    s_web.inc(t.matmul(
                        pga[n % 2][:, :], me.onehots[:, 128 * e:128 * (e + 1)],
                        wfT[:, 512 * hf:512 * (hf + 1)], start=True, stop=True))
            t.wait_ge(s_webe.h, 16)
            for n in range(32):
                path, ib, slot = n // 16, (n // 8) % 2, n % 8
                if n == 16:
                    t.wait_ge(s_au.h, 32)
                if n >= 2:
                    t.wait_ge(s_ue.h, n - 1)
                for ch in range(NCH):
                    ins = t.matmul(
                        pga[n % 2][0:32, :],
                        au3v[:, ch, 32 * slot:32 * (slot + 1)],
                        ln2T3[:, ch, 512 * ib:512 * (ib + 1)],
                        start=(ch == 0), stop=(ch == NCH - 1))
                    if ch == NCH - 1:
                        s_um.inc(ins)

        @block.gpsimd
        def _(g):
            g.wait_ge(s_um.h, 16)
            srca = me.au3[:, :].rearrange("(c p) q -> p c q", p=128)
            s_au.inc(g.dma_start(out=au3v, in_=srca), 16)

        self.esa = esa
        self.s_ue, self.s_webe = s_ue, s_webe
        self.p6_shared = dict(wvT=wvT, wv_sb=wv_sb, wf=wf,
                              ln2T3=ln2T3, u1sb=u1sb, u3sb=u3sb, web3=web3,
                              uall1=uall1, uall3=uall3, esL=esL, es6=es6)

    def emit_gate_v(self, v, it, lgs, vals, dbuf, eqa, eqb, wf, pgc,
                    s_lg, s_gv, s_gs, s_wfv):
        me = self
        v.wait_ge(s_lg.h, it + 1)
        v.tensor_copy(lgs[:, :], pgc[:, 0:8])
        v.drain()
        v.max(vals[:, :], lgs[:, :])
        v.drain()
        s_gv.inc(v.tensor_scalar(dbuf[:, 0:1], vals[:, 0:1], vals[:, 1:2],
                                 None, op0=ALU.subtract))
        v.wait_ge(s_gs.h, it + 1)
        v.tensor_scalar(dbuf[:, 2:3], dbuf[:, 1:2], -1.0, 1.0,
                        op0=ALU.mult, op1=ALU.add)
        v.tensor_scalar(eqa[:, :], lgs[:, :], vals[:, 0:1], None, op0=ALU.is_equal)
        v.tensor_scalar(eqb[:, :], lgs[:, :], vals[:, 1:2], None, op0=ALU.is_equal)
        v.drain()
        v.tensor_tensor(lgs[:, :], eqb[:, :], eqa[:, :], op=ALU.mult)
        v.drain()
        v.tensor_tensor(eqb[:, :], eqb[:, :], lgs[:, :], op=ALU.subtract)
        v.tensor_scalar(eqa[:, :], eqa[:, :], dbuf[:, 1:2], None, op0=ALU.mult)
        v.drain()
        s_wfv.inc(v.scalar_tensor_tensor(wf[:, 8 * it:8 * (it + 1)], eqb[:, :],
                                         dbuf[:, 2:3], eqa[:, :],
                                         op0=ALU.mult, op1=ALU.add))

    def phase6b(self):
        nc, block, me = self.nc, self.block, self
        sh = self.p6_shared
        es6 = sh["es6"]
        ln2T3, web3 = sh["ln2T3"], sh["web3"]
        uall1, uall3 = sh["uall1"], sh["uall3"]
        sem = self.sem
        self.pv = [es6.enter_context(nc.psum_tensor(f"pv{ib}", [128, 512], F32))
                   for ib in range(2)]
        pv = self.pv
        esb = es6.enter_context(contextlib.ExitStack())
        sbB = lambda n, s, d: esb.enter_context(nc.sbuf_tensor(n, s, d))
        psB = lambda n, s, d: esb.enter_context(nc.psum_tensor(n, s, d))
        self.esb = esb
        w1b = sbB("w1b", [128, NCH * 128], BF16)
        w3b = sbB("w3b", [128, NCH * 128], BF16)
        a2b = [sbB(f"a2b{i}", [128, 128], BF16) for i in range(2)]
        b1b = [[sbB(f"b1b{i}_{ib}", [128, 8 * 128], BF16) for ib in range(2)]
               for i in range(2)]
        b3b = [[sbB(f"b3b{i}_{ib}", [128, 8 * 128], BF16) for ib in range(2)]
               for i in range(2)]
        bs1_ = [sbB(f"bs1_{ib}", [128, 512], BF16) for ib in range(2)]
        bs3_ = [sbB(f"bs3_{ib}", [128, 512], BF16) for ib in range(2)]
        bs1 = [bs1_, bs1_]
        bs3 = [bs3_, bs3_]
        s1 = [sbB(f"s1_{i}", [128, 512], BF16) for i in range(2)]
        act = [sbB(f"act_{i}", [128, 512], BF16) for i in range(4)]
        wact = [sbB(f"wact_{i}", [128, 512], BF16) for i in range(4)]
        HB1 = [psB(f"hb1_{i}", [128, 512], F32) for i in range(2)]
        HB3 = [psB(f"hb3_{i}", [128, 512], F32) for i in range(2)]
        pmac = [psB(f"pmac{ib}", [128, 512], F32) for ib in range(2)]

        s_wt = [sem("s_wt0"), sem("s_wt1")]
        s_b1 = sem("s_b1"); s_b3 = sem("s_b3")
        s_bev1 = sem("s_bev1"); s_bev3 = sem("s_bev3")
        s_h1e = sem("s_h1e"); s_h3e = sem("s_h3e")
        s_silu = sem("s_silu"); s_act = sem("s_act"); s_wact = sem("s_wact")
        s_vm = sem("s_vm"); s_imm = sem("s_imm")
        s_mev = [sem("s_mev0"), sem("s_mev1")]
        s_mst = sem("s_mst")
        mstage = [sbB(f"mstage{i}", [128, 512], BF16) for i in range(2)]
        self.s_mst = s_mst
        s_bbz = sem("s_bbz")
        self.s_vm, self.s_mev = s_vm, s_mev
        self.s_vm_total = 16 * NFT
        ue_total = 32

        pool1_free = []
        pool3_free = []

        @block.gpsimd
        def _(g):
            g.wait_ge(me.s_ue.h, ue_total)
            g.wait_ge(s_bbz.h, 1)
            for ft in range(NFT):
                par = ft % 2
                s = s_wt[par]
                if ft >= 1:
                    g.wait_ge(s_b1.h, 2 * ft)
                    g.wait_ge(s_b3.h, 2 * ft)
                if ft >= 2:
                    g.wait_ge(s_h1e.h, 16 * (ft - 1))
                    g.wait_ge(s_h3e.h, 16 * (ft - 1))
                    g.wait_ge(s_vm.h, 16 * (ft - 1))
                src1 = me.w1T[:, 128 * ft:128 * (ft + 1)].rearrange("(c p) q -> p c q", p=128)
                s.inc(g.dma_start(out=w1b[:, :].rearrange("p (c q) -> p c q", c=NCH), in_=src1), 16)
                src3 = me.w3T[:, 128 * ft:128 * (ft + 1)].rearrange("(c p) q -> p c q", p=128)
                s.inc(g.dma_start(out=w3b[:, :].rearrange("p (c q) -> p c q", c=NCH), in_=src3), 16)
                s.inc(g.dma_start(out=a2b[par][:, :], in_=me.a2cat[128 * ft:128 * (ft + 1), :]), 16)
                s.inc(g.dma_start(out=b1b[par][0][0:16, :], in_=me.b1pk[ft:ft + 1].rearrange("o p q -> (o p) q")), 16)
                s.inc(g.dma_start(out=b1b[par][1][32:48, :], in_=me.b1pk[ft:ft + 1].rearrange("o p q -> (o p) q")), 16)
                s.inc(g.dma_start(out=b3b[par][0][0:16, :], in_=me.b3pk[ft:ft + 1].rearrange("o p q -> (o p) q")), 16)
                s.inc(g.dma_start(out=b3b[par][1][32:48, :], in_=me.b3pk[ft:ft + 1].rearrange("o p q -> (o p) q")), 16)

        @block.sync
        def _(sy):
            for ft in range(NFT):
                for ib in range(2):
                    n = 2 * ft + ib
                    sy.wait_ge(s_mev[ib].h, ft + 1)
                    s_mst.inc(sy.dma_start(
                        out=me.maccd[ib][ft:ft + 1].rearrange("o p q -> (o p) q"),
                        in_=mstage[n % 2][:, :]), 16)

        LAG = 3

        @block.tensor
        def _(t):
            t.wait_ge(me.s_ue.h, ue_total)
            t.wait_ge(me.s_webe.h, 16)
            w1v = w1b[:, :].rearrange("p (c q) -> p c q", c=NCH)
            w3v = w3b[:, :].rearrange("p (c q) -> p c q", c=NCH)
            pending = []

            def emit_vi(step):
                ft_, e_, ib_, G_ = step
                t.wait_ge(s_act.h, G_ + 1)
                for isub in range(4):
                    off = 16 * (8 * isub + e_)
                    ins = t.matmul(
                        pv[ib_][:, off:off + 16],
                        act[G_ % 4][:, 128 * isub:128 * (isub + 1)],
                        a2b[ft_ % 2][:, 16 * e_:16 * (e_ + 1)],
                        start=(ft_ == 0 and e_ == 0 and isub == 0),
                        stop=(ft_ == NFT - 1 and e_ == 7 and isub == 3),
                        skip_group_check=True)
                    if isub == 3:
                        s_vm.inc(ins)
                t.wait_ge(s_wact.h, G_ + 1)
                if e_ == 0:
                    t.wait_ge(s_mev[ib_].h, ft_)
                s_imm.inc(t.matmul(
                    pmac[ib_][:, :], me.idb[:, :], wact[G_ % 4][:, :],
                    start=(e_ == 0), stop=(e_ == 7)))

            for ft in range(NFT):
                par = ft % 2
                t.wait_ge(s_wt[par].h, 112 * (ft // 2 + 1))
                for ib in range(2):
                    u = len(pool1_free)
                    if u >= 2:
                        t.wait_ge(*pool1_free[u - 2])
                    bank = HB1[u % 2]
                    for ch in range(NCH):
                        ins = t.matmul(bank[:, :], w1v[:, ch, :],
                                       ln2T3[:, ch, 512 * ib:512 * (ib + 1)],
                                       start=(ch == 0), stop=(ch == NCH - 1))
                    s_b1.inc(ins)
                    pool1_free.append((s_bev1.h, 2 * ft + ib + 1))
                    u = len(pool3_free)
                    if u >= 2:
                        t.wait_ge(*pool3_free[u - 2])
                    bank = HB3[u % 2]
                    for ch in range(NCH):
                        ins = t.matmul(bank[:, :], w3v[:, ch, :],
                                       ln2T3[:, ch, 512 * ib:512 * (ib + 1)],
                                       start=(ch == 0), stop=(ch == NCH - 1))
                    s_b3.inc(ins)
                    pool3_free.append((s_bev3.h, 2 * ft + ib + 1))
                for e in range(8):
                    for ib in range(2):
                        G = 16 * ft + 2 * e + ib
                        u = len(pool1_free)
                        if u >= 2:
                            t.wait_ge(*pool1_free[u - 2])
                        t.wait_ge(s_bev1.h, 2 * ft + ib + 1)
                        bank = HB1[u % 2]
                        t.matmul(bank[:, :], me.idb[:, :], bs1[par][ib][:, :],
                                 start=True, stop=False)
                        s_h1e.inc(t.matmul(bank[:, :], b1b[par][ib][:, 128 * e:128 * (e + 1)],
                                           uall1[:, 512 * e:512 * (e + 1)],
                                           start=False, stop=True))
                        pool1_free.append((s_silu.h, G + 1))
                        u = len(pool3_free)
                        if u >= 2:
                            t.wait_ge(*pool3_free[u - 2])
                        t.wait_ge(s_bev3.h, 2 * ft + ib + 1)
                        bank = HB3[u % 2]
                        t.matmul(bank[:, :], me.idb[:, :], bs3[par][ib][:, :],
                                 start=True, stop=False)
                        s_h3e.inc(t.matmul(bank[:, :], b3b[par][ib][:, 128 * e:128 * (e + 1)],
                                           uall3[:, 512 * e:512 * (e + 1)],
                                           start=False, stop=True))
                        pool3_free.append((s_act.h, G + 1))
                        pending.append((ft, e, ib, G))
                        while len(pending) > LAG:
                            emit_vi(pending.pop(0))
            while pending:
                emit_vi(pending.pop(0))

        @block.scalar
        def _(s):
            u1 = 0
            for ft in range(NFT):
                par = ft % 2
                for ib in range(2):
                    s.wait_ge(s_b1.h, 2 * ft + ib + 1)
                    bank = HB1[u1 % 2]
                    s_bev1.inc(s.copy(bs1[par][ib][:, :], bank[:, :]))
                    u1 += 1
                    s.wait_ge(s_b3.h, 2 * ft + ib + 1)
                    bank = HB3[(u1 - 1) % 2]
                    s_bev3.inc(s.copy(bs3[par][ib][:, :], bank[:, :]))
                for e in range(8):
                    for ib in range(2):
                        G = 16 * ft + 2 * e + ib
                        s.wait_ge(s_h1e.h, G + 1)
                        if G >= 2:
                            s.wait_ge(s_act.h, G - 1)
                        bank = HB1[(18 * ft + 2 + 2 * e + ib) % 2]
                        s_silu.inc(s.activation(s1[G % 2][:, :], bank[:, :],
                                                AF.Silu, bias=0.0, scale=1.0))
                u1 = 18 * (ft + 1)

        @block.vector
        def _(v):
            v.drain()
            ins = None
            for par in range(2):
                for ibv in range(2):
                    v.memset(b1b[par][ibv][:, :], 0.0)
                    ins = v.memset(b3b[par][ibv][:, :], 0.0)
            s_bbz.inc(ins)
            for ft in range(NFT):
                for e in range(8):
                    for ib in range(2):
                        G = 16 * ft + 2 * e + ib
                        v.wait_ge(s_silu.h, G + 1)
                        v.wait_ge(s_h3e.h, G + 1)
                        if G >= 4:
                            v.wait_ge(s_vm.h, G - 3)
                        bank = HB3[(18 * ft + 2 + 2 * e + ib) % 2]
                        s_act.inc(v.tensor_tensor(
                            act[G % 4][:, :], s1[G % 2][:, :], bank[:, :],
                            op=ALU.mult))
                    for ib in range(2):
                        G = 16 * ft + 2 * e + ib
                        if G >= 4:
                            v.wait_ge(s_imm.h, G - 3)
                        if SIM_SAFE[0]:
                            v.drain()
                        s_wact.inc(v.tensor_tensor(
                            wact[G % 4][:, :], act[G % 4][:, :],
                            web3[:, e, 512 * ib:512 * (ib + 1)], op=ALU.mult))
                        if e == 1 and ft >= 1:
                            v.wait_ge(s_imm.h, 16 * (ft - 1) + 15 + ib)
                            n = 2 * (ft - 1) + ib
                            if n >= 2:
                                v.wait_ge(s_mst.h, 16 * n)
                            s_mev[ib].inc(v.tensor_copy(
                                mstage[n % 2][:, :], pmac[ib][:, :]))
            for ib in range(2):
                v.wait_ge(s_imm.h, 16 * (NFT - 1) + 15 + ib)
                n = 2 * (NFT - 1) + ib
                v.wait_ge(s_mst.h, 16 * n)
                s_mev[ib].inc(v.tensor_copy(mstage[n % 2][:, :], pmac[ib][:, :]))

    def phase6c(self):
        nc, block, me = self.nc, self.block, self
        sh = self.p6_shared
        es6, esL = sh["es6"], sh["esL"]
        wvT, wv_sb, wf = sh["wvT"], sh["wv_sb"], sh["wf"]
        pv = self.pv
        self.esb.close()
        esL.close()
        sem = self.sem
        esc = es6.enter_context(contextlib.ExitStack())
        sbC = lambda n, s, d: esc.enter_context(nc.sbuf_tensor(n, s, d))
        psC = lambda n, s, d: esc.enter_context(nc.psum_tensor(n, s, d))
        b2full = sbC("b2full", [128, E * D], BF16)
        y2v = [sbC(f"y2v{i}", [128, D], F32) for i in range(4)]
        x1b = [sbC(f"x1b{i}", [128, D], F32) for i in range(2)]
        x2t_ = sbC("x2t", [128, D], F32)
        x2t = [x2t_, x2t_]
        CH6 = 8
        w2b = [sbC(f"w2b{i}", [128, CH6 * 512], BF16) for i in range(2)]
        mb = [sbC(f"mb{i}", [128, CH6 * 512], BF16) for i in range(2)]
        po = [psC(f"po{i}", [128, 512], F32) for i in range(4)]
        pbc = psC("pbc", [16, 1024], BF16)

        s_b2 = sem("s_b2")
        s_zz = sem("s_zz")
        s_wv = [sem("s_wv0"), sem("s_wv1")]
        s_wvt = sem("s_wvt"); s_wve = sem("s_wve")
        s_w2 = [sem("s_w2a"), sem("s_w2b")]
        s_m2 = [sem("s_m2a"), sem("s_m2b")]
        s_w2c = sem("s_w2c")
        blocks6 = []
        ft0 = 0
        while ft0 < NFT:
            blocks6.append((ft0, min(NFT, ft0 + CH6)))
            ft0 += CH6
        s_o2 = sem("s_o2")
        s_ev6 = sem("s_ev6")
        s_st6c, s_rs6c, s_nb6c = me.s_stat, me.s_rstd, me.s_nb
        st6c_0, rs6c_0, nb6c_0 = s_st6c.v, s_rs6c.v, s_nb6c.v
        s_y6 = sem("s_y6")
        s_x2 = sem("s_x2")
        s_xc = [sem("s_xc0"), sem("s_xc1")]
        s_outs = sem("s_outs")
        self.s_outs = s_outs

        @block.vector
        def _(v):
            v.drain()
            v.memset(b2full[:, :], 0.0)
            v.memset(wvT[0][:, :], 0.0)
            s_zz.inc(v.memset(wvT[1][:, :], 0.0))
            for ib in range(2):
                v.wait_ge(me.s_vm.h, me.s_vm_total)
                for isub in range(4):
                    for e in range(8):
                        off = 16 * (8 * isub + e)
                        col = 8 * (4 * ib + isub) + e
                        ins = v.tensor_scalar(
                            wv_sb[ib][:, off:off + 16], pv[ib][:, off:off + 16],
                            wf[:, col:col + 1], 2.0, op0=ALU.mult, op1=ALU.mult)
                        if isub == 3 and e == 7:
                            s_wv[ib].inc(ins)
            for k in range(8):
                ib, isub = k // 4, k % 4
                v.wait_ge(s_wvt.h, k + 1)
                s_wve.inc(v.tensor_copy(
                    wvT[ib][0:16, 1024 * isub:1024 * (isub + 1)], pbc[0:16, :]))

        @block.tensor
        def _(t):
            for k in range(8):
                ib, isub = k // 4, k % 4
                t.wait_ge(s_wv[ib].h, 1)
                if k >= 1:
                    t.wait_ge(s_wve.h, k)
                for e in range(8):
                    off = 16 * (8 * isub + e)
                    ins = t.transpose(pbc[0:16, 128 * e:128 * (e + 1)],
                                      wv_sb[ib][:, off:off + 16], me.idb[:, :])
                    if e == 7:
                        s_wvt.inc(ins)
            t.wait_ge(s_b2.h, 16)
            t.wait_ge(s_wve.h, 8)
            n2 = 0
            for grp in range(2):
                for oc in range(4):
                    noc = 4 * grp + oc
                    if noc >= 1:
                        t.wait_ge(s_ev6.h, 4 * noc)
                    for isub in range(4):
                        for e in range(8):
                            t.matmul(po[isub][:, :],
                                     wvT[grp][:, 1024 * isub + 128 * e:1024 * isub + 128 * (e + 1)],
                                     b2full[:, 2048 * e + 512 * oc:2048 * e + 512 * (oc + 1)],
                                     start=(e == 0), stop=False)
                    for (f0, f1) in blocks6:
                        t.wait_ge(s_w2[n2 % 2].h, 16 * (n2 // 2 + 1))
                        t.wait_ge(s_m2[n2 % 2].h, 16 * (n2 // 2 + 1))
                        for ft in range(f0, f1):
                            for isub in range(4):
                                ins = t.matmul(
                                    po[isub][:, :],
                                    mb[n2 % 2][:, 512 * (ft - f0) + 128 * isub:
                                               512 * (ft - f0) + 128 * (isub + 1)],
                                    w2b[n2 % 2][:, 512 * (ft - f0):512 * (ft - f0 + 1)],
                                    start=False, stop=(ft == NFT - 1))
                                if isub == 3 and ft == f1 - 1:
                                    if ft == NFT - 1:
                                        s_o2.inc(ins)
                                    else:
                                        s_w2c.inc(ins)
                        n2 += 1

        @block.gpsimd
        def _(g):
            g.wait_ge(me.s_mst.h, 16 * 2 * NFT)
            g.wait_ge(s_zz.h, 1)
            s_b2.inc(g.dma_start(out=b2full[0:16, :], in_=me.b2cat[:, :]), 16)
            n2 = 0
            nb = len(blocks6)
            for grp in range(2):
                for oc in range(4):
                    for bi, (f0, f1) in enumerate(blocks6):
                        if n2 >= 2:
                            prev = n2 - 2
                            bi_prev = prev % nb
                            if bi_prev == nb - 1:
                                g.wait_ge(s_o2.h, prev // nb + 1)
                            else:
                                g.wait_ge(s_w2c.h, prev + 1 - (prev // nb))
                        src = me.w2T[128 * f0:128 * f1,
                                     512 * oc:512 * (oc + 1)].rearrange(
                                         "(c p) q -> p c q", p=128)
                        dst = w2b[n2 % 2][:, 0:512 * (f1 - f0)].rearrange(
                            "p (c q) -> p c q", c=f1 - f0)
                        s_w2[n2 % 2].inc(g.dma_start(out=dst, in_=src), 16)
                        n2 += 1


        @block.scalar
        def _(s):
            for grp in range(2):
                if grp >= 1:
                    s.wait_ge(s_x2.h, 4 * grp)
                for oc in range(4):
                    noc = 4 * grp + oc
                    s.wait_ge(s_o2.h, 4 * grp + oc + 1)
                    for isub in range(4):
                        s_ev6.inc(s.copy(y2v[isub][:, 512 * oc:512 * (oc + 1)],
                                         po[isub][:, :]))
                for isub in range(4):
                    i128 = 4 * grp + isub
                    s.wait_ge(s_st6c.h, st6c_0 + 4 * grp + isub + 1)
                    s_rs6c.inc(s.activation(
                        me.rstd[isub % 2][:, :], me.mv[isub % 2][:, 1:2],
                        AF.Sqrt, bias=me.epsv[:, 0:1], scale=1.0))
                    s.wait_ge(s_nb6c.h, nb6c_0 + 4 * grp + isub + 1)
                    s_y6.inc(s.activation(
                        y2v[isub][:, :], y2v[isub][:, :], AF.Identity,
                        bias=me.nbias[isub % 2][:, 0:1],
                        scale=me.rstd[isub % 2][:, 0:1]))

        @block.vector
        def _(v):
            for grp in range(2):
                v.wait_ge(s_ev6.h, 16 * (grp + 1))
                for isub in range(4):
                    i128 = 4 * grp + isub
                    for q in range(4):
                        v.bn_stats(me.stats[:, 6 * q:6 * q + 6],
                                   y2v[isub][:, 512 * q:512 * (q + 1)])
                    v.drain()
                    s_st6c.inc(v.bn_aggr(me.mv[isub % 2][:, :], me.stats[:, :]))
                    v.wait_ge(s_rs6c.h, rs6c_0 + 4 * grp + isub + 1)
                    v.reciprocal(me.rstd[isub % 2][:, :], me.rstd[isub % 2][:, :])
                    v.drain()
                    s_nb6c.inc(v.tensor_scalar(
                        me.nbias[isub % 2][:, :], me.rstd[isub % 2][:, :],
                        me.mv[isub % 2][:, 0:1], -1.0, op0=ALU.mult, op1=ALU.mult))
                    v.wait_ge(s_y6.h, 4 * grp + isub + 1)
                    v.wait_ge(s_xc[i128 % 2].h, 16 * (i128 // 2 + 1))
                    if i128 >= 1:
                        v.wait_ge(s_outs.h, 16 * i128)
                    s_x2.inc(v.tensor_tensor(x2t[i128 % 2][:, :], y2v[isub][:, :],
                                             x1b[i128 % 2][:, :], op=ALU.add))

        @block.sync
        def _(sy):
            sy.wait_ge(me.s_mst.h, 16 * 2 * NFT)
            n2 = 0
            nb = len(blocks6)
            for grp in range(2):
                for oc in range(4):
                    for bi, (f0, f1) in enumerate(blocks6):
                        if n2 >= 2:
                            prev = n2 - 2
                            bi_prev = prev % nb
                            if bi_prev == nb - 1:
                                sy.wait_ge(s_o2.h, prev // nb + 1)
                            else:
                                sy.wait_ge(s_w2c.h, prev + 1 - (prev // nb))
                        dstm = mb[n2 % 2][:, 0:512 * (f1 - f0)].rearrange(
                            "p (c q) -> p c q", c=f1 - f0)
                        srcm = me.maccd[grp][f0:f1].rearrange("o p q -> p o q")
                        s_m2[n2 % 2].inc(sy.dma_start(out=dstm, in_=srcm), 16)
                        n2 += 1
                for isub in range(4):
                    i128 = 4 * grp + isub
                    if i128 >= 2:
                        sy.wait_ge(s_x2.h, i128 - 1)
                    s_xc[i128 % 2].inc(sy.dma_start(
                        out=x1b[i128 % 2][:, :],
                        in_=me.x1dram[128 * i128:128 * (i128 + 1), :]), 16)
                    sy.wait_ge(s_x2.h, i128 + 1)
                    s_outs.inc(sy.dma_start(
                        out=me.out[128 * i128:128 * (i128 + 1), :],
                        in_=x2t[i128 % 2][:, :]), 16)
            sy.wait_ge(s_outs.h, 16 * 8)

    def build(self, upto="p3"):
        self.declare_dram()
        self.es = contextlib.ExitStack()
        nc = self.nc
        self.sb = lambda n, s, d: self.es.enter_context(nc.sbuf_tensor(n, s, d))
        self.sem = lambda n: Sem(self.es, nc, n)
        with self.es:
            self.phase0_alloc()
            self.esA = self.es.enter_context(contextlib.ExitStack())
            self.sbA = lambda n, s, d: self.esA.enter_context(nc.sbuf_tensor(n, s, d))
            self.psA = lambda n, s, d: self.esA.enter_context(nc.psum_tensor(n, s, d))
            self.block = self.es.enter_context(nc.Block())
            self.phase0()
            self.phase1()
            if upto == "p1":
                self.finish_dev_p1()
                return nc
            self.phase2()
            self.es1.close()
            self.phase3()
            if upto == "p3":
                self.finish_dev_p3()
                return nc
            self.phase4()
            self.esA.close()
            self.phase5()
            if upto == "p5":
                self.finish_dev_p5()
                return nc
            self.es5.close()
            self.phase6()
            self.esa.close()
            self.phase6b()
            self.phase6c()
            self.finish_dbg_all()
        return nc

    def finish_dbg_all(self):
        me, block = self, self.block
        dx1 = self.dbgout("x1", [TOK, D])
        dwf = self.dbgout("wf", [128, 64])
        dm0 = dm1 = None
        if not (dx1 or dwf):
            return
        sh = self.p6_shared

        @block.sync
        def _(sy):
            sy.wait_ge(me.s_outs.h, 16 * 8)
            n = 0
            if dx1 is not None:
                me.sem_x.inc(sy.dma_start(out=dx1[:, :], in_=me.x1dram[:, :]), 16)
            if dwf is not None:
                me.sem_x.inc(sy.dma_start(out=dwf[:, :], in_=sh["wf"][:, :]), 16)

            sy.wait_ge(me.sem_x.h, me.sem_x.v)

    def finish_dev_p5(self):
        me, block = self, self.block
        dx1 = self.dbgout("x1", [TOK, D])

        @block.sync
        def _(sy):
            sy.wait_ge(me.s_x1s.h, 16 * 8)
            if dx1 is not None:
                me.sem_x.inc(sy.dma_start(out=dx1[:, :], in_=me.x1dram[:, :]), 16)
            sy.wait_ge(me.sem_x.h, me.sem_x.v)

    def finish_dev_p1(self):
        me, block = self, self.block
        dq = self.dbgout("qT", [4 * 128, S], BF16)
        dk = self.dbgout("kT", [4 * 128, S], BF16)
        dv = self.dbgout("Vt", [4 * 128, 32 * 129], BF16)

        @block.sync
        def _(sy):
            sy.wait_ge(me.s_ev.h, me.p1_ev_total)
            sy.wait_ge(me.s_tpev.h, me.p1_tpev_total)
            for h in range(4):
                if dq is not None:
                    me.sem_x.inc(sy.dma_start(out=dq[128 * h:128 * (h + 1), :],
                                              in_=me.qT[h][:, :]), 16)
                if dk is not None:
                    me.sem_x.inc(sy.dma_start(out=dk[128 * h:128 * (h + 1), :],
                                              in_=me.kT[h][:, :]), 16)
                if dv is not None:
                    me.sem_x.inc(sy.dma_start(out=dv[128 * h:128 * (h + 1), :],
                                              in_=me.Vt[h][:, :]), 16)

            sy.wait_ge(me.sem_x.h, me.sem_x.v)

    def finish_dev_p3(self):
        me, block = self, self.block
        da = self.dbgout("aoT", [4 * 128, S], BF16)
        db = self.dbgout("bnd", [1, 20])

        @block.sync
        def _(sy):
            sy.wait_ge(me.s_aot.h, 32)
            for h in range(4):
                if da is not None:
                    me.sem_x.inc(sy.dma_start(out=da[128 * h:128 * (h + 1), :],
                                              in_=me.aoT[h][:, :]), 16)
            if db is not None:
                me.sem_x.inc(sy.dma_start(out=db[:, :], in_=me.bnd[:, 0:20]), 16)
            sy.wait_ge(me.sem_x.h, me.sem_x.v)


def build_program(dbg_names=(), upto="p1"):
    b = Builder(dbg_names)
    nc = b.build(upto=upto)
    return nc, b.dbg


def host_inputs(core_id, I):
    c = core_id
    b, hq = c // 4, c % 4
    heads = [4 * hq + h for h in range(4)]
    C = np.ascontiguousarray
    Wqkv = np.asarray(I["L0_attn_Wqkv"], dtype=np.float32)
    bqkv = np.asarray(I["L0_attn_bqkv"], dtype=np.float32)
    wq = np.concatenate([Wqkv[:, 128 * h:128 * (h + 1)] for h in heads], axis=1)
    wk = np.concatenate([Wqkv[:, D + 128 * h:D + 128 * (h + 1)] for h in heads], axis=1)
    wv = np.concatenate([Wqkv[:, 2 * D + 128 * h:2 * D + 128 * (h + 1)] for h in heads], axis=1)
    wqkv = C(np.concatenate([wq, wk, wv], axis=1))
    bq = np.concatenate([bqkv[128 * h:128 * (h + 1)] for h in heads])
    bk = np.concatenate([bqkv[D + 128 * h:D + 128 * (h + 1)] for h in heads])
    bv = np.concatenate([bqkv[2 * D + 128 * h:2 * D + 128 * (h + 1)] for h in heads])
    bqkv_col = C(np.concatenate([bq, bk, bv]).reshape(12, 128).T)

    a1 = np.asarray(I["L0_ffn_a1"], np.float32)
    a3 = np.asarray(I["L0_ffn_a3"], np.float32)
    a2 = np.asarray(I["L0_ffn_a2"], np.float32)
    b1 = np.asarray(I["L0_ffn_b1"], np.float32)
    b3 = np.asarray(I["L0_ffn_b3"], np.float32)
    b2 = np.asarray(I["L0_ffn_b2"], np.float32)

    def aucat(a):
        z = np.zeros((D, 16), np.float32)
        cols = []
        for t in range(8):
            cols.append(a[t].T)
            cols.append(z)
        return C(np.concatenate(cols, axis=1))

    def bpk(bm):
        bT = bm.transpose(0, 2, 1)
        r = bT.reshape(8, 16, NFT, 128).transpose(2, 1, 0, 3).reshape(NFT, 16, 8 * 128)
        return C(r)

    a2c = C(np.concatenate([a2[e].T for e in range(E)], axis=1))
    b2c = C(b2.transpose(2, 0, 1).reshape(16, E * D))

    identf = np.eye(128, dtype=np.float32)
    onehot = np.zeros((128, 8 * 128), np.float32)
    for e in range(E):
        onehot[e, 128 * e:128 * (e + 1)] = 1.0
    e1row = np.zeros((128, 128), np.float32); e1row[0, :] = 1.0
    onecol = np.zeros((128, 128), np.float32); onecol[:, 0] = 1.0
    slneg = np.zeros((1, 4 * S), np.float32)
    for h in range(4):
        slneg[0, S * h:S * (h + 1)] = -SLOPES[heads[h]] * np.arange(S, dtype=np.float32)
    sjh = np.zeros((128, 4 * 32), np.float32)
    for h in range(4):
        sl = SLOPES[heads[h]]
        for jt in range(32):
            sjh[:, 32 * h + jt] = sl * (128 * jt + np.arange(128))
    mask = np.zeros((128, 4 * 512), np.float32)
    for k in range(4):
        jj = np.arange(128)[:, None]
        ii = np.arange(512)[None, :]
        mask[:, 512 * k:512 * (k + 1)] = np.where(jj + 128 * k <= ii, 0.0, NEGBIG)
    svec = np.full((32, 1), LORA, np.float32)
    gidx = np.zeros((128, NCH), np.uint32)
    for ch in range(NCH):
        gidx[:, ch] = (2048 * (ch % 4) + 512 * hq + 128 * (ch // 4)
                       + np.arange(128))

    x = np.asarray(I["L0_attn_x"], np.float32)
    return {
        "xb": C(x[b]),
        "xsl": C(x[b][1024 * hq:1024 * (hq + 1)]),
        "wqkv": wqkv, "bqkv_col": bqkv_col,
        "wo": C(np.asarray(I["L0_attn_Wo"], np.float32)),
        "bo_row": C(np.asarray(I["L0_attn_bo"], np.float32).reshape(1, D)),
        "gwT": C(np.asarray(I["L0_ffn_gate_w"], np.float32).T),
        "w1T": C(np.asarray(I["L0_ffn_w1"], np.float32).T),
        "w3T": C(np.asarray(I["L0_ffn_w3"], np.float32).T),
        "w2T": C(np.asarray(I["L0_ffn_w2"], np.float32).T),
        "au1": aucat(a1), "au3": aucat(a3),
        "b1pk": bpk(b1), "b3pk": bpk(b3),
        "a2cat": a2c, "b2cat": b2c,
        "identf": identf, "onehot": onehot,
        "e1row_d": e1row, "onecol_d": onecol,
        "slneg_d": slneg, "sjh_d": sjh, "mask_d": mask, "svec_d": svec,
        "gidx_d": gidx,
    }


def kernel(**inputs):
    nc, _ = build_program(upto="all")
    in_maps = [host_inputs(c, inputs) for c in range(8)]
    res = run_bass_kernel_spmd(nc, in_maps, core_ids=list(range(8)))
    y = np.zeros((B, S, D), np.float32)
    for c in range(8):
        b, hq = c // 4, c % 4
        y[b, 1024 * hq:1024 * (hq + 1), :] = res.results[c]["out"]
    return y
